# version 03
# baseline (speedup 1.0000x reference)
import numpy as np
import ml_dtypes
import concourse.bass as bass
import concourse.mybir as mybir
from concourse.bass_utils import run_bass_kernel_spmd
from contextlib import ExitStack

F32 = mybir.dt.float32
BF16 = mybir.dt.bfloat16
I16 = mybir.dt.int16
U16 = mybir.dt.uint16
AF = mybir.ActivationFunctionType
ALU = mybir.AluOpType
AX = mybir.AxisListType

T = 4096
D = 1024
C8 = 8
TT = 512
NT = T // TT
DI = 3072
MI = DI // 128
N_BLOCKS = 4
DEBUG = False
L1CUT = 'full'

ENGS = ("tensor", "vector", "scalar", "gpsimd", "sync")
Z = ("v", 0)


class Bld:
    def __init__(self):
        self.q = {e: [] for e in ENGS}
        self.cnt = {}

    def op(self, eng, fn, inc=None, amt=1):
        self.q[eng].append(("op", fn, inc, amt))
        if inc is not None:
            self.cnt[inc] = self.cnt.get(inc, 0) + amt
            return self.cnt[inc]
        return None

    def dma(self, eng, fn, sem):
        return self.op(eng, fn, sem, 16)

    def wait(self, eng, sem, val):
        assert val <= self.cnt.get(sem, 0), f"wait {sem}>={val} emitted={self.cnt.get(sem, 0)}"
        if val > 0:
            self.q[eng].append(("wait", sem, val, None))

    def waitp(self, eng, pair):
        if pair and pair[1] > 0:
            self.wait(eng, pair[0], pair[1])

    def drain(self, eng):
        self.q[eng].append(("drain", None, None, None))

    def tick(self, sem):
        return self.cnt.get(sem, 0)


def build_nc():
    nc = bass.Bass("TRN2", target_bir_lowering=False)

    xT = nc.dram_tensor("xT", [D, T], F32, kind="ExternalInput").ap()
    outT = nc.dram_tensor("outT", [D, T], F32, kind="ExternalOutput").ap()
    xmA = nc.dram_tensor("xmA", [D, T], F32).ap()
    dbg = nc.dram_tensor("dbg", [128, 4096], F32, kind="ExternalOutput").ap() if DEBUG else None

    wz = {}
    def win(name, shape):
        wz[name] = nc.dram_tensor(name, shape, F32, kind="ExternalInput").ap()
    for nm in ("WqT", "WkT", "WvT", "WoT", "WrT", "WkfT", "WvfT"):
        win(nm, [D, D])
    for nm in ("fmq1T", "fmq2T", "fmk1T", "fmk2T"):
        win(nm, [128, 128])
    win("fmb", [128, 4]); win("rmsw", [128, 1]); win("mus", [128, 16])
    win("qpT", [D, 256]); win("kpT", [D, 256])
    win("wg_nat", [DI, D]); win("wu_nat", [DI, D]); win("wd_nat", [D, DI])
    win("cst_f", [128, 1024]); win("cstf_f", [128, 384]); win("sel8T_f", [8, 1024])

    wgq = nc.dram_tensor("wgq", [DI, D], BF16).ap()
    wuq = nc.dram_tensor("wuq", [DI, D], BF16).ap()
    wdq = nc.dram_tensor("wdq", [D, DI], BF16).ap()
    sp_gate = nc.dram_tensor("sp_gate", [DI, T], BF16).ap()
    sp_up = nc.dram_tensor("sp_up", [DI, T], BF16).ap()
    sp_t4 = nc.dram_tensor("sp_t4", [DI, T], BF16).ap()
    sp_op = nc.dram_tensor("sp_op", [DI, T], BF16).ap()
    cc_in = nc.dram_tensor("cc_in", [1, 8], F32).ap()
    cc_out = nc.dram_tensor("cc_out", [1, 8], F32).ap()
    wobf = nc.dram_tensor("wobf", [D, D], BF16).ap()

    es = ExitStack()
    S = lambda name, shape, dt: es.enter_context(nc.sbuf_tensor(name, shape, dt))

    XB = S("XB", [128, C8 * T], BF16)
    WB = S("WB", [128, 26624], BF16)
    SB = S("SB", [128, 21504], BF16)
    SF = S("SF", [128, 7680], F32)
    ROWS = S("ROWS", [1, 1536], F32)
    SI = S("SI", [128, 1024], I16)
    SU = S("SU", [128, 512], U16)
    CST = S("CST", [128, 1024], BF16)
    CSTF = S("CSTF", [128, 384], F32)
    SEL = S("SEL", [8, 1024], BF16)
    SELI = S("SELI", [8, 512], BF16)
    FMT = S("FMT", [128, 512], BF16)

    PB = [es.enter_context(nc.psum_tensor(f"PB{i}", [128, 512], F32)) for i in range(6)]
    PT = [es.enter_context(nc.psum_tensor(f"PT{i}", [128, 1024], BF16)) for i in range(2)]

    semlist = ["t", "v", "s", "g", "dS", "dG", "dC", "dX", "cc", "bar"]
    sems = {n: es.enter_context(nc.semaphore(n)) for n in semlist}
    b = Bld()

    IDT = CST[:, 0:128]
    ONESC = CST[:, 256:257]
    ONES8 = CST[:, 320:384]
    FM = {"q1": FMT[:, 0:128], "q2": FMT[:, 128:256],
          "k1": FMT[:, 256:384], "k2": FMT[:, 384:512]}
    RMSW = CSTF[:, 0:1]
    MUR = lambda c: CSTF[:, 16 + c:17 + c]
    MUK = lambda c: CSTF[:, 24 + c:25 + c]
    FMB = {"q1": CSTF[:, 32:33], "q2": CSTF[:, 33:34],
           "k1": CSTF[:, 34:35], "k2": CSTF[:, 35:36]}
    COL = lambda j: CSTF[:, 64 + j:65 + j]
    EPS5 = CSTF[:, 40:41]
    XLAST = CSTF[:, 96:104]
    IDTF = CSTF[:, 128:256]
    ROW1 = CSTF[0:1, 256:384]

    xb_ready = [0] * NT

    def MM(out_ap, lhsT, rhs, start, stop, inc=None):
        return b.op("tensor", lambda e, o=out_ap, l=lhsT, r=rhs, s1=start, s2=stop:
                    e.matmul(o, l, r, start=s1, stop=s2), inc)

    def TR(out_ap, in_ap, ident=None, inc=None):
        idt = IDT if ident is None else ident
        return b.op("tensor", lambda e, o=out_ap, i=in_ap, d=idt: e.transpose(o, i, d), inc)

    def V(fn, inc=None):
        return b.op("vector", fn, inc)

    def SC(fn, inc=None):
        return b.op("scalar", fn, inc)

    def barrier():
        b.wait("sync", "dS", b.tick("dS"))
        b.wait("sync", "dX", b.tick("dX"))
        b.wait("gpsimd", "dG", b.tick("dG"))
        b.wait("gpsimd", "dC", b.tick("dC"))
        b.wait("gpsimd", "cc", b.tick("cc"))
        for e in ENGS:
            b.op(e, lambda eng: eng.nop(), "bar")
        tgt = b.tick("bar")
        for e in ENGS:
            b.wait(e, "bar", tgt)

    b.dma("gpsimd", lambda e: e.dma_start(
        out=bass.AP(XB, 0, [[C8 * T, 128], [T, 8], [1, T]]),
        in_=bass.AP(xT.tensor, 0, [[T, 128], [T * 128, 8], [1, T]])), "dG")
    b.dma("sync", lambda e: e.dma_start(out=xmA, in_=xT), "dX")
    t_cstload = b.dma("gpsimd", lambda e: e.dma_start(out=CST[:, 0:1024], in_=wz["cst_f"]), "dG")
    b.dma("gpsimd", lambda e: e.dma_start(out=SEL[:, :], in_=wz["sel8T_f"]), "dC")
    tcf = b.dma("sync", lambda e: e.dma_start(out=CSTF[:, 0:384], in_=wz["cstf_f"]), "dS")

    def load_wb(names, width=1024):
        tks = []
        for j, nm in enumerate(names):
            for c in range(C8):
                o = j * C8 * width + c * width
                tk = b.dma("gpsimd", lambda e, nm=nm, c=c, o=o, w=width: e.dma_start(
                    out=WB[:, o:o + w], in_=wz[nm][128 * c:128 * (c + 1), 0:w]), "dG")
            tks.append(tk)
        return tks

    twq = load_wb(["WqT", "WkT", "WvT"])
    t_wobf = b.dma("gpsimd", lambda e: e.dma_start(out=wobf, in_=wz["WoT"]), "dC")
    WQ = lambda K, M: WB[:, 1024 * K + 128 * M: 1024 * K + 128 * (M + 1)]
    WKa = lambda K, M: WB[:, 8192 + 1024 * K + 128 * M: 8192 + 1024 * K + 128 * (M + 1)]
    WVrow = lambda K: WB[:, 16384 + 1024 * K: 16384 + 1024 * (K + 1)]
    WOW = lambda i, K: WB[:, 24576 + 1024 * (i % 2) + 128 * K: 24576 + 1024 * (i % 2) + 128 * (K + 1)]

    for i in range(NT):
        xb_ready[i] = b.tick("dG")
    b.wait("gpsimd", "dG", b.tick("dG"))
    b.wait("gpsimd", "dC", b.tick("dC"))
    t_init = b.op("gpsimd", lambda e: e.nop(), "g")
    b.wait("sync", "dS", tcf)
    b.wait("vector", "g", t_init)
    V(lambda e: e.memset(ROWS[0:1, 0:128], 0.0))
    b.drain("vector")
    for nm, key in [("fmq1T", "q1"), ("fmq2T", "q2"), ("fmk1T", "k1"), ("fmk2T", "k2")]:
        tk = b.dma("sync", lambda e, nm=nm: e.dma_start(out=SF[:, 0:128], in_=wz[nm]), "dS")
        b.wait("vector", "dS", tk)
        tv = V(lambda e, key=key: e.tensor_scalar(FM[key], SF[:, 0:128], 1.0, None, ALU.mult), "v")
        b.wait("sync", "v", tv)
    tk = b.dma("sync", lambda e: e.dma_start(out=SF[:, 0:4], in_=wz["fmb"]), "dS")
    b.wait("vector", "dS", tk)
    V(lambda e: e.tensor_scalar(CSTF[:, 32:33], SF[:, 0:1], float(128 ** -0.5), None, ALU.mult))
    tv = V(lambda e: e.tensor_copy(CSTF[:, 33:36], SF[:, 1:4]), "v")
    b.wait("sync", "v", tv)
    tk = b.dma("sync", lambda e: e.dma_start(out=SF[:, 0:1], in_=wz["rmsw"]), "dS")
    b.wait("vector", "dS", tk)
    tv = V(lambda e: e.tensor_copy(CSTF[:, 0:1], SF[:, 0:1]), "v")
    b.wait("sync", "v", tv)
    tk = b.dma("sync", lambda e: e.dma_start(out=SF[:, 0:16], in_=wz["mus"]), "dS")
    b.wait("vector", "dS", tk)
    tv = V(lambda e: e.tensor_copy(CSTF[:, 16:32], SF[:, 0:16]), "v")
    b.wait("sync", "v", tv)

    def dequant(src, dst, nrow_chunks, width):
        nb = 32
        nslab = width // 1024
        for r in range(nrow_chunks):
            for sl in range(nslab):
                tk = b.dma("sync", lambda e, r=r, sl=sl, src=src: e.dma_start(
                    out=SF[:, 0:1024], in_=src[128 * r:128 * (r + 1), 1024 * sl:1024 * (sl + 1)]), "dS")
                b.wait("scalar", "dS", tk)
                tks = SC(lambda e: e.activation(SF[:, 1024:2048], SF[:, 0:1024], AF.Abs), "s")
                b.wait("vector", "s", tks)
                hw = 16
                while hw >= 2:
                    V(lambda e, hw=hw: e.tensor_tensor(
                        bass.AP(SF, 1024, [[7680, 128], [32, nb], [1, hw]]),
                        bass.AP(SF, 1024, [[7680, 128], [32, nb], [1, hw]]),
                        bass.AP(SF, 1024 + hw, [[7680, 128], [32, nb], [1, hw]]), ALU.max))
                    b.drain("vector")
                    hw //= 2
                V(lambda e: e.tensor_tensor(
                    bass.AP(SF, 2048, [[7680, 128], [1, nb]]),
                    bass.AP(SF, 1024, [[7680, 128], [32, nb]]),
                    bass.AP(SF, 1025, [[7680, 128], [32, nb]]), ALU.max))
                b.drain("vector")
                V(lambda e: e.tensor_scalar(
                    SF[:, 2048:2048 + nb], SF[:, 2048:2048 + nb], 1.0 / 7.0, 1e-12,
                    ALU.mult, ALU.max))
                b.drain("vector")
                V(lambda e: e.reciprocal(SF[:, 2112:2112 + nb], SF[:, 2048:2048 + nb]))
                b.drain("vector")
                V(lambda e: e.tensor_tensor(
                    bass.AP(SI, 0, [[1024, 128], [32, nb], [1, 32]]),
                    bass.AP(SF, 0, [[7680, 128], [32, nb], [1, 32]]),
                    bass.AP(SF, 2112, [[7680, 128], [1, nb], [0, 32]]), ALU.mult), "v")
                b.drain("vector")
                tvw = V(lambda e: e.tensor_tensor(
                    bass.AP(SB, 0, [[21504, 128], [32, nb], [1, 32]]),
                    bass.AP(SI, 0, [[1024, 128], [32, nb], [1, 32]]),
                    bass.AP(SF, 2048, [[7680, 128], [1, nb], [0, 32]]), ALU.mult), "v")
                b.wait("sync", "v", tvw)
                tst = b.dma("sync", lambda e, r=r, sl=sl, dst=dst: e.dma_start(
                    out=dst[128 * r:128 * (r + 1), 1024 * sl:1024 * (sl + 1)], in_=SB[:, 0:1024]), "dS")
                b.wait("sync", "s", tks)
                b.wait("sync", "dS", tst)

    b.wait("sync", "v", b.tick("v"))
    dequant(wz["wg_nat"], wgq, MI, D)
    dequant(wz["wu_nat"], wuq, MI, D)
    dequant(wz["wd_nat"], wdq, C8, DI)
    barrier()

    BINV = SF[:, 0:512]
    BMINV = SF[:, 512:1024]
    t_stats = [0]

    def ln_tiles(nt, sq_cols, ba, bb, pre_t=(), pre_s=()):
        xw = xb_ready[nt]
        b.wait("scalar", "dG", xw)
        b.wait("tensor", "dG", xw)
        for p in pre_s:
            b.waitp("scalar", p)
        for p in pre_t:
            b.waitp("tensor", p)
        for c in range(C8):
            MM(ba[0:1, :], ONESC, XB[:, T * c + TT * nt: T * c + TT * nt + TT],
               c == 0, c == C8 - 1, "t" if c == C8 - 1 else None)
        tmmq = [0] * C8
        for c in range(C8):
            if c >= 2:
                b.wait("scalar", "t", tmmq[c - 2])
            tsq = SC(lambda e, c=c, o=sq_cols, nt=nt: e.activation(
                SB[:, o + 512 * (c % 2): o + 512 * (c % 2) + 512],
                XB[:, T * c + TT * nt: T * c + TT * nt + TT], AF.Square), "s")
            b.wait("tensor", "s", tsq)
            tmmq[c] = MM(bb[0:1, :], ONESC,
                         SB[:, sq_cols + 512 * (c % 2): sq_cols + 512 * (c % 2) + 512],
                         c == 0, c == C8 - 1, "t")
        t_stats[0] = tmmq[-1]
        tk = b.tick("t")
        b.wait("vector", "t", tk)
        V(lambda e, ba=ba: e.tensor_scalar(ROWS[0:1, 0:512], ba[0:1, :], 1.0 / D, None, ALU.mult))
        V(lambda e, bb=bb: e.tensor_scalar(ROWS[0:1, 512:1024], bb[0:1, :], 1.0 / D, None, ALU.mult), "v")
        b.drain("vector")
        V(lambda e: e.tensor_mul(ROWS[0:1, 1024:1536], ROWS[0:1, 0:512], ROWS[0:1, 0:512]))
        b.drain("vector")
        tv = V(lambda e: e.tensor_sub(ROWS[0:1, 512:1024], ROWS[0:1, 512:1024], ROWS[0:1, 1024:1536]), "v")
        b.wait("scalar", "v", tv)
        ts2 = SC(lambda e: e.activation(ROWS[0:1, 1024:1536], ROWS[0:1, 512:1024],
                                        AF.Sqrt, bias=EPS5[0:1, :], scale=1.0), "s")
        b.wait("vector", "s", ts2)
        V(lambda e: e.reciprocal(ROWS[0:1, 512:1024], ROWS[0:1, 1024:1536]))
        b.drain("vector")
        tv = V(lambda e: e.tensor_mul(ROWS[0:1, 1024:1536], ROWS[0:1, 0:512], ROWS[0:1, 512:1024]), "v")
        b.wait("tensor", "v", tv)
        MM(ba[:, :], ROW1, ROWS[0:1, 512:1024], True, True)
        tk = MM(bb[:, :], ROW1, ROWS[0:1, 1024:1536], True, True, "t")
        b.wait("vector", "t", tk)
        V(lambda e, ba=ba: e.tensor_copy(BINV, ba[:, :]))
        tv = V(lambda e, bb=bb: e.tensor_copy(BMINV, bb[:, :]), "v")
        b.drain("vector")
        return tv

    def bcast_col(rc, j, bank):
        V(lambda e, rc=rc: e.tensor_tensor(
            ROWS[0:1, 1280:1408], ROW1,
            bass.AP(ROWS, rc, [[1536, 1], [0, 128]]), ALU.mult), "v")
        b.drain("vector")
        b.wait("tensor", "v", b.tick("v"))
        tk = MM(bank[:, 0:128], ROW1, ROWS[0:1, 1280:1408], True, True, "t")
        b.wait("vector", "t", tk)
        tv = V(lambda e, j=j, bank=bank: e.tensor_copy(COL(j), bank[:, 0:1]), "v")
        b.drain("vector")
        return tv

    XNEW = lambda m: SF[:, 5632 + 512 * (m % 2): 5632 + 512 * (m % 2) + 512]

    def mk_residual(nt, master):
        tks = []
        def accum(m, tv):
            b.wait("gpsimd", "v", tv)
            tk = b.dma("gpsimd", lambda e, m=m, nt=nt, ms=master: e.dma_start(
                out=ms[128 * m:128 * (m + 1), TT * nt: TT * nt + TT],
                in_=XNEW(m), accum_op=ALU.add), "dG")
            tks.append(tk)
            return tk
        def readback():
            b.wait("gpsimd", "dG", tks[-1])
            tk = b.dma("gpsimd", lambda e, nt=nt, ms=master: e.dma_start(
                out=bass.AP(XB, TT * nt, [[C8 * T, 128], [T, 8], [1, TT]]),
                in_=bass.AP(ms.tensor, TT * nt, [[T, 128], [T * 128, 8], [1, TT]])), "dG")
            xb_ready[nt] = tk
            return tk
        return accum, readback

    def dump(dst_cols, src_ap):
        b.wait("gpsimd", "v", b.tick("v"))
        b.wait("gpsimd", "s", b.tick("s"))
        tk = b.dma("gpsimd", lambda e, d=dst_cols, s_=src_ap: e.dma_start(
            out=dbg[:, d:d + 512], in_=s_), "dG")
        for eng in ("vector", "scalar", "tensor"):
            b.wait(eng, "dG", tk)

    def l0_attn(master):
        XH = lambda c: SB[:, 512 * c: 512 * (c + 1)]
        QF = lambda h: SB[:, 4096 + 512 * h: 4096 + 512 * (h + 1)]
        KF = lambda h: SB[:, 8192 + 512 * h: 8192 + 512 * (h + 1)]
        OBF = lambda h: SB[:, 12288 + 512 * h: 12288 + 512 * (h + 1)]
        VT = lambda s: SB[:, 16384 + 1024 * (s % 2): 16384 + 1024 * (s % 2) + 1024]
        KFT = lambda g: SB[:, 18432 + 512 * g: 18432 + 512 * (g + 1)]
        ATM = KFT
        QRAW = lambda i: SB[:, 19456 + 512 * (i % 2): 19456 + 512 * (i % 2) + 512]
        SBH = lambda h: SB[:, 20480 + 128 * h: 20480 + 128 * (h + 1)]
        SST = SF[:, 2048:3072]
        A2 = SF[:, 3072:3584]
        A1 = SF[:, 3584:4096]
        MSK4 = lambda g: bass.AP(CST, 128, [[1024, 128], [0, 4], [1, 128]])
        ATM3 = lambda g: bass.AP(SB, 18432 + 512 * g, [[21504, 128], [128, 4], [1, 128]])
        AT3 = lambda bank: bass.AP(bank, 0, [[512, 128], [128, 4], [1, 128]])

        b.wait("tensor", "dG", twq[-1])
        V(lambda e: e.memset(SST, 0.0))
        tv0 = V(lambda e: e.memset(SB[:, 20480:21504], 0.0), "v")
        b.wait("tensor", "v", tv0)

        if DEBUG:
            tv_ = V(lambda e: e.tensor_copy(SF[:, 3712:3840], FM["q1"]), "v")
            b.wait("gpsimd", "v", tv_)
            tkw = b.dma("gpsimd", lambda e: e.dma_start(out=dbg[:, 3200:3328], in_=SF[:, 3712:3840]), "dG")
            for eng_ in ("vector", "scalar", "tensor"):
                b.wait(eng_, "dG", tkw)
        bankf = {i: Z for i in range(6)}
        tptf = [Z, Z]
        kft_guard = [0, 0]
        a2_guard = [0]
        wo_prev = [0]
        accum_hold = [0, 0]
        tms_prev = 0
        wo_hold = [0, 0]

        for nt in range(NT):
            tln = ln_tiles(nt, 19456, PB[0], PB[1],
                           pre_t=(bankf[0], bankf[1]),
                           pre_s=((("t", tms_prev)) ,) if tms_prev else ())
            bankf[0] = bankf[1] = ("v", tln)
            b.wait("vector", "t", b.tick("t"))
            for c in range(C8):
                V(lambda e, c=c, nt=nt: e.tensor_tensor(
                    SF[:, 1024 + 512 * (c % 2): 1024 + 512 * (c % 2) + 512],
                    XB[:, T * c + TT * nt: T * c + TT * nt + TT], BINV, ALU.mult))
                b.drain("vector")
                V(lambda e, c=c: e.tensor_tensor(
                    XH(c), SF[:, 1024 + 512 * (c % 2): 1024 + 512 * (c % 2) + 512],
                    BMINV, ALU.subtract), "v")
            txh = b.tick("v")

            for Wf, fm1, fm2, fb1, fb2, DST in (
                    (WQ, FM["q1"], FM["q2"], FMB["q1"], FMB["q2"], QF),
                    (WKa, FM["k1"], FM["k2"], FMB["k1"], FMB["k2"], KF)):
                for h in range(C8):
                    bank = PB[2 + h % 2]
                    b.wait("tensor", "v", txh)
                    b.waitp("tensor", bankf[2 + h % 2])
                    for K in range(C8):
                        MM(bank[:, :], Wf(K, h), XH(K), K == 0, K == C8 - 1,
                           "t" if K == C8 - 1 else None)
                    tmm = b.tick("t")
                    b.wait("vector", "t", tmm)
                    tqr = V(lambda e, h=h, bank=bank: e.tensor_copy(QRAW(h), bank[:, :]), "v")
                    if DEBUG and nt == 0 and h == 0 and DST is QF:
                        dump(0, QRAW(0))
                    b.drain("vector")
                    tqr = V(lambda e: e.engine_nop(), "v")
                    bankf[2 + h % 2] = ("v", tqr)
                    b.wait("tensor", "v", tqr)
                    b.waitp("tensor", bankf[4])
                    b.waitp("tensor", bankf[5])
                    MM(PB[4][:, :], fm1, QRAW(h), True, True, "t")
                    t45 = MM(PB[5][:, :], fm2, QRAW(h), True, True, "t")
                    b.wait("scalar", "t", t45)
                    b.wait("scalar", "v", a2_guard[0])
                    sc_ = float(128 ** -0.5) if DST is QF else 1.0
                    SC(lambda e, fb1=fb1, sc_=sc_: e.activation(
                        A1, PB[4][:, :], AF.Identity, bias=fb1, scale=sc_), "s")
                    b.drain("scalar")
                    ts1 = SC(lambda e, fb2=fb2: e.activation(
                        A2, PB[5][:, :], AF.Identity, bias=fb2, scale=1.0), "s")
                    if DEBUG and nt == 0 and h == 0 and DST is QF:
                        b.wait("gpsimd", "s", b.tick("s"))
                        tk_ = b.dma("gpsimd", lambda e: e.dma_start(out=dbg[:, 512:1024], in_=A1), "dG")
                        tk_ = b.dma("gpsimd", lambda e: e.dma_start(out=dbg[:, 1024:1536], in_=A2), "dG")
                        for eng_ in ("vector", "scalar", "tensor"):
                            b.wait(eng_, "dG", tk_)
                    b.wait("vector", "s", ts1)
                    tvq = V(lambda e, h=h, DST=DST: e.tensor_tensor(
                        DST(h), A1, A2, ALU.mult), "v")
                    a2_guard[0] = tvq
                    if DEBUG and nt == 0 and DST is QF and h == 0:
                        dump(1536, QF(0))
                        b.wait("gpsimd", "v", b.tick("v"))

                    bankf[4] = bankf[5] = ("v", tvq)
            tqk = b.tick("v")
            if DEBUG and nt == 0:
                dump(2048, QF(0))
            for s_ in range(4):
                b.wait("tensor", "v", tqk)
                b.waitp("tensor", bankf[2])
                b.waitp("tensor", bankf[3])
                for K in range(C8):
                    MM(PB[2][:, :], XH(K)[:, 128 * s_:128 * (s_ + 1)], WVrow(K)[:, 0:512],
                       K == 0, K == C8 - 1, "t" if K == C8 - 1 else None)
                for K in range(C8):
                    MM(PB[3][:, :], XH(K)[:, 128 * s_:128 * (s_ + 1)], WVrow(K)[:, 512:1024],
                       K == 0, K == C8 - 1, "t" if K == C8 - 1 else None)
                tvm = b.tick("t")
                b.wait("vector", "t", tvm)
                V(lambda e, s_=s_: e.tensor_copy(VT(s_)[:, 0:512], PB[2][:, :]))
                tvt = V(lambda e, s_=s_: e.tensor_copy(VT(s_)[:, 512:1024], PB[3][:, :]), "v")
                bankf[2] = bankf[3] = ("v", tvt)
                for g in range(2):
                    b.waitp("tensor", tptf[g])
                    for j in range(4):
                        h = 4 * g + j
                        TR(PT[g][:, 128 * j:128 * (j + 1)], KF(h)[:, 128 * s_:128 * (s_ + 1)],
                           inc="t" if j == 3 else None)
                    ttr = b.tick("t")
                    b.wait("vector", "t", ttr)
                    if kft_guard[g]:
                        b.wait("vector", "t", kft_guard[g])
                    tke = V(lambda e, g=g: e.tensor_copy(KFT(g), PT[g][:, 0:512]), "v")
                    tptf[g] = ("v", tke)
                    atb, kvb_, ob = PB[4 + g], PB[0 + g], PB[2 + g]
                    b.waitp("tensor", bankf[4 + g])
                    for j in range(4):
                        h = 4 * g + j
                        MM(atb[:, 128 * j:128 * (j + 1)], KF(h)[:, 128 * s_:128 * (s_ + 1)],
                           QF(h)[:, 128 * s_:128 * (s_ + 1)], True, True,
                           "t" if j == 3 else None)
                    tat = b.tick("t")
                    b.wait("tensor", "v", max(tke, tvt))
                    b.waitp("tensor", bankf[0 + g])
                    for j in range(4):
                        h = 4 * g + j
                        MM(kvb_[:, 128 * j:128 * (j + 1)], KFT(g)[:, 128 * j:128 * (j + 1)],
                           VT(s_)[:, 128 * h:128 * (h + 1)], True, True,
                           "t" if j == 3 else None)
                    tkv = b.tick("t")
                    b.wait("vector", "t", max(tat, tkv))
                    tam = V(lambda e, g=g, atb=atb: e.tensor_tensor(
                        ATM3(g), AT3(atb), MSK4(g), ALU.mult), "v")
                    bankf[4 + g] = ("v", tam)
                    b.wait("tensor", "v", tam)
                    b.waitp("tensor", bankf[2 + g])
                    for j in range(4):
                        h = 4 * g + j
                        MM(ob[:, 128 * j:128 * (j + 1)], SBH(h),
                           QF(h)[:, 128 * s_:128 * (s_ + 1)], True, False)
                        MM(ob[:, 128 * j:128 * (j + 1)], VT(s_)[:, 128 * h:128 * (h + 1)],
                           ATM(g)[:, 128 * j:128 * (j + 1)], False, True,
                           "t" if j == 3 else None)
                    tob = b.tick("t")
                    kft_guard[g] = tob
                    b.wait("vector", "t", tkv)
                    V(lambda e, g=g, kvb_=kvb_: e.tensor_tensor(
                        SST[:, 512 * g:512 * (g + 1)], SST[:, 512 * g:512 * (g + 1)],
                        kvb_[:, :], ALU.add), "v")
                    bankf[0 + g] = ("v", b.tick("v"))
                    b.drain("vector")
                    b.wait("vector", "t", tob)
                    V(lambda e, g=g: e.tensor_copy(
                        SB[:, 20480 + 512 * g: 20480 + 512 * (g + 1)],
                        SST[:, 512 * g:512 * (g + 1)]), "v")
                    b.wait("scalar", "t", tob)
                    tse = SC(lambda e, s_=s_, g=g, ob=ob: e.activation(
                        bass.AP(SB, 12288 + 512 * (4 * g) + 128 * s_,
                                [[21504, 128], [512, 4], [1, 128]]),
                        ob[:, :], AF.Copy), "s")
                    bankf[2 + g] = ("s", tse)
            tchunk_s = b.tick("s")

            b.drain("scalar")
            tsql = [0] * C8
            for h in range(C8):
                if h >= 2:
                    b.wait("scalar", "t", tsql[h - 2])
                tss = SC(lambda e, h=h: e.activation(QRAW(h), OBF(h), AF.Square), "s")
                b.drain("scalar")
                b.wait("tensor", "s", tss)
                if h == 0:
                    b.waitp("tensor", bankf[0])
                tsql[h] = MM(PB[0][0:8, :], ONES8[:, 8 * h:8 * h + 8], QRAW(h),
                             h == 0, h == C8 - 1, "t")
            tms = tsql[-1]
            tms_prev = tms
            b.wait("scalar", "t", tms)
            ts3 = SC(lambda e: e.activation(SF[0:8, 4608:5120], PB[0][0:8, :],
                                            AF.Sqrt, bias=EPS5[0:8, :], scale=1.0 / 128.0), "s")
            bankf[0] = ("s", ts3)
            b.wait("vector", "s", ts3)
            V(lambda e: e.reciprocal(SF[0:8, 5120:5632], SF[0:8, 4608:5120]))
            b.drain("vector")
            tv4 = V(lambda e: e.tensor_copy(SELI[:, :], SF[0:8, 5120:5632]), "v")
            b.wait("tensor", "v", tv4)
            b.wait("tensor", "s", tchunk_s)
            ton = 0
            for h in range(C8):
                bank = PB[1 + h % 2]
                b.waitp("tensor", bankf[1 + h % 2])
                tbi = MM(bank[:, :], SEL[:, 128 * h:128 * (h + 1)], SELI[:, :],
                         True, True, "t")
                b.wait("vector", "t", tbi)
                ton = V(lambda e, h=h, bank=bank: e.scalar_tensor_tensor(
                    QF(h), OBF(h), RMSW, bank[:, :], ALU.mult, ALU.mult), "v")
                bankf[1 + h % 2] = ("v", ton)
            accum, readback = mk_residual(nt, master)

            b.wait("tensor", "v", ton)
            for m in range(C8):
                if wo_hold[m % 2]:
                    b.wait("sync", "t", wo_hold[m % 2])
                else:
                    b.wait("sync", "dS", b.tick("dS"))
                wo_tk = b.dma("sync", lambda e, m=m: e.dma_start(
                    out=bass.AP(WB, 24576 + 1024 * (m % 2), [[26624, 128], [128, 8], [1, 128]]),
                    in_=bass.AP(wobf.tensor, 128 * m, [[D, 128], [D * 128, 8], [1, 128]])), "dS")
                wo_prev[0] = wo_tk
                bank = PB[3 + m % 2]
                b.waitp("tensor", bankf[3 + m % 2])
                b.wait("tensor", "dS", wo_tk)
                for K in range(C8):
                    MM(bank[:, :], WOW(m, K), QF(K), K == 0, K == C8 - 1,
                       "t" if K == C8 - 1 else None)
                two = b.tick("t")
                wo_hold[m % 2] = two
                b.wait("vector", "t", two)
                b.wait("vector", "dG", accum_hold[m % 2])
                tvx = V(lambda e, m=m, bank=bank: e.tensor_copy(XNEW(m), bank[:, :]), "v")
                bankf[3 + m % 2] = ("v", tvx)
                accum_hold[m % 2] = accum(m, tvx)
            readback()

    def l0_ffn(master):
        XR = lambda c: SB[:, 512 * c: 512 * (c + 1)]
        XK = lambda c: SB[:, 4096 + 512 * c: 4096 + 512 * (c + 1)]
        KACT = lambda c: SB[:, 8192 + 512 * c: 8192 + 512 * (c + 1)]
        RSIG = lambda m: SF[:, 1024 + 512 * m: 1024 + 512 * (m + 1)]
        XHT = lambda i: SF[:, 6656 + 512 * (i % 2): 6656 + 512 * (i % 2) + 512]
        DT_ = SF[:, 5120:5632]

        tw = load_wb(["WrT", "WkfT", "WvfT"])
        WR = lambda K, M: WB[:, 1024 * K + 128 * M: 1024 * K + 128 * (M + 1)]
        WKF = lambda K, M: WB[:, 8192 + 1024 * K + 128 * M: 8192 + 1024 * K + 128 * (M + 1)]
        WVF = lambda K, M: WB[:, 16384 + 1024 * K + 128 * M: 16384 + 1024 * K + 128 * (M + 1)]
        b.wait("tensor", "dG", tw[-1])
        V(lambda e: e.memset(CSTF[:, 96:104], 0.0), "v")

        bankf = {i: Z for i in range(6)}
        accum_hold = [0, 0]
        tka_prev = 0
        for nt in range(NT):
            tln = ln_tiles(nt, 19456, PB[0], PB[1], pre_t=(bankf[0], bankf[1]))
            bankf[0] = bankf[1] = ("v", tln)
            if tka_prev:
                b.wait("vector", "s", tka_prev)
            b.wait("vector", "t", b.tick("t"))
            for c in range(C8):
                V(lambda e, c=c, nt=nt: e.tensor_tensor(
                    XHT(c), XB[:, T * c + TT * nt: T * c + TT * nt + TT], BINV, ALU.mult))
                b.drain("vector")
                V(lambda e, c=c: e.tensor_tensor(XHT(c), XHT(c), BMINV, ALU.subtract))
                b.drain("vector")
                V(lambda e, c=c: e.tensor_tensor(
                    DT_[:, 1:512], XHT(c)[:, 1:512], XHT(c)[:, 0:511], ALU.subtract))
                V(lambda e, c=c: e.tensor_tensor(
                    DT_[:, 0:1], XHT(c)[:, 0:1], XLAST[:, c:c + 1], ALU.subtract))
                b.drain("vector")
                V(lambda e, c=c: e.scalar_tensor_tensor(
                    XR(c)[:, 1:512], DT_[:, 1:512], MUR(c), XHT(c)[:, 0:511],
                    ALU.mult, ALU.add))
                V(lambda e, c=c: e.scalar_tensor_tensor(
                    XR(c)[:, 0:1], DT_[:, 0:1], MUR(c), XLAST[:, c:c + 1],
                    ALU.mult, ALU.add))
                V(lambda e, c=c: e.scalar_tensor_tensor(
                    XK(c)[:, 1:512], DT_[:, 1:512], MUK(c), XHT(c)[:, 0:511],
                    ALU.mult, ALU.add))
                V(lambda e, c=c: e.scalar_tensor_tensor(
                    XK(c)[:, 0:1], DT_[:, 0:1], MUK(c), XLAST[:, c:c + 1],
                    ALU.mult, ALU.add), "v")
                b.drain("vector")
                V(lambda e, c=c: e.tensor_copy(XLAST[:, c:c + 1], XHT(c)[:, 511:512]), "v")
                b.drain("vector")
            txr = b.tick("v")
            for m in range(C8):
                bank = PB[2 + m % 2]
                b.wait("tensor", "v", txr)
                b.waitp("tensor", bankf[2 + m % 2])
                for K in range(C8):
                    MM(bank[:, :], WR(K, m), XR(K), K == 0, K == C8 - 1,
                       "t" if K == C8 - 1 else None)
                tr_ = b.tick("t")
                b.wait("scalar", "t", tr_)
                trs = SC(lambda e, m=m, bank=bank: e.activation(
                    RSIG(m), bank[:, :], AF.Sigmoid), "s")
                bankf[2 + m % 2] = ("s", trs)
            for m in range(C8):
                bank = PB[4 + m % 2]
                b.wait("tensor", "v", txr)
                b.waitp("tensor", bankf[4 + m % 2])
                for K in range(C8):
                    MM(bank[:, :], WKF(K, m), XK(K), K == 0, K == C8 - 1,
                       "t" if K == C8 - 1 else None)
                tk_ = b.tick("t")
                b.wait("scalar", "t", tk_)
                SC(lambda e, m=m, bank=bank: e.activation(XHT(m), bank[:, :], AF.Relu), "s")
                b.drain("scalar")
                tka = SC(lambda e, m=m: e.activation(KACT(m), XHT(m), AF.Square), "s")
                bankf[4 + m % 2] = ("s", tka)
            tka_prev = b.tick("s")
            accum, readback = mk_residual(nt, master)
            for m in range(C8):
                bank = PB[2 + m % 2]
                b.wait("tensor", "s", tka_prev)
                b.waitp("tensor", bankf[2 + m % 2])
                for K in range(C8):
                    MM(bank[:, :], WVF(K, m), KACT(K), K == 0, K == C8 - 1,
                       "t" if K == C8 - 1 else None)
                tvm = b.tick("t")
                b.wait("vector", "t", tvm)
                b.wait("vector", "s", tka_prev)
                b.wait("vector", "dG", accum_hold[m % 2])
                tvx = V(lambda e, m=m, bank=bank: e.tensor_tensor(
                    XNEW(m), bank[:, :], RSIG(m), ALU.mult), "v")
                bankf[2 + m % 2] = ("v", tvx)
                accum_hold[m % 2] = accum(m, tvx)
            readback()

    def l1_attn(master):
        XH = lambda c: SB[:, 512 * c: 512 * (c + 1)]
        QP = lambda e_, nt: SB[:, 4096 + 4096 * e_ + 512 * nt: 4096 + 4096 * e_ + 512 * nt + 512]
        KP = lambda e_: SB[:, 12288 + 512 * e_: 12288 + 512 * (e_ + 1)]
        XHT_ = SB[:, 13312:14336]
        KPT = SB[:, 14336:14592]
        KVB = lambda e_: SB[:, 14592 + 1024 * e_: 14592 + 1024 * (e_ + 1)]
        BZ = SF[:, 1024:1536]

        tw = load_wb(["qpT", "kpT"], width=256)
        QPJ = lambda K, E: WB[:, 256 * K + 128 * E: 256 * K + 128 * (E + 1)]
        KPJ = lambda K, E: WB[:, 2048 + 256 * K + 128 * E: 2048 + 256 * K + 128 * (E + 1)]
        b.wait("tensor", "dG", tw[-1])

        bank45 = [Z, Z]
        tptf = [Z, Z]
        for nt in range(NT):
            tln = ln_tiles(nt, 19456, PB[4], PB[5], pre_t=(bank45[0], bank45[1]))
            bank45[0] = bank45[1] = ("v", tln)
            b.wait("vector", "t", b.tick("t"))
            for c in range(C8):
                V(lambda e, c=c, nt=nt: e.tensor_tensor(
                    SF[:, 1536 + 512 * (c % 2): 1536 + 512 * (c % 2) + 512],
                    XB[:, T * c + TT * nt: T * c + TT * nt + TT], BINV, ALU.mult))
                b.drain("vector")
                V(lambda e, c=c: e.tensor_tensor(
                    XH(c), SF[:, 1536 + 512 * (c % 2): 1536 + 512 * (c % 2) + 512],
                    BMINV, ALU.subtract), "v")
            txh = b.tick("v")
            for e_ in range(2):
                b.wait("tensor", "v", txh)
                b.waitp("tensor", bank45[0])
                for K in range(C8):
                    MM(PB[4][:, :], QPJ(K, e_), XH(K), K == 0, K == C8 - 1,
                       "t" if K == C8 - 1 else None)
                tmm = b.tick("t")
                b.wait("scalar", "t", tmm)
                te = SC(lambda e, e_=e_, nt=nt: e.activation(
                    QP(e_, nt), PB[4][:, :], AF.Exp), "s")
                bank45[0] = ("s", te)
                b.waitp("tensor", bank45[1])
                for K in range(C8):
                    MM(PB[5][:, :], KPJ(K, e_), XH(K), K == 0, K == C8 - 1,
                       "t" if K == C8 - 1 else None)
                tmm = b.tick("t")
                b.wait("scalar", "t", tmm)
                te = SC(lambda e, e_=e_: e.activation(KP(e_), PB[5][:, :], AF.Exp), "s")
                bank45[1] = ("s", te)
            tqk = b.tick("s")
            for s_ in range(4):
                b.wait("tensor", "s", tqk)
                for half in range(2):
                    b.waitp("tensor", tptf[half])
                    for j in range(4):
                        c = 4 * half + j
                        TR(PT[half][:, 128 * j:128 * (j + 1)],
                           XH(c)[:, 128 * s_:128 * (s_ + 1)],
                           inc="t" if j == 3 else None)
                    ttr = b.tick("t")
                    b.wait("vector", "t", ttr)
                    tev = V(lambda e, half=half: e.tensor_copy(
                        XHT_[:, 512 * half:512 * (half + 1)], PT[half][:, 0:512]), "v")
                    tptf[half] = ("v", tev)
                b.waitp("tensor", tptf[0])
                for e_ in range(2):
                    TR(PT[0][:, 512 + 128 * e_: 512 + 128 * (e_ + 1)],
                       KP(e_)[:, 128 * s_:128 * (s_ + 1)],
                       inc="t" if e_ == 1 else None)
                ttr = b.tick("t")
                b.wait("vector", "t", ttr)
                tev = V(lambda e: e.tensor_copy(KPT[:, 0:256], PT[0][:, 512:768]), "v")
                tptf[0] = ("v", tev)
                b.wait("tensor", "v", tev)
                first = nt == 0 and s_ == 0
                lastk = nt == NT - 1 and s_ == 3
                for e_ in range(2):
                    for half in range(2):
                        MM(PB[2 * e_ + half][:, :], KPT[:, 128 * e_:128 * (e_ + 1)],
                           XHT_[:, 512 * half:512 * (half + 1)], first, lastk,
                           "t" if (lastk and e_ == 1 and half == 1) else None)
        tkv = b.tick("t")
        b.wait("vector", "t", tkv)
        for e_ in range(2):
            for half in range(2):
                V(lambda e, e_=e_, half=half: e.tensor_copy(
                    KVB(e_)[:, 512 * half:512 * (half + 1)],
                    PB[2 * e_ + half][:, :]), "v")
        tkvb = b.tick("v")
        bankf = {i: ("v", tkvb) for i in range(6)}
        accum_hold = [0, 0]
        for nt in range(NT):
            b.waitp("tensor", bankf[4])
            for e_ in range(2):
                MM(PB[4][0:1, :], ONESC, QP(e_, nt), e_ == 0, e_ == 1,
                   "t" if e_ == 1 else None)
            tz = b.tick("t")
            b.wait("vector", "t", tz)
            V(lambda e: e.tensor_scalar(ROWS[0:1, 0:512], PB[4][0:1, :], 1e-8, None, ALU.add), "v")
            bankf[4] = ("v", b.tick("v"))
            b.drain("vector")
            tvz = V(lambda e: e.reciprocal(ROWS[0:1, 512:1024], ROWS[0:1, 0:512]), "v")
            b.wait("tensor", "v", tvz)
            b.waitp("tensor", bankf[5])
            tbz = MM(PB[5][:, :], ROW1, ROWS[0:1, 512:1024], True, True, "t")
            b.wait("vector", "t", tbz)
            tvb = V(lambda e: e.tensor_copy(BZ, PB[5][:, :]), "v")
            bankf[5] = ("v", tvb)
            accum, readback = mk_residual(nt, master)
            for m in range(C8):
                bank = PB[m % 2]
                b.wait("tensor", "v", tvb)
                b.waitp("tensor", bankf[m % 2])
                for e_ in range(2):
                    MM(bank[:, :], KVB(e_)[:, 128 * m:128 * (m + 1)], QP(e_, nt),
                       e_ == 0, e_ == 1, "t" if e_ == 1 else None)
                tnm = b.tick("t")
                b.wait("vector", "t", tnm)
                b.wait("vector", "dG", accum_hold[m % 2])
                tvx = V(lambda e, m=m, bank=bank: e.tensor_tensor(
                    XNEW(m), bank[:, :], BZ, ALU.mult), "v")
                bankf[m % 2] = ("v", tvx)
                accum_hold[m % 2] = accum(m, tvx)
            readback()

    def l1_ffn(master):
        XHW = lambda i: SF[:, 1024 + 512 * (i % 2): 1024 + 512 * (i % 2) + 512]
        MX = lambda j: SF[:, 2048 + j:2049 + j]
        SG_ = lambda i: SF[:, 2560 + 512 * (i % 2): 2560 + 512 * (i % 2) + 512]
        T4U = lambda i: SF[:, 3584 + 512 * (i % 2): 3584 + 512 * (i % 2) + 512]
        GB = lambda i: SB[:, 512 * (i % 4): 512 * (i % 4) + 512]
        LB = lambda i: SB[:, 2048 + 512 * (i % 4): 2048 + 512 * (i % 4) + 512]
        T4B = lambda i: SB[:, 4096 + 512 * (i % 4): 4096 + 512 * (i % 4) + 512]
        UB = lambda i: SB[:, 6144 + 512 * (i % 4): 6144 + 512 * (i % 4) + 512]
        OPB = lambda i: SB[:, 8192 + 512 * (i % 4): 8192 + 512 * (i % 4) + 512]
        OOP = lambda k: SB[:, 512 * k: 512 * (k + 1)]
        GB4 = lambda i: SB[:, 19456 + 512 * (i % 2): 19456 + 512 * (i % 2) + 512]
        I16A = lambda i: SI[:, 0:512]
        I16B = lambda i: SI[:, 512:1024]
        I16R = lambda i: SI[:, 512 * (i % 2): 512 * (i % 2) + 512]
        SUr = lambda i: SU[:, 0:512]

        wT = lambda K: WB[:, 3072 * K: 3072 * (K + 1)]
        wTd = lambda K: WB[:, 1024 * K: 1024 * (K + 1)]

        def allreduce(srcs, dsts):
            n = len(srcs)
            for j, src in enumerate(srcs):
                b.wait("tensor", "v", b.tick("v"))
                ttr = TR(PB[4][0:1, 0:128], src, ident=IDTF, inc="t")
                b.wait("vector", "t", ttr)
                V(lambda e, j=j: e.reduce_max(
                    ROWS[0:1, 1024 + j:1025 + j], PB[4][0:1, 0:128], AX.X), "v")
                b.drain("vector")
            tv = b.tick("v")
            b.wait("gpsimd", "v", tv)
            tg = b.dma("gpsimd", lambda e, n=n: e.dma_start(
                out=cc_in[0:1, 0:n], in_=ROWS[0:1, 1024:1024 + n]), "dC")
            b.wait("gpsimd", "dC", tg)
            tcc = b.op("gpsimd", lambda e: e.collective_compute(
                "AllReduce", ALU.max, replica_groups=[list(range(8))],
                ins=[cc_in], outs=[cc_out]), "cc")
            b.wait("gpsimd", "cc", tcc)
            tg = b.dma("gpsimd", lambda e, n=n: e.dma_start(
                out=ROWS[0:1, 1024:1024 + n], in_=cc_out[0:1, 0:n]), "dC")
            b.wait("vector", "dC", tg)
            for j, dc in enumerate(dsts):
                V(lambda e, j=j, dc=dc: e.tensor_copy(
                    ROWS[0:1, dc:dc + 1], ROWS[0:1, 1024 + j:1025 + j]), "v")
            b.drain("vector")

        def mkscale(m_rc, s_rc, rs_rc):
            V(lambda e, a=m_rc, o=s_rc: e.tensor_scalar(
                ROWS[0:1, o:o + 1], ROWS[0:1, a:a + 1], 1.0 / 32767.0, 1e-12,
                ALU.mult, ALU.max))
            b.drain("vector")
            V(lambda e, o=s_rc, r=rs_rc: e.reciprocal(
                ROWS[0:1, r:r + 1], ROWS[0:1, o:o + 1]), "v")
            b.drain("vector")

        b.dma("sync", lambda e: e.dma_start(out=master, in_=xmA), "dX")
        t_pre = b.tick("dX")

        V(lambda e: e.memset(MX(0), 0.0), "v")
        guard = [Z, Z]
        for nt in range(NT):
            tln = ln_tiles(nt, 19456, PB[0], PB[1], pre_t=(guard[0], guard[1]))
            guard[0] = guard[1] = ("v", tln)
            b.wait("vector", "t", b.tick("t"))
            for c in range(C8):
                V(lambda e, c=c, nt=nt: e.tensor_tensor(
                    XHW(c), XB[:, T * c + TT * nt: T * c + TT * nt + TT], BINV, ALU.mult))
                b.drain("vector")
                V(lambda e, c=c: e.tensor_tensor(XHW(c), XHW(c), BMINV, ALU.subtract))
                b.drain("vector")
                V(lambda e, c=c: e.reduce_max(MX(1), XHW(c), AX.X,
                                              apply_absolute_value=True))
                b.drain("vector")
                V(lambda e: e.tensor_max(MX(0), MX(0), MX(1)), "v")
                b.drain("vector")
        if L1CUT == 'A0':
            return
        allreduce([MX(0)], [0])
        if L1CUT == 'A':
            return
        mkscale(0, 2, 3)
        if L1CUT == 'Am':
            return
        bcast_col(2, 0, PB[4])
        bcast_col(3, 1, PB[4])
        if L1CUT == 'Ab':
            return
        def load_wt(src, nchunk, wide, pre_t):
            b.wait("sync", "t", pre_t)
            tks = []
            for K in range(nchunk):
                tk = b.dma("sync", lambda e, K=K, src=src, w=wide: e.dma_start(
                    out=WB[:, w * K: w * (K + 1)], in_=src[:, 128 * K:128 * (K + 1)],
                    transpose=True), "dS")
                tks.append(tk)
            return tks

        def mm_spill_phase(dst_spill, mxacc, mxtmp):
            for nt in range(NT):
                for mi in range(MI):
                    i = nt * MI + mi
                    bank = PB[2 + mi % 2]
                    b.waitp("tensor", bank_ev[mi % 2])
                    for K in range(C8):
                        MM(bank[:, :], wT(K)[:, 128 * mi:128 * (mi + 1)],
                           XB[:, T * K + TT * nt: T * K + TT * nt + TT],
                           K == 0, K == C8 - 1, "t" if K == C8 - 1 else None)
                    tmm = b.tick("t")
                    b.wait("vector", "t", tmm)
                    b.wait("vector", "dS", spill_hold[i % 4])
                    V(lambda e, i=i, bank=bank: e.tensor_copy(GB(i), bank[:, :]))
                    V(lambda e, bank=bank, mxtmp=mxtmp: e.reduce_max(
                        MX(mxtmp), bank[:, :], AX.X, apply_absolute_value=True), "v")
                    b.drain("vector")
                    tve = V(lambda e, mxacc=mxacc, mxtmp=mxtmp: e.tensor_max(
                        MX(mxacc), MX(mxacc), MX(mxtmp)), "v")
                    bank_ev[mi % 2] = ("v", tve)
                    b.wait("sync", "v", tve)
                    spill_hold[i % 4] = b.dma("sync", lambda e, nt=nt, mi=mi, i=i, d=dst_spill:
                        e.dma_start(out=d[128 * mi:128 * (mi + 1), TT * nt: TT * nt + TT],
                                    in_=GB(i)), "dS")
                    yield i, nt, mi

        twg = load_wt(wgq, C8, 3072, b.tick("t"))
        V(lambda e: e.memset(MX(2), 0.0), "v")
        spill_hold = [0, 0, 0, 0]
        bank_ev = [Z, Z]
        guard2 = [Z, Z]
        for nt in range(NT):
            tln = ln_tiles(nt, 19456, PB[0], PB[1], pre_t=(guard2[0], guard2[1]))
            guard2[0] = guard2[1] = ("v", tln)
            b.wait("vector", "t", t_stats[0])
            for c in range(C8):
                V(lambda e, c=c, nt=nt: e.tensor_tensor(
                    XHW(c), XB[:, T * c + TT * nt: T * c + TT * nt + TT], BINV, ALU.mult))
                b.drain("vector")
                V(lambda e, c=c: e.tensor_tensor(XHW(c), XHW(c), BMINV, ALU.subtract))
                b.drain("vector")
                V(lambda e, c=c: e.tensor_scalar(I16R(c), XHW(c), COL(1), None, ALU.mult))
                b.drain("vector")
                V(lambda e, c=c, nt=nt: e.tensor_scalar(
                    XB[:, T * c + TT * nt: T * c + TT * nt + TT], I16R(c), COL(0),
                    None, ALU.mult), "v")
                b.drain("vector")
            txq_nt = b.tick("v")
            if nt == 0:
                b.wait("tensor", "dS", twg[-1])
            for mi in range(MI):
                i = nt * MI + mi
                bank = PB[2 + mi % 2]
                b.waitp("tensor", bank_ev[mi % 2])
                b.wait("tensor", "v", txq_nt)
                for K in range(C8):
                    MM(bank[:, :], wT(K)[:, 128 * mi:128 * (mi + 1)],
                       XB[:, T * K + TT * nt: T * K + TT * nt + TT],
                       K == 0, K == C8 - 1, "t" if K == C8 - 1 else None)
                tmm = b.tick("t")
                b.wait("vector", "t", tmm)
                b.wait("vector", "dS", spill_hold[i % 4])
                V(lambda e, i=i, bank=bank: e.tensor_copy(GB(i), bank[:, :]))
                V(lambda e, bank=bank: e.reduce_max(
                    MX(3), bank[:, :], AX.X, apply_absolute_value=True), "v")
                b.drain("vector")
                tve = V(lambda e: e.tensor_max(MX(2), MX(2), MX(3)), "v")
                bank_ev[mi % 2] = ("v", tve)
                b.wait("sync", "v", tve)
                spill_hold[i % 4] = b.dma("sync", lambda e, nt=nt, mi=mi, i=i:
                    e.dma_start(out=sp_gate[128 * mi:128 * (mi + 1), TT * nt: TT * nt + TT],
                                in_=GB(i)), "dS")
        if L1CUT == 'B':
            return
        allreduce([MX(2)], [4])
        mkscale(4, 5, 6)
        bcast_col(5, 2, PB[4])
        bcast_col(6, 3, PB[4])

        twu = load_wt(wuq, C8, 3072, b.tick("t"))
        b.wait("tensor", "dS", twu[-1])
        V(lambda e: e.memset(MX(4), 0.0))
        V(lambda e: e.memset(MX(6), 0.0), "v")
        p2_hold = [0, 0, 0, 0]
        t4_hold = [0, 0, 0, 0]
        p2_prev = {"sg": 0, "su": 0, "i16": 0}
        for i, nt, mi in mm_spill_phase(sp_up, 4, 5):
            b.wait("sync", "dS", p2_hold[i % 4])
            tld = b.dma("sync", lambda e, nt=nt, mi=mi, i=i: e.dma_start(
                out=LB(i), in_=sp_gate[128 * mi:128 * (mi + 1), TT * nt: TT * nt + TT]), "dS")
            b.wait("vector", "dS", tld)
            if p2_prev["sg"]:
                b.wait("vector", "s", p2_prev["sg"])
            tvi = V(lambda e, i=i: e.tensor_scalar(I16A(i), LB(i), COL(3), None, ALU.mult), "v")
            b.wait("scalar", "v", tvi)
            SC(lambda e, i=i: e.activation(SG_(i), I16A(i), AF.Sigmoid, scale=COL(2)), "s")
            b.drain("scalar")
            if p2_prev["su"]:
                b.wait("scalar", "v", p2_prev["su"])
            tsg = SC(lambda e, i=i: e.activation(SUr(i), SG_(i), AF.Copy, scale=65536.0), "s")
            p2_prev["sg"] = tsg
            b.wait("vector", "s", tsg)
            V(lambda e, i=i: e.tensor_tensor(T4U(i), I16A(i), SUr(i), ALU.mult))
            b.drain("vector")
            V(lambda e, i=i: e.reduce_max(MX(7), T4U(i), AX.X, apply_absolute_value=True))
            b.drain("vector")
            V(lambda e: e.tensor_max(MX(6), MX(6), MX(7)))
            b.wait("vector", "dS", t4_hold[i % 4])
            tvt = V(lambda e, i=i: e.tensor_copy(T4B(i), T4U(i)), "v")
            p2_prev["su"] = tvt
            p2_hold[i % 4] = tld
            b.wait("sync", "v", tvt)
            t4_hold[i % 4] = b.dma("sync", lambda e, nt=nt, mi=mi, i=i: e.dma_start(
                out=sp_t4[128 * mi:128 * (mi + 1), TT * nt: TT * nt + TT],
                in_=T4B(i)), "dS")
        if L1CUT == 'p2':
            return
        allreduce([MX(4), MX(6)], [8, 10])
        mkscale(8, 9, 12)
        V(lambda e: e.tensor_scalar(ROWS[0:1, 13:14], ROWS[0:1, 5:6], 1.0 / 65536.0,
                                    None, ALU.mult))
        b.drain("vector")
        V(lambda e: e.tensor_mul(ROWS[0:1, 14:15], ROWS[0:1, 10:11], ROWS[0:1, 13:14]))
        b.drain("vector")
        V(lambda e: e.tensor_scalar(ROWS[0:1, 15:16], ROWS[0:1, 14:15], 1.0 / 32767.0,
                                    1e-12, ALU.mult, ALU.max))
        b.drain("vector")
        V(lambda e: e.reciprocal(ROWS[0:1, 16:17], ROWS[0:1, 15:16]))
        b.drain("vector")
        V(lambda e: e.tensor_mul(ROWS[0:1, 17:18], ROWS[0:1, 13:14], ROWS[0:1, 16:17]), "v")
        b.drain("vector")
        bcast_col(12, 4, PB[4])
        bcast_col(17, 5, PB[4])
        V(lambda e: e.memset(MX(0), 0.0), "v")
        ld_hold = [0, 0, 0, 0]
        op_hold = [0, 0, 0, 0]
        p3_prev = {"a": 0, "b": 0}
        for i in range(MI * NT):
            mi, nt = i % MI, i // MI
            b.wait("sync", "dS", ld_hold[i % 4])
            tl1 = b.dma("sync", lambda e, nt=nt, mi=mi, i=i: e.dma_start(
                out=T4B(i), in_=sp_t4[128 * mi:128 * (mi + 1), TT * nt: TT * nt + TT]), "dS")
            tl2 = b.dma("sync", lambda e, nt=nt, mi=mi, i=i: e.dma_start(
                out=UB(i), in_=sp_up[128 * mi:128 * (mi + 1), TT * nt: TT * nt + TT]), "dS")
            b.wait("scalar", "dS", tl2)
            if p3_prev["a"]:
                b.wait("scalar", "v", p3_prev["a"])
            SC(lambda e, i=i: e.activation(I16A(i), T4B(i), AF.Copy, scale=COL(5)), "s")
            b.drain("scalar")
            tsc = SC(lambda e, i=i: e.activation(I16B(i), UB(i), AF.Copy, scale=COL(4)), "s")
            b.wait("vector", "s", tsc)
            V(lambda e, i=i: e.tensor_tensor(T4U(i), I16A(i), I16B(i), ALU.mult))
            b.drain("vector")
            V(lambda e, i=i: e.reduce_max(MX(1), T4U(i), AX.X, apply_absolute_value=True))
            b.drain("vector")
            V(lambda e: e.tensor_max(MX(0), MX(0), MX(1)))
            b.wait("vector", "dS", op_hold[i % 4])
            tvo = V(lambda e, i=i: e.tensor_copy(OPB(i), T4U(i)), "v")
            p3_prev["a"] = tvo
            ld_hold[i % 4] = tl2
            b.wait("sync", "v", tvo)
            op_hold[i % 4] = b.dma("sync", lambda e, nt=nt, mi=mi, i=i: e.dma_start(
                out=sp_op[128 * mi:128 * (mi + 1), TT * nt: TT * nt + TT],
                in_=OPB(i)), "dS")
        if L1CUT == 'p3':
            return
        allreduce([MX(0)], [0])
        V(lambda e: e.tensor_mul(ROWS[0:1, 18:19], ROWS[0:1, 15:16], ROWS[0:1, 9:10]))
        b.drain("vector")
        V(lambda e: e.tensor_mul(ROWS[0:1, 19:20], ROWS[0:1, 0:1], ROWS[0:1, 18:19]))
        b.drain("vector")
        V(lambda e: e.tensor_scalar(ROWS[0:1, 20:21], ROWS[0:1, 19:20], 1.0 / 32767.0,
                                    1e-12, ALU.mult, ALU.max))
        b.drain("vector")
        V(lambda e: e.reciprocal(ROWS[0:1, 21:22], ROWS[0:1, 20:21]))
        b.drain("vector")
        V(lambda e: e.tensor_mul(ROWS[0:1, 22:23], ROWS[0:1, 18:19], ROWS[0:1, 21:22]), "v")
        b.drain("vector")
        bcast_col(22, 6, PB[4])
        bcast_col(20, 7, PB[4])
        twd = load_wt(wdq, MI, 1024, b.tick("t"))
        b.wait("tensor", "dS", twd[-1])
        b.wait("sync", "dX", t_pre)
        accum_hold = [0, 0]
        bank_ev2 = [Z, Z]
        p4_prev = 0
        tdm_last = 0
        tall = 0
        for nt in range(NT):
            if tdm_last:
                b.wait("vector", "t", tdm_last)
            for k in range(MI):
                b.wait("sync", "s", p4_prev)
                tl = b.dma("sync", lambda e, k=k, nt=nt: e.dma_start(
                    out=GB4(k), in_=sp_op[128 * k:128 * (k + 1), TT * nt: TT * nt + TT]), "dS")
                b.wait("scalar", "dS", tl)
                if tall:
                    b.wait("scalar", "v", tall)
                tsc = SC(lambda e, k=k: e.activation(I16R(k), GB4(k), AF.Copy,
                                                     scale=COL(6)), "s")
                p4_prev = tsc
                b.wait("vector", "s", tsc)
                tall = V(lambda e, k=k: e.tensor_copy(OOP(k), I16R(k)), "v")
            accum, _rb = mk_residual(nt, master)
            for m in range(C8):
                bank = PB[2 + m % 2]
                b.wait("tensor", "v", tall)
                b.waitp("tensor", bank_ev2[m % 2])
                for K in range(MI):
                    MM(bank[:, :], wTd(K)[:, 128 * m:128 * (m + 1)], OOP(K),
                       K == 0, K == MI - 1, "t" if K == MI - 1 else None)
                tdm = b.tick("t")
                tdm_last = tdm
                b.wait("vector", "t", tdm)
                b.wait("vector", "dG", accum_hold[m % 2])
                tvx = V(lambda e, m=m, bank=bank: e.tensor_scalar(
                    XNEW(m), bank[:, :], COL(7), None, ALU.mult), "v")
                bank_ev2[m % 2] = ("v", tvx)
                accum_hold[m % 2] = accum(m, tvx)

    seq = [lambda: l0_attn(xmA), lambda: l0_ffn(xmA),
           lambda: l1_attn(xmA), lambda: l1_ffn(outT)]
    for i in range(N_BLOCKS):
        seq[i]()
        barrier()
    if N_BLOCKS < 4:
        b.wait("sync", "dG", b.tick("dG"))
        b.dma("sync", lambda e: e.dma_start(out=outT, in_=xmA), "dX")
    b.wait("sync", "dS", b.tick("dS"))
    b.wait("sync", "dG", b.tick("dG"))
    b.wait("sync", "dX", b.tick("dX"))
    b.wait("sync", "dC", b.tick("dC"))

    block = es.enter_context(nc.Block())

    def mk(eng_name):
        def run(eng):
            for kind, a1, a2, a3 in b.q[eng_name]:
                if kind == "op":
                    inst = a1(eng)
                    if a2 is not None:
                        inst.then_inc(sems[a2], a3)
                elif kind == "wait":
                    eng.wait_ge(sems[a1], a2)
                else:
                    eng.drain()
        return run

    block.tensor(mk("tensor"))
    block.vector(mk("vector"))
    block.scalar(mk("scalar"))
    block.gpsimd(mk("gpsimd"))
    block.sync(mk("sync"))

    es.close()
    return nc


def kernel(**inputs):
    xs = np.ascontiguousarray(np.asarray(inputs["L0_attn_hidden_states"], np.float32))
    tr = lambda a: np.ascontiguousarray(np.asarray(a, dtype=np.float32).T)

    cst_f = np.zeros((128, 1024), np.float32)
    cst_f[:, 0:128] = np.eye(128, dtype=np.float32)
    cst_f[:, 128:256] = np.tril(np.ones((128, 128), np.float32)).T
    cst_f[:, 256] = 1.0
    for h in range(8):
        cst_f[:, 320 + 8 * h + h] = 1.0

    cstf_f = np.zeros((128, 384), np.float32)
    cstf_f[:, 128:256] = np.eye(128, dtype=np.float32)
    cstf_f[0, 256:384] = 1.0
    cstf_f[:, 40] = 1e-5

    sel8 = np.zeros((8, 1024), np.float32)
    for h in range(8):
        sel8[h, 128 * h:128 * (h + 1)] = 1.0

    shared = {
        "WqT": tr(inputs["L0_attn_Wq"]), "WkT": tr(inputs["L0_attn_Wk"]),
        "WvT": tr(inputs["L0_attn_Wv"]), "WoT": tr(inputs["L0_attn_Wo"]),
        "fmq1T": tr(inputs["L0_attn_fmq_w1"]), "fmq2T": tr(inputs["L0_attn_fmq_w2"]),
        "fmk1T": tr(inputs["L0_attn_fmk_w1"]), "fmk2T": tr(inputs["L0_attn_fmk_w2"]),
        "fmb": np.ascontiguousarray(np.stack([
            np.asarray(inputs["L0_attn_fmq_b1"], np.float32),
            np.asarray(inputs["L0_attn_fmq_b2"], np.float32),
            np.asarray(inputs["L0_attn_fmk_b1"], np.float32),
            np.asarray(inputs["L0_attn_fmk_b2"], np.float32)], axis=1)),
        "rmsw": np.ascontiguousarray(np.asarray(inputs["L0_attn_rms_w"], np.float32)[:, None]),
        "WrT": tr(inputs["L0_ffn_W_r"]), "WkfT": tr(inputs["L0_ffn_W_k"]),
        "WvfT": tr(inputs["L0_ffn_W_v"]),
        "mus": np.ascontiguousarray(np.concatenate([
            np.asarray(inputs["L0_ffn_mu_r"], np.float32).reshape(8, 128).T,
            np.asarray(inputs["L0_ffn_mu_k"], np.float32).reshape(8, 128).T], axis=1)),
        "qpT": np.ascontiguousarray(np.asarray(inputs["L1_attn_q_proj"], np.float32)),
        "kpT": np.ascontiguousarray(np.asarray(inputs["L1_attn_k_proj"], np.float32)),
        "wg_nat": np.ascontiguousarray(np.asarray(inputs["L1_ffn_w_gate"], np.float32)),
        "wu_nat": np.ascontiguousarray(np.asarray(inputs["L1_ffn_w_up"], np.float32)),
        "wd_nat": np.ascontiguousarray(np.asarray(inputs["L1_ffn_w_down"], np.float32)),
        "cst_f": cst_f, "cstf_f": cstf_f, "sel8T_f": sel8,
    }
    in_maps = []
    for core in range(8):
        m = dict(shared)
        m["xT"] = np.ascontiguousarray(xs[core].T)
        in_maps.append(m)
    nc = build_nc()
    res = run_bass_kernel_spmd(nc, in_maps, core_ids=list(range(8)))
    out = np.empty((8, T, D), np.float32)
    for core in range(8):
        out[core] = np.asarray(res.results[core]["outT"]).T
    return out


# version 04
# speedup vs baseline: 1.0144x; 1.0144x over previous
import numpy as np
import ml_dtypes
import concourse.bass as bass
import concourse.mybir as mybir
from concourse.bass_utils import run_bass_kernel_spmd
from contextlib import ExitStack

F32 = mybir.dt.float32
BF16 = mybir.dt.bfloat16
I16 = mybir.dt.int16
U16 = mybir.dt.uint16
AF = mybir.ActivationFunctionType
ALU = mybir.AluOpType
AX = mybir.AxisListType

T = 4096
D = 1024
C8 = 8
TT = 512
NT = T // TT
DI = 3072
MI = DI // 128
N_BLOCKS = 4
DEBUG = False
L1CUT = 'full'

ENGS = ("tensor", "vector", "scalar", "gpsimd", "sync")
Z = ("v", 0)


class Bld:
    def __init__(self):
        self.q = {e: [] for e in ENGS}
        self.cnt = {}

    def op(self, eng, fn, inc=None, amt=1):
        self.q[eng].append(("op", fn, inc, amt))
        if inc is not None:
            self.cnt[inc] = self.cnt.get(inc, 0) + amt
            return self.cnt[inc]
        return None

    def dma(self, eng, fn, sem):
        return self.op(eng, fn, sem, 16)

    def wait(self, eng, sem, val):
        assert val <= self.cnt.get(sem, 0), f"wait {sem}>={val} emitted={self.cnt.get(sem, 0)}"
        if val > 0:
            self.q[eng].append(("wait", sem, val, None))

    def waitp(self, eng, pair):
        if pair and pair[1] > 0:
            self.wait(eng, pair[0], pair[1])

    def drain(self, eng):
        self.q[eng].append(("drain", None, None, None))

    def tick(self, sem):
        return self.cnt.get(sem, 0)


def build_nc():
    nc = bass.Bass("TRN2", target_bir_lowering=False)

    xT = nc.dram_tensor("xT", [D, T], F32, kind="ExternalInput").ap()
    outT = nc.dram_tensor("outT", [D, T], F32, kind="ExternalOutput").ap()
    xmA = nc.dram_tensor("xmA", [D, T], F32).ap()
    dbg = nc.dram_tensor("dbg", [128, 4096], F32, kind="ExternalOutput").ap() if DEBUG else None

    wz = {}
    def win(name, shape):
        wz[name] = nc.dram_tensor(name, shape, F32, kind="ExternalInput").ap()
    for nm in ("WqT", "WkT", "WvT", "WoT", "WrT", "WkfT", "WvfT"):
        win(nm, [D, D])
    for nm in ("fmq1T", "fmq2T", "fmk1T", "fmk2T"):
        win(nm, [128, 128])
    win("fmb", [128, 4]); win("rmsw", [128, 1]); win("mus", [128, 16])
    win("qpT", [D, 256]); win("kpT", [D, 256])
    win("wg_nat", [DI, D]); win("wu_nat", [DI, D]); win("wd_nat", [D, DI])
    win("cst_f", [128, 1024]); win("cstf_f", [128, 384]); win("sel8T_f", [8, 1024])

    wgq = nc.dram_tensor("wgq", [DI, D], BF16).ap()
    wuq = nc.dram_tensor("wuq", [DI, D], BF16).ap()
    wdq = nc.dram_tensor("wdq", [D, DI], BF16).ap()
    sp_gate = nc.dram_tensor("sp_gate", [DI, T], BF16).ap()
    sp_up = nc.dram_tensor("sp_up", [DI, T], BF16).ap()
    sp_t4 = nc.dram_tensor("sp_t4", [DI, T], BF16).ap()
    sp_op = nc.dram_tensor("sp_op", [DI, T], BF16).ap()
    cc_in = nc.dram_tensor("cc_in", [1, 8], F32).ap()
    cc_out = nc.dram_tensor("cc_out", [1, 8], F32).ap()
    wobf = nc.dram_tensor("wobf", [D, D], BF16).ap()

    es = ExitStack()
    S = lambda name, shape, dt: es.enter_context(nc.sbuf_tensor(name, shape, dt))

    XB = S("XB", [128, C8 * T], BF16)
    WB = S("WB", [128, 26624], BF16)
    SB = S("SB", [128, 21504], BF16)
    SF = S("SF", [128, 7680], F32)
    ROWS = S("ROWS", [1, 1536], F32)
    SI = S("SI", [128, 1024], I16)
    SU = S("SU", [128, 512], U16)
    CST = S("CST", [128, 1024], BF16)
    CSTF = S("CSTF", [128, 384], F32)
    SEL = S("SEL", [8, 1024], BF16)
    SELI = S("SELI", [8, 512], BF16)
    FMT = S("FMT", [128, 512], BF16)

    PB = [es.enter_context(nc.psum_tensor(f"PB{i}", [128, 512], F32)) for i in range(6)]
    PT = [es.enter_context(nc.psum_tensor(f"PT{i}", [128, 1024], BF16)) for i in range(2)]

    semlist = ["t", "v", "s", "g", "dS", "dG", "dC", "dX", "cc", "bar"]
    sems = {n: es.enter_context(nc.semaphore(n)) for n in semlist}
    b = Bld()

    IDT = CST[:, 0:128]
    ONESC = CST[:, 256:257]
    ONES8 = CST[:, 320:384]
    FM = {"q1": FMT[:, 0:128], "q2": FMT[:, 128:256],
          "k1": FMT[:, 256:384], "k2": FMT[:, 384:512]}
    RMSW = CSTF[:, 0:1]
    MUR = lambda c: CSTF[:, 16 + c:17 + c]
    MUK = lambda c: CSTF[:, 24 + c:25 + c]
    FMB = {"q1": CSTF[:, 32:33], "q2": CSTF[:, 33:34],
           "k1": CSTF[:, 34:35], "k2": CSTF[:, 35:36]}
    COL = lambda j: CSTF[:, 64 + j:65 + j]
    EPS5 = CSTF[:, 40:41]
    XLAST = CSTF[:, 96:104]
    IDTF = CSTF[:, 128:256]
    ROW1 = CSTF[0:1, 256:384]

    xb_ready = [0] * NT

    def MM(out_ap, lhsT, rhs, start, stop, inc=None):
        return b.op("tensor", lambda e, o=out_ap, l=lhsT, r=rhs, s1=start, s2=stop:
                    e.matmul(o, l, r, start=s1, stop=s2), inc)

    def TR(out_ap, in_ap, ident=None, inc=None):
        idt = IDT if ident is None else ident
        return b.op("tensor", lambda e, o=out_ap, i=in_ap, d=idt: e.transpose(o, i, d), inc)

    def V(fn, inc=None):
        return b.op("vector", fn, inc)

    def SC(fn, inc=None):
        return b.op("scalar", fn, inc)

    def barrier():
        b.wait("sync", "dS", b.tick("dS"))
        b.wait("sync", "dX", b.tick("dX"))
        b.wait("gpsimd", "dG", b.tick("dG"))
        b.wait("gpsimd", "dC", b.tick("dC"))
        b.wait("gpsimd", "cc", b.tick("cc"))
        for e in ENGS:
            b.op(e, lambda eng: eng.nop(), "bar")
        tgt = b.tick("bar")
        for e in ENGS:
            b.wait(e, "bar", tgt)

    b.dma("gpsimd", lambda e: e.dma_start(
        out=bass.AP(XB, 0, [[C8 * T, 128], [T, 8], [1, T]]),
        in_=bass.AP(xT.tensor, 0, [[T, 128], [T * 128, 8], [1, T]])), "dG")
    b.dma("sync", lambda e: e.dma_start(out=xmA, in_=xT), "dX")
    t_cstload = b.dma("gpsimd", lambda e: e.dma_start(out=CST[:, 0:1024], in_=wz["cst_f"]), "dG")
    b.dma("gpsimd", lambda e: e.dma_start(out=SEL[:, :], in_=wz["sel8T_f"]), "dC")
    tcf = b.dma("sync", lambda e: e.dma_start(out=CSTF[:, 0:384], in_=wz["cstf_f"]), "dS")

    def load_wb(names, width=1024):
        tks = []
        for j, nm in enumerate(names):
            for c in range(C8):
                o = j * C8 * width + c * width
                tk = b.dma("gpsimd", lambda e, nm=nm, c=c, o=o, w=width: e.dma_start(
                    out=WB[:, o:o + w], in_=wz[nm][128 * c:128 * (c + 1), 0:w]), "dG")
            tks.append(tk)
        return tks

    twq = load_wb(["WqT", "WkT", "WvT"])
    t_wobf = b.dma("gpsimd", lambda e: e.dma_start(out=wobf, in_=wz["WoT"]), "dC")
    WQ = lambda K, M: WB[:, 1024 * K + 128 * M: 1024 * K + 128 * (M + 1)]
    WKa = lambda K, M: WB[:, 8192 + 1024 * K + 128 * M: 8192 + 1024 * K + 128 * (M + 1)]
    WVrow = lambda K: WB[:, 16384 + 1024 * K: 16384 + 1024 * (K + 1)]
    WOW = lambda i, K: WB[:, 24576 + 1024 * (i % 2) + 128 * K: 24576 + 1024 * (i % 2) + 128 * (K + 1)]

    for i in range(NT):
        xb_ready[i] = b.tick("dG")
    b.wait("gpsimd", "dG", b.tick("dG"))
    b.wait("gpsimd", "dC", b.tick("dC"))
    t_init = b.op("gpsimd", lambda e: e.nop(), "g")
    b.wait("sync", "dS", tcf)
    b.wait("vector", "g", t_init)
    V(lambda e: e.memset(ROWS[0:1, 0:128], 0.0))
    b.drain("vector")
    for nm, key in [("fmq1T", "q1"), ("fmq2T", "q2"), ("fmk1T", "k1"), ("fmk2T", "k2")]:
        tk = b.dma("sync", lambda e, nm=nm: e.dma_start(out=SF[:, 0:128], in_=wz[nm]), "dS")
        b.wait("vector", "dS", tk)
        tv = V(lambda e, key=key: e.tensor_scalar(FM[key], SF[:, 0:128], 1.0, None, ALU.mult), "v")
        b.wait("sync", "v", tv)
    tk = b.dma("sync", lambda e: e.dma_start(out=SF[:, 0:4], in_=wz["fmb"]), "dS")
    b.wait("vector", "dS", tk)
    V(lambda e: e.tensor_scalar(CSTF[:, 32:33], SF[:, 0:1], float(128 ** -0.5), None, ALU.mult))
    tv = V(lambda e: e.tensor_copy(CSTF[:, 33:36], SF[:, 1:4]), "v")
    b.wait("sync", "v", tv)
    tk = b.dma("sync", lambda e: e.dma_start(out=SF[:, 0:1], in_=wz["rmsw"]), "dS")
    b.wait("vector", "dS", tk)
    tv = V(lambda e: e.tensor_copy(CSTF[:, 0:1], SF[:, 0:1]), "v")
    b.wait("sync", "v", tv)
    tk = b.dma("sync", lambda e: e.dma_start(out=SF[:, 0:16], in_=wz["mus"]), "dS")
    b.wait("vector", "dS", tk)
    tv = V(lambda e: e.tensor_copy(CSTF[:, 16:32], SF[:, 0:16]), "v")
    b.wait("sync", "v", tv)

    def dequant(src, dst, nrow_chunks, width):
        nb = 32
        nslab = width // 1024
        for r in range(nrow_chunks):
            for sl in range(nslab):
                tk = b.dma("sync", lambda e, r=r, sl=sl, src=src: e.dma_start(
                    out=SF[:, 0:1024], in_=src[128 * r:128 * (r + 1), 1024 * sl:1024 * (sl + 1)]), "dS")
                b.wait("scalar", "dS", tk)
                tks = SC(lambda e: e.activation(SF[:, 1024:2048], SF[:, 0:1024], AF.Abs), "s")
                b.wait("vector", "s", tks)
                hw = 16
                while hw >= 2:
                    V(lambda e, hw=hw: e.tensor_tensor(
                        bass.AP(SF, 1024, [[7680, 128], [32, nb], [1, hw]]),
                        bass.AP(SF, 1024, [[7680, 128], [32, nb], [1, hw]]),
                        bass.AP(SF, 1024 + hw, [[7680, 128], [32, nb], [1, hw]]), ALU.max))
                    b.drain("vector")
                    hw //= 2
                V(lambda e: e.tensor_tensor(
                    bass.AP(SF, 2048, [[7680, 128], [1, nb]]),
                    bass.AP(SF, 1024, [[7680, 128], [32, nb]]),
                    bass.AP(SF, 1025, [[7680, 128], [32, nb]]), ALU.max))
                b.drain("vector")
                V(lambda e: e.tensor_scalar(
                    SF[:, 2048:2048 + nb], SF[:, 2048:2048 + nb], 1.0 / 7.0, 1e-12,
                    ALU.mult, ALU.max))
                b.drain("vector")
                V(lambda e: e.reciprocal(SF[:, 2112:2112 + nb], SF[:, 2048:2048 + nb]))
                b.drain("vector")
                V(lambda e: e.tensor_tensor(
                    bass.AP(SI, 0, [[1024, 128], [32, nb], [1, 32]]),
                    bass.AP(SF, 0, [[7680, 128], [32, nb], [1, 32]]),
                    bass.AP(SF, 2112, [[7680, 128], [1, nb], [0, 32]]), ALU.mult), "v")
                b.drain("vector")
                tvw = V(lambda e: e.tensor_tensor(
                    bass.AP(SB, 0, [[21504, 128], [32, nb], [1, 32]]),
                    bass.AP(SI, 0, [[1024, 128], [32, nb], [1, 32]]),
                    bass.AP(SF, 2048, [[7680, 128], [1, nb], [0, 32]]), ALU.mult), "v")
                b.wait("sync", "v", tvw)
                tst = b.dma("sync", lambda e, r=r, sl=sl, dst=dst: e.dma_start(
                    out=dst[128 * r:128 * (r + 1), 1024 * sl:1024 * (sl + 1)], in_=SB[:, 0:1024]), "dS")
                b.wait("sync", "s", tks)
                b.wait("sync", "dS", tst)

    b.wait("sync", "v", b.tick("v"))
    dequant(wz["wg_nat"], wgq, MI, D)
    dequant(wz["wu_nat"], wuq, MI, D)
    dequant(wz["wd_nat"], wdq, C8, DI)
    barrier()

    BINV = SF[:, 0:512]
    BMINV = SF[:, 512:1024]
    t_stats = [0]

    def ln_tiles(nt, sq_cols, ba, bb, pre_t=(), pre_s=()):
        xw = xb_ready[nt]
        b.wait("scalar", "dG", xw)
        b.wait("tensor", "dG", xw)
        for p in pre_s:
            b.waitp("scalar", p)
        for p in pre_t:
            b.waitp("tensor", p)
        for c in range(C8):
            MM(ba[0:1, :], ONESC, XB[:, T * c + TT * nt: T * c + TT * nt + TT],
               c == 0, c == C8 - 1, "t" if c == C8 - 1 else None)
        tmmq = [0] * C8
        for c in range(C8):
            if c >= 2:
                b.wait("scalar", "t", tmmq[c - 2])
            tsq = SC(lambda e, c=c, o=sq_cols, nt=nt: e.activation(
                SB[:, o + 512 * (c % 2): o + 512 * (c % 2) + 512],
                XB[:, T * c + TT * nt: T * c + TT * nt + TT], AF.Square), "s")
            b.wait("tensor", "s", tsq)
            tmmq[c] = MM(bb[0:1, :], ONESC,
                         SB[:, sq_cols + 512 * (c % 2): sq_cols + 512 * (c % 2) + 512],
                         c == 0, c == C8 - 1, "t")
        t_stats[0] = tmmq[-1]
        tk = b.tick("t")
        b.wait("vector", "t", tk)
        V(lambda e, ba=ba: e.tensor_scalar(ROWS[0:1, 0:512], ba[0:1, :], 1.0 / D, None, ALU.mult))
        V(lambda e, bb=bb: e.tensor_scalar(ROWS[0:1, 512:1024], bb[0:1, :], 1.0 / D, None, ALU.mult), "v")
        V(lambda e: e.tensor_mul(ROWS[0:1, 1024:1536], ROWS[0:1, 0:512], ROWS[0:1, 0:512]))
        b.drain("vector")
        tv = V(lambda e: e.tensor_sub(ROWS[0:1, 512:1024], ROWS[0:1, 512:1024], ROWS[0:1, 1024:1536]), "v")
        b.wait("scalar", "v", tv)
        ts2 = SC(lambda e: e.activation(ROWS[0:1, 1024:1536], ROWS[0:1, 512:1024],
                                        AF.Sqrt, bias=EPS5[0:1, :], scale=1.0), "s")
        b.wait("vector", "s", ts2)
        V(lambda e: e.reciprocal(ROWS[0:1, 512:1024], ROWS[0:1, 1024:1536]))
        b.drain("vector")
        tv = V(lambda e: e.tensor_mul(ROWS[0:1, 1024:1536], ROWS[0:1, 0:512], ROWS[0:1, 512:1024]), "v")
        b.wait("tensor", "v", tv)
        MM(ba[:, :], ROW1, ROWS[0:1, 512:1024], True, True)
        tk = MM(bb[:, :], ROW1, ROWS[0:1, 1024:1536], True, True, "t")
        b.wait("vector", "t", tk)
        V(lambda e, ba=ba: e.tensor_copy(BINV, ba[:, :]))
        tv = V(lambda e, bb=bb: e.tensor_copy(BMINV, bb[:, :]), "v")
        b.drain("vector")
        return tv

    def bcast_col(rc, j, bank):
        V(lambda e, rc=rc: e.tensor_tensor(
            ROWS[0:1, 1280:1408], ROW1,
            bass.AP(ROWS, rc, [[1536, 1], [0, 128]]), ALU.mult), "v")
        b.drain("vector")
        b.wait("tensor", "v", b.tick("v"))
        tk = MM(bank[:, 0:128], ROW1, ROWS[0:1, 1280:1408], True, True, "t")
        b.wait("vector", "t", tk)
        tv = V(lambda e, j=j, bank=bank: e.tensor_copy(COL(j), bank[:, 0:1]), "v")
        b.drain("vector")
        return tv

    XNEW = lambda m: SF[:, 5632 + 512 * (m % 2): 5632 + 512 * (m % 2) + 512]

    def mk_residual(nt, master):
        tks = []
        def accum(m, tv):
            b.wait("gpsimd", "v", tv)
            tk = b.dma("gpsimd", lambda e, m=m, nt=nt, ms=master: e.dma_start(
                out=ms[128 * m:128 * (m + 1), TT * nt: TT * nt + TT],
                in_=XNEW(m), accum_op=ALU.add), "dG")
            tks.append(tk)
            return tk
        def readback():
            b.wait("gpsimd", "dG", tks[-1])
            tk = b.dma("gpsimd", lambda e, nt=nt, ms=master: e.dma_start(
                out=bass.AP(XB, TT * nt, [[C8 * T, 128], [T, 8], [1, TT]]),
                in_=bass.AP(ms.tensor, TT * nt, [[T, 128], [T * 128, 8], [1, TT]])), "dG")
            xb_ready[nt] = tk
            return tk
        return accum, readback

    def dump(dst_cols, src_ap):
        b.wait("gpsimd", "v", b.tick("v"))
        b.wait("gpsimd", "s", b.tick("s"))
        tk = b.dma("gpsimd", lambda e, d=dst_cols, s_=src_ap: e.dma_start(
            out=dbg[:, d:d + 512], in_=s_), "dG")
        for eng in ("vector", "scalar", "tensor"):
            b.wait(eng, "dG", tk)

    def l0_attn(master):
        XH = lambda c: SB[:, 512 * c: 512 * (c + 1)]
        QF = lambda h: SB[:, 4096 + 512 * h: 4096 + 512 * (h + 1)]
        KF = lambda h: SB[:, 8192 + 512 * h: 8192 + 512 * (h + 1)]
        OBF = lambda h: SB[:, 12288 + 512 * h: 12288 + 512 * (h + 1)]
        VT = lambda s: SB[:, 16384 + 1024 * (s % 2): 16384 + 1024 * (s % 2) + 1024]
        KFT = lambda g: SB[:, 18432 + 512 * g: 18432 + 512 * (g + 1)]
        ATM = KFT
        QRAW = lambda i: SB[:, 19456 + 512 * (i % 2): 19456 + 512 * (i % 2) + 512]
        SBH = lambda h: SB[:, 20480 + 128 * h: 20480 + 128 * (h + 1)]
        SST = SF[:, 2048:3072]
        A2 = SF[:, 3072:3584]
        A1 = SF[:, 3584:4096]
        MSK4 = lambda g: bass.AP(CST, 128, [[1024, 128], [0, 4], [1, 128]])
        ATM3 = lambda g: bass.AP(SB, 18432 + 512 * g, [[21504, 128], [128, 4], [1, 128]])
        AT3 = lambda bank: bass.AP(bank, 0, [[512, 128], [128, 4], [1, 128]])

        b.wait("tensor", "dG", twq[-1])
        V(lambda e: e.memset(SST, 0.0))
        tv0 = V(lambda e: e.memset(SB[:, 20480:21504], 0.0), "v")
        b.wait("tensor", "v", tv0)

        if DEBUG:
            tv_ = V(lambda e: e.tensor_copy(SF[:, 3712:3840], FM["q1"]), "v")
            b.wait("gpsimd", "v", tv_)
            tkw = b.dma("gpsimd", lambda e: e.dma_start(out=dbg[:, 3200:3328], in_=SF[:, 3712:3840]), "dG")
            for eng_ in ("vector", "scalar", "tensor"):
                b.wait(eng_, "dG", tkw)
        bankf = {i: Z for i in range(6)}
        tptf = [Z, Z]
        kft_guard = [0, 0]
        a2_guard = [0]
        wo_prev = [0]
        accum_hold = [0, 0]
        tms_prev = 0
        wo_hold = [0, 0]

        for nt in range(NT):
            tln = ln_tiles(nt, 19456, PB[0], PB[1],
                           pre_t=(bankf[0], bankf[1]),
                           pre_s=((("t", tms_prev)) ,) if tms_prev else ())
            bankf[0] = bankf[1] = ("v", tln)
            b.wait("vector", "t", b.tick("t"))
            for cp in range(0, C8, 2):
                for c in (cp, cp + 1):
                    V(lambda e, c=c, nt=nt: e.tensor_tensor(
                        SF[:, 1024 + 512 * (c % 2): 1024 + 512 * (c % 2) + 512],
                        XB[:, T * c + TT * nt: T * c + TT * nt + TT], BINV, ALU.mult))
                for c in (cp, cp + 1):
                    V(lambda e, c=c: e.tensor_tensor(
                        XH(c), SF[:, 1024 + 512 * (c % 2): 1024 + 512 * (c % 2) + 512],
                        BMINV, ALU.subtract), "v")
            txh = b.tick("v")

            for Wf, fm1, fm2, fb1, fb2, DST in (
                    (WQ, FM["q1"], FM["q2"], FMB["q1"], FMB["q2"], QF),
                    (WKa, FM["k1"], FM["k2"], FMB["k1"], FMB["k2"], KF)):
                for h in range(C8):
                    bank = PB[2 + h % 2]
                    b.wait("tensor", "v", txh)
                    b.waitp("tensor", bankf[2 + h % 2])
                    for K in range(C8):
                        MM(bank[:, :], Wf(K, h), XH(K), K == 0, K == C8 - 1,
                           "t" if K == C8 - 1 else None)
                    tmm = b.tick("t")
                    b.wait("vector", "t", tmm)
                    tqr = V(lambda e, h=h, bank=bank: e.tensor_copy(QRAW(h), bank[:, :]), "v")
                    if DEBUG and nt == 0 and h == 0 and DST is QF:
                        dump(0, QRAW(0))
                    b.drain("vector")
                    tqr = V(lambda e: e.engine_nop(), "v")
                    bankf[2 + h % 2] = ("v", tqr)
                    b.wait("tensor", "v", tqr)
                    b.waitp("tensor", bankf[4])
                    b.waitp("tensor", bankf[5])
                    MM(PB[4][:, :], fm1, QRAW(h), True, True, "t")
                    t45 = MM(PB[5][:, :], fm2, QRAW(h), True, True, "t")
                    b.wait("scalar", "t", t45)
                    b.wait("scalar", "v", a2_guard[0])
                    sc_ = float(128 ** -0.5) if DST is QF else 1.0
                    SC(lambda e, fb1=fb1, sc_=sc_: e.activation(
                        A1, PB[4][:, :], AF.Identity, bias=fb1, scale=sc_), "s")
                    b.drain("scalar")
                    ts1 = SC(lambda e, fb2=fb2: e.activation(
                        A2, PB[5][:, :], AF.Identity, bias=fb2, scale=1.0), "s")
                    if DEBUG and nt == 0 and h == 0 and DST is QF:
                        b.wait("gpsimd", "s", b.tick("s"))
                        tk_ = b.dma("gpsimd", lambda e: e.dma_start(out=dbg[:, 512:1024], in_=A1), "dG")
                        tk_ = b.dma("gpsimd", lambda e: e.dma_start(out=dbg[:, 1024:1536], in_=A2), "dG")
                        for eng_ in ("vector", "scalar", "tensor"):
                            b.wait(eng_, "dG", tk_)
                    b.wait("vector", "s", ts1)
                    tvq = V(lambda e, h=h, DST=DST: e.tensor_tensor(
                        DST(h), A1, A2, ALU.mult), "v")
                    a2_guard[0] = tvq
                    if DEBUG and nt == 0 and DST is QF and h == 0:
                        dump(1536, QF(0))
                        b.wait("gpsimd", "v", b.tick("v"))

                    bankf[4] = bankf[5] = ("v", tvq)
            tqk = b.tick("v")
            if DEBUG and nt == 0:
                dump(2048, QF(0))
            for s_ in range(4):
                b.wait("tensor", "v", tqk)
                b.waitp("tensor", bankf[2])
                b.waitp("tensor", bankf[3])
                for K in range(C8):
                    MM(PB[2][:, :], XH(K)[:, 128 * s_:128 * (s_ + 1)], WVrow(K)[:, 0:512],
                       K == 0, K == C8 - 1, "t" if K == C8 - 1 else None)
                for K in range(C8):
                    MM(PB[3][:, :], XH(K)[:, 128 * s_:128 * (s_ + 1)], WVrow(K)[:, 512:1024],
                       K == 0, K == C8 - 1, "t" if K == C8 - 1 else None)
                tvm = b.tick("t")
                b.wait("vector", "t", tvm)
                V(lambda e, s_=s_: e.tensor_copy(VT(s_)[:, 0:512], PB[2][:, :]))
                tvt = V(lambda e, s_=s_: e.tensor_copy(VT(s_)[:, 512:1024], PB[3][:, :]), "v")
                bankf[2] = bankf[3] = ("v", tvt)
                for g in range(2):
                    b.waitp("tensor", tptf[g])
                    for j in range(4):
                        h = 4 * g + j
                        TR(PT[g][:, 128 * j:128 * (j + 1)], KF(h)[:, 128 * s_:128 * (s_ + 1)],
                           inc="t" if j == 3 else None)
                    ttr = b.tick("t")
                    b.wait("vector", "t", ttr)
                    if kft_guard[g]:
                        b.wait("vector", "t", kft_guard[g])
                    tke = V(lambda e, g=g: e.tensor_copy(KFT(g), PT[g][:, 0:512]), "v")
                    tptf[g] = ("v", tke)
                    atb, kvb_, ob = PB[4 + g], PB[0 + g], PB[2 + g]
                    b.waitp("tensor", bankf[4 + g])
                    for j in range(4):
                        h = 4 * g + j
                        MM(atb[:, 128 * j:128 * (j + 1)], KF(h)[:, 128 * s_:128 * (s_ + 1)],
                           QF(h)[:, 128 * s_:128 * (s_ + 1)], True, True,
                           "t" if j == 3 else None)
                    tat = b.tick("t")
                    b.wait("tensor", "v", max(tke, tvt))
                    b.waitp("tensor", bankf[0 + g])
                    for j in range(4):
                        h = 4 * g + j
                        MM(kvb_[:, 128 * j:128 * (j + 1)], KFT(g)[:, 128 * j:128 * (j + 1)],
                           VT(s_)[:, 128 * h:128 * (h + 1)], True, True,
                           "t" if j == 3 else None)
                    tkv = b.tick("t")
                    b.wait("vector", "t", max(tat, tkv))
                    tam = V(lambda e, g=g, atb=atb: e.tensor_tensor(
                        ATM3(g), AT3(atb), MSK4(g), ALU.mult), "v")
                    bankf[4 + g] = ("v", tam)
                    b.wait("tensor", "v", tam)
                    b.waitp("tensor", bankf[2 + g])
                    for j in range(4):
                        h = 4 * g + j
                        MM(ob[:, 128 * j:128 * (j + 1)], SBH(h),
                           QF(h)[:, 128 * s_:128 * (s_ + 1)], True, False)
                        MM(ob[:, 128 * j:128 * (j + 1)], VT(s_)[:, 128 * h:128 * (h + 1)],
                           ATM(g)[:, 128 * j:128 * (j + 1)], False, True,
                           "t" if j == 3 else None)
                    tob = b.tick("t")
                    kft_guard[g] = tob
                    b.wait("vector", "t", tkv)
                    V(lambda e, g=g, kvb_=kvb_: e.tensor_tensor(
                        SST[:, 512 * g:512 * (g + 1)], SST[:, 512 * g:512 * (g + 1)],
                        kvb_[:, :], ALU.add), "v")
                    bankf[0 + g] = ("v", b.tick("v"))
                    b.drain("vector")
                    b.wait("vector", "t", tob)
                    V(lambda e, g=g: e.tensor_copy(
                        SB[:, 20480 + 512 * g: 20480 + 512 * (g + 1)],
                        SST[:, 512 * g:512 * (g + 1)]), "v")
                    b.wait("scalar", "t", tob)
                    tse = SC(lambda e, s_=s_, g=g, ob=ob: e.activation(
                        bass.AP(SB, 12288 + 512 * (4 * g) + 128 * s_,
                                [[21504, 128], [512, 4], [1, 128]]),
                        ob[:, :], AF.Copy), "s")
                    bankf[2 + g] = ("s", tse)
            tchunk_s = b.tick("s")

            b.drain("scalar")
            tsql = [0] * C8
            for h in range(C8):
                if h >= 2:
                    b.wait("scalar", "t", tsql[h - 2])
                tss = SC(lambda e, h=h: e.activation(QRAW(h), OBF(h), AF.Square), "s")
                b.drain("scalar")
                b.wait("tensor", "s", tss)
                if h == 0:
                    b.waitp("tensor", bankf[0])
                tsql[h] = MM(PB[0][0:8, :], ONES8[:, 8 * h:8 * h + 8], QRAW(h),
                             h == 0, h == C8 - 1, "t")
            tms = tsql[-1]
            tms_prev = tms
            b.wait("scalar", "t", tms)
            ts3 = SC(lambda e: e.activation(SF[0:8, 4608:5120], PB[0][0:8, :],
                                            AF.Sqrt, bias=EPS5[0:8, :], scale=1.0 / 128.0), "s")
            bankf[0] = ("s", ts3)
            b.wait("vector", "s", ts3)
            V(lambda e: e.reciprocal(SF[0:8, 5120:5632], SF[0:8, 4608:5120]))
            b.drain("vector")
            tv4 = V(lambda e: e.tensor_copy(SELI[:, :], SF[0:8, 5120:5632]), "v")
            b.wait("tensor", "v", tv4)
            b.wait("tensor", "s", tchunk_s)
            ton = 0
            for h in range(C8):
                bank = PB[1 + h % 2]
                b.waitp("tensor", bankf[1 + h % 2])
                tbi = MM(bank[:, :], SEL[:, 128 * h:128 * (h + 1)], SELI[:, :],
                         True, True, "t")
                b.wait("vector", "t", tbi)
                ton = V(lambda e, h=h, bank=bank: e.scalar_tensor_tensor(
                    QF(h), OBF(h), RMSW, bank[:, :], ALU.mult, ALU.mult), "v")
                bankf[1 + h % 2] = ("v", ton)
            accum, readback = mk_residual(nt, master)

            b.wait("tensor", "v", ton)
            for m in range(C8):
                if wo_hold[m % 2]:
                    b.wait("sync", "t", wo_hold[m % 2])
                else:
                    b.wait("sync", "dS", b.tick("dS"))
                wo_tk = b.dma("sync", lambda e, m=m: e.dma_start(
                    out=bass.AP(WB, 24576 + 1024 * (m % 2), [[26624, 128], [128, 8], [1, 128]]),
                    in_=bass.AP(wobf.tensor, 128 * m, [[D, 128], [D * 128, 8], [1, 128]])), "dS")
                wo_prev[0] = wo_tk
                bank = PB[3 + m % 2]
                b.waitp("tensor", bankf[3 + m % 2])
                b.wait("tensor", "dS", wo_tk)
                for K in range(C8):
                    MM(bank[:, :], WOW(m, K), QF(K), K == 0, K == C8 - 1,
                       "t" if K == C8 - 1 else None)
                two = b.tick("t")
                wo_hold[m % 2] = two
                b.wait("vector", "t", two)
                b.wait("vector", "dG", accum_hold[m % 2])
                tvx = V(lambda e, m=m, bank=bank: e.tensor_copy(XNEW(m), bank[:, :]), "v")
                bankf[3 + m % 2] = ("v", tvx)
                accum_hold[m % 2] = accum(m, tvx)
            readback()

    def l0_ffn(master):
        XR = lambda c: SB[:, 512 * c: 512 * (c + 1)]
        XK = lambda c: SB[:, 4096 + 512 * c: 4096 + 512 * (c + 1)]
        KACT = lambda c: SB[:, 8192 + 512 * c: 8192 + 512 * (c + 1)]
        RSIG = lambda m: SF[:, 1024 + 512 * m: 1024 + 512 * (m + 1)]
        XHT = lambda i: SF[:, 6656 + 512 * (i % 2): 6656 + 512 * (i % 2) + 512]
        DT_ = SF[:, 5120:5632]

        tw = load_wb(["WrT", "WkfT", "WvfT"])
        WR = lambda K, M: WB[:, 1024 * K + 128 * M: 1024 * K + 128 * (M + 1)]
        WKF = lambda K, M: WB[:, 8192 + 1024 * K + 128 * M: 8192 + 1024 * K + 128 * (M + 1)]
        WVF = lambda K, M: WB[:, 16384 + 1024 * K + 128 * M: 16384 + 1024 * K + 128 * (M + 1)]
        b.wait("tensor", "dG", tw[-1])
        V(lambda e: e.memset(CSTF[:, 96:104], 0.0), "v")

        bankf = {i: Z for i in range(6)}
        accum_hold = [0, 0]
        tka_prev = 0
        for nt in range(NT):
            tln = ln_tiles(nt, 19456, PB[0], PB[1], pre_t=(bankf[0], bankf[1]))
            bankf[0] = bankf[1] = ("v", tln)
            if tka_prev:
                b.wait("vector", "s", tka_prev)
            b.wait("vector", "t", b.tick("t"))
            for c in range(C8):
                V(lambda e, c=c, nt=nt: e.tensor_tensor(
                    XHT(c), XB[:, T * c + TT * nt: T * c + TT * nt + TT], BINV, ALU.mult))
                b.drain("vector")
                V(lambda e, c=c: e.tensor_tensor(XHT(c), XHT(c), BMINV, ALU.subtract))
                b.drain("vector")
                V(lambda e, c=c: e.tensor_tensor(
                    DT_[:, 1:512], XHT(c)[:, 1:512], XHT(c)[:, 0:511], ALU.subtract))
                V(lambda e, c=c: e.tensor_tensor(
                    DT_[:, 0:1], XHT(c)[:, 0:1], XLAST[:, c:c + 1], ALU.subtract))
                b.drain("vector")
                V(lambda e, c=c: e.scalar_tensor_tensor(
                    XR(c)[:, 1:512], DT_[:, 1:512], MUR(c), XHT(c)[:, 0:511],
                    ALU.mult, ALU.add))
                V(lambda e, c=c: e.scalar_tensor_tensor(
                    XR(c)[:, 0:1], DT_[:, 0:1], MUR(c), XLAST[:, c:c + 1],
                    ALU.mult, ALU.add))
                V(lambda e, c=c: e.scalar_tensor_tensor(
                    XK(c)[:, 1:512], DT_[:, 1:512], MUK(c), XHT(c)[:, 0:511],
                    ALU.mult, ALU.add))
                V(lambda e, c=c: e.scalar_tensor_tensor(
                    XK(c)[:, 0:1], DT_[:, 0:1], MUK(c), XLAST[:, c:c + 1],
                    ALU.mult, ALU.add), "v")
                b.drain("vector")
                V(lambda e, c=c: e.tensor_copy(XLAST[:, c:c + 1], XHT(c)[:, 511:512]), "v")
                b.drain("vector")
            txr = b.tick("v")
            for m in range(C8):
                bank = PB[2 + m % 2]
                b.wait("tensor", "v", txr)
                b.waitp("tensor", bankf[2 + m % 2])
                for K in range(C8):
                    MM(bank[:, :], WR(K, m), XR(K), K == 0, K == C8 - 1,
                       "t" if K == C8 - 1 else None)
                tr_ = b.tick("t")
                b.wait("scalar", "t", tr_)
                trs = SC(lambda e, m=m, bank=bank: e.activation(
                    RSIG(m), bank[:, :], AF.Sigmoid), "s")
                bankf[2 + m % 2] = ("s", trs)
            for m in range(C8):
                bank = PB[4 + m % 2]
                b.wait("tensor", "v", txr)
                b.waitp("tensor", bankf[4 + m % 2])
                for K in range(C8):
                    MM(bank[:, :], WKF(K, m), XK(K), K == 0, K == C8 - 1,
                       "t" if K == C8 - 1 else None)
                tk_ = b.tick("t")
                b.wait("scalar", "t", tk_)
                SC(lambda e, m=m, bank=bank: e.activation(XHT(m), bank[:, :], AF.Relu), "s")
                b.drain("scalar")
                tka = SC(lambda e, m=m: e.activation(KACT(m), XHT(m), AF.Square), "s")
                bankf[4 + m % 2] = ("s", tka)
            tka_prev = b.tick("s")
            accum, readback = mk_residual(nt, master)
            for m in range(C8):
                bank = PB[2 + m % 2]
                b.wait("tensor", "s", tka_prev)
                b.waitp("tensor", bankf[2 + m % 2])
                for K in range(C8):
                    MM(bank[:, :], WVF(K, m), KACT(K), K == 0, K == C8 - 1,
                       "t" if K == C8 - 1 else None)
                tvm = b.tick("t")
                b.wait("vector", "t", tvm)
                b.wait("vector", "s", tka_prev)
                b.wait("vector", "dG", accum_hold[m % 2])
                tvx = V(lambda e, m=m, bank=bank: e.tensor_tensor(
                    XNEW(m), bank[:, :], RSIG(m), ALU.mult), "v")
                bankf[2 + m % 2] = ("v", tvx)
                accum_hold[m % 2] = accum(m, tvx)
            readback()

    def l1_attn(master):
        XH = lambda c: SB[:, 512 * c: 512 * (c + 1)]
        QP = lambda e_, nt: SB[:, 4096 + 4096 * e_ + 512 * nt: 4096 + 4096 * e_ + 512 * nt + 512]
        KP = lambda e_: SB[:, 12288 + 512 * e_: 12288 + 512 * (e_ + 1)]
        XHT_ = SB[:, 13312:14336]
        KPT = SB[:, 14336:14592]
        KVB = lambda e_: SB[:, 14592 + 1024 * e_: 14592 + 1024 * (e_ + 1)]
        BZ = SF[:, 1024:1536]

        tw = load_wb(["qpT", "kpT"], width=256)
        QPJ = lambda K, E: WB[:, 256 * K + 128 * E: 256 * K + 128 * (E + 1)]
        KPJ = lambda K, E: WB[:, 2048 + 256 * K + 128 * E: 2048 + 256 * K + 128 * (E + 1)]
        b.wait("tensor", "dG", tw[-1])

        bank45 = [Z, Z]
        tptf = [Z, Z]
        for nt in range(NT):
            tln = ln_tiles(nt, 19456, PB[4], PB[5], pre_t=(bank45[0], bank45[1]))
            bank45[0] = bank45[1] = ("v", tln)
            b.wait("vector", "t", b.tick("t"))
            for cp in range(0, C8, 2):
                for c in (cp, cp + 1):
                    V(lambda e, c=c, nt=nt: e.tensor_tensor(
                        SF[:, 1536 + 512 * (c % 2): 1536 + 512 * (c % 2) + 512],
                        XB[:, T * c + TT * nt: T * c + TT * nt + TT], BINV, ALU.mult))
                for c in (cp, cp + 1):
                    V(lambda e, c=c: e.tensor_tensor(
                        XH(c), SF[:, 1536 + 512 * (c % 2): 1536 + 512 * (c % 2) + 512],
                        BMINV, ALU.subtract), "v")
            txh = b.tick("v")
            for e_ in range(2):
                b.wait("tensor", "v", txh)
                b.waitp("tensor", bank45[0])
                for K in range(C8):
                    MM(PB[4][:, :], QPJ(K, e_), XH(K), K == 0, K == C8 - 1,
                       "t" if K == C8 - 1 else None)
                tmm = b.tick("t")
                b.wait("scalar", "t", tmm)
                te = SC(lambda e, e_=e_, nt=nt: e.activation(
                    QP(e_, nt), PB[4][:, :], AF.Exp), "s")
                bank45[0] = ("s", te)
                b.waitp("tensor", bank45[1])
                for K in range(C8):
                    MM(PB[5][:, :], KPJ(K, e_), XH(K), K == 0, K == C8 - 1,
                       "t" if K == C8 - 1 else None)
                tmm = b.tick("t")
                b.wait("scalar", "t", tmm)
                te = SC(lambda e, e_=e_: e.activation(KP(e_), PB[5][:, :], AF.Exp), "s")
                bank45[1] = ("s", te)
            tqk = b.tick("s")
            for s_ in range(4):
                b.wait("tensor", "s", tqk)
                for half in range(2):
                    b.waitp("tensor", tptf[half])
                    for j in range(4):
                        c = 4 * half + j
                        TR(PT[half][:, 128 * j:128 * (j + 1)],
                           XH(c)[:, 128 * s_:128 * (s_ + 1)],
                           inc="t" if j == 3 else None)
                    ttr = b.tick("t")
                    b.wait("vector", "t", ttr)
                    tev = V(lambda e, half=half: e.tensor_copy(
                        XHT_[:, 512 * half:512 * (half + 1)], PT[half][:, 0:512]), "v")
                    tptf[half] = ("v", tev)
                b.waitp("tensor", tptf[0])
                for e_ in range(2):
                    TR(PT[0][:, 512 + 128 * e_: 512 + 128 * (e_ + 1)],
                       KP(e_)[:, 128 * s_:128 * (s_ + 1)],
                       inc="t" if e_ == 1 else None)
                ttr = b.tick("t")
                b.wait("vector", "t", ttr)
                tev = V(lambda e: e.tensor_copy(KPT[:, 0:256], PT[0][:, 512:768]), "v")
                tptf[0] = ("v", tev)
                b.wait("tensor", "v", tev)
                first = nt == 0 and s_ == 0
                lastk = nt == NT - 1 and s_ == 3
                for e_ in range(2):
                    for half in range(2):
                        MM(PB[2 * e_ + half][:, :], KPT[:, 128 * e_:128 * (e_ + 1)],
                           XHT_[:, 512 * half:512 * (half + 1)], first, lastk,
                           "t" if (lastk and e_ == 1 and half == 1) else None)
        tkv = b.tick("t")
        b.wait("vector", "t", tkv)
        for e_ in range(2):
            for half in range(2):
                V(lambda e, e_=e_, half=half: e.tensor_copy(
                    KVB(e_)[:, 512 * half:512 * (half + 1)],
                    PB[2 * e_ + half][:, :]), "v")
        tkvb = b.tick("v")
        bankf = {i: ("v", tkvb) for i in range(6)}
        accum_hold = [0, 0]
        for nt in range(NT):
            b.waitp("tensor", bankf[4])
            for e_ in range(2):
                MM(PB[4][0:1, :], ONESC, QP(e_, nt), e_ == 0, e_ == 1,
                   "t" if e_ == 1 else None)
            tz = b.tick("t")
            b.wait("vector", "t", tz)
            V(lambda e: e.tensor_scalar(ROWS[0:1, 0:512], PB[4][0:1, :], 1e-8, None, ALU.add), "v")
            bankf[4] = ("v", b.tick("v"))
            b.drain("vector")
            tvz = V(lambda e: e.reciprocal(ROWS[0:1, 512:1024], ROWS[0:1, 0:512]), "v")
            b.wait("tensor", "v", tvz)
            b.waitp("tensor", bankf[5])
            tbz = MM(PB[5][:, :], ROW1, ROWS[0:1, 512:1024], True, True, "t")
            b.wait("vector", "t", tbz)
            tvb = V(lambda e: e.tensor_copy(BZ, PB[5][:, :]), "v")
            bankf[5] = ("v", tvb)
            accum, readback = mk_residual(nt, master)
            for m in range(C8):
                bank = PB[m % 2]
                b.wait("tensor", "v", tvb)
                b.waitp("tensor", bankf[m % 2])
                for e_ in range(2):
                    MM(bank[:, :], KVB(e_)[:, 128 * m:128 * (m + 1)], QP(e_, nt),
                       e_ == 0, e_ == 1, "t" if e_ == 1 else None)
                tnm = b.tick("t")
                b.wait("vector", "t", tnm)
                b.wait("vector", "dG", accum_hold[m % 2])
                tvx = V(lambda e, m=m, bank=bank: e.tensor_tensor(
                    XNEW(m), bank[:, :], BZ, ALU.mult), "v")
                bankf[m % 2] = ("v", tvx)
                accum_hold[m % 2] = accum(m, tvx)
            readback()

    def l1_ffn(master):
        XHW = lambda i: SF[:, 1024 + 512 * (i % 2): 1024 + 512 * (i % 2) + 512]
        MX = lambda j: SF[:, 2048 + j:2049 + j]
        SG_ = lambda i: SF[:, 2560 + 512 * (i % 2): 2560 + 512 * (i % 2) + 512]
        T4U = lambda i: SF[:, 3584 + 512 * (i % 2): 3584 + 512 * (i % 2) + 512]
        GB = lambda i: SB[:, 512 * (i % 4): 512 * (i % 4) + 512]
        LB = lambda i: SB[:, 2048 + 512 * (i % 4): 2048 + 512 * (i % 4) + 512]
        T4B = lambda i: SB[:, 4096 + 512 * (i % 4): 4096 + 512 * (i % 4) + 512]
        UB = lambda i: SB[:, 6144 + 512 * (i % 4): 6144 + 512 * (i % 4) + 512]
        OPB = lambda i: SB[:, 8192 + 512 * (i % 4): 8192 + 512 * (i % 4) + 512]
        OOP = lambda k: SB[:, 512 * k: 512 * (k + 1)]
        GB4 = lambda i: SB[:, 19456 + 512 * (i % 2): 19456 + 512 * (i % 2) + 512]
        I16A = lambda i: SI[:, 0:512]
        I16B = lambda i: SI[:, 512:1024]
        I16R = lambda i: SI[:, 512 * (i % 2): 512 * (i % 2) + 512]
        SUr = lambda i: SU[:, 0:512]

        wT = lambda K: WB[:, 3072 * K: 3072 * (K + 1)]
        wTd = lambda K: WB[:, 1024 * K: 1024 * (K + 1)]

        def allreduce(srcs, dsts):
            n = len(srcs)
            for j, src in enumerate(srcs):
                b.wait("tensor", "v", b.tick("v"))
                ttr = TR(PB[4][0:1, 0:128], src, ident=IDTF, inc="t")
                b.wait("vector", "t", ttr)
                V(lambda e, j=j: e.reduce_max(
                    ROWS[0:1, 1024 + j:1025 + j], PB[4][0:1, 0:128], AX.X), "v")
                b.drain("vector")
            tv = b.tick("v")
            b.wait("gpsimd", "v", tv)
            tg = b.dma("gpsimd", lambda e, n=n: e.dma_start(
                out=cc_in[0:1, 0:n], in_=ROWS[0:1, 1024:1024 + n]), "dC")
            b.wait("gpsimd", "dC", tg)
            tcc = b.op("gpsimd", lambda e: e.collective_compute(
                "AllReduce", ALU.max, replica_groups=[list(range(8))],
                ins=[cc_in], outs=[cc_out]), "cc")
            b.wait("gpsimd", "cc", tcc)
            tg = b.dma("gpsimd", lambda e, n=n: e.dma_start(
                out=ROWS[0:1, 1024:1024 + n], in_=cc_out[0:1, 0:n]), "dC")
            b.wait("vector", "dC", tg)
            for j, dc in enumerate(dsts):
                V(lambda e, j=j, dc=dc: e.tensor_copy(
                    ROWS[0:1, dc:dc + 1], ROWS[0:1, 1024 + j:1025 + j]), "v")
            b.drain("vector")

        def mkscale(m_rc, s_rc, rs_rc):
            V(lambda e, a=m_rc, o=s_rc: e.tensor_scalar(
                ROWS[0:1, o:o + 1], ROWS[0:1, a:a + 1], 1.0 / 32767.0, 1e-12,
                ALU.mult, ALU.max))
            b.drain("vector")
            V(lambda e, o=s_rc, r=rs_rc: e.reciprocal(
                ROWS[0:1, r:r + 1], ROWS[0:1, o:o + 1]), "v")
            b.drain("vector")

        b.dma("sync", lambda e: e.dma_start(out=master, in_=xmA), "dX")
        t_pre = b.tick("dX")

        V(lambda e: e.memset(MX(0), 0.0), "v")
        guard = [Z, Z]
        for nt in range(NT):
            tln = ln_tiles(nt, 19456, PB[0], PB[1], pre_t=(guard[0], guard[1]))
            guard[0] = guard[1] = ("v", tln)
            b.wait("vector", "t", b.tick("t"))
            for cp in range(0, C8, 2):
                for c in (cp, cp + 1):
                    V(lambda e, c=c, nt=nt: e.tensor_tensor(
                        XHW(c), XB[:, T * c + TT * nt: T * c + TT * nt + TT], BINV, ALU.mult))
                for c in (cp, cp + 1):
                    V(lambda e, c=c: e.tensor_tensor(XHW(c), XHW(c), BMINV, ALU.subtract))
                V(lambda e, c=cp: e.reduce_max(MX(1), XHW(c), AX.X,
                                               apply_absolute_value=True))
                V(lambda e, c=cp + 1: e.reduce_max(MX(8), XHW(c), AX.X,
                                                   apply_absolute_value=True))
                V(lambda e: e.tensor_max(MX(0), MX(0), MX(1)))
                b.drain("vector")
                V(lambda e: e.tensor_max(MX(0), MX(0), MX(8)), "v")
                b.drain("vector")
        if L1CUT == 'A0':
            return
        allreduce([MX(0)], [0])
        if L1CUT == 'A':
            return
        mkscale(0, 2, 3)
        if L1CUT == 'Am':
            return
        bcast_col(2, 0, PB[4])
        bcast_col(3, 1, PB[4])
        if L1CUT == 'Ab':
            return
        def load_wt(src, nchunk, wide, pre_t):
            b.wait("sync", "t", pre_t)
            tks = []
            for K in range(nchunk):
                tk = b.dma("sync", lambda e, K=K, src=src, w=wide: e.dma_start(
                    out=WB[:, w * K: w * (K + 1)], in_=src[:, 128 * K:128 * (K + 1)],
                    transpose=True), "dS")
                tks.append(tk)
            return tks

        def mm_spill_phase(dst_spill, mxacc, mxtmp):
            for nt in range(NT):
                for mi in range(MI):
                    i = nt * MI + mi
                    bank = PB[2 + mi % 2]
                    b.waitp("tensor", bank_ev[mi % 2])
                    for K in range(C8):
                        MM(bank[:, :], wT(K)[:, 128 * mi:128 * (mi + 1)],
                           XB[:, T * K + TT * nt: T * K + TT * nt + TT],
                           K == 0, K == C8 - 1, "t" if K == C8 - 1 else None)
                    tmm = b.tick("t")
                    b.wait("vector", "t", tmm)
                    b.wait("vector", "dS", spill_hold[i % 4])
                    V(lambda e, bank=bank, mxtmp=mxtmp: e.reduce_max(
                        MX(mxtmp), bank[:, :], AX.X, apply_absolute_value=True))
                    V(lambda e, i=i, bank=bank: e.tensor_copy(GB(i), bank[:, :]), "v")
                    tve = V(lambda e, mxacc=mxacc, mxtmp=mxtmp: e.tensor_max(
                        MX(mxacc), MX(mxacc), MX(mxtmp)), "v")
                    bank_ev[mi % 2] = ("v", tve)
                    b.wait("sync", "v", tve)
                    spill_hold[i % 4] = b.dma("sync", lambda e, nt=nt, mi=mi, i=i, d=dst_spill:
                        e.dma_start(out=d[128 * mi:128 * (mi + 1), TT * nt: TT * nt + TT],
                                    in_=GB(i)), "dS")
                    yield i, nt, mi

        twg = load_wt(wgq, C8, 3072, b.tick("t"))
        V(lambda e: e.memset(MX(2), 0.0), "v")
        spill_hold = [0, 0, 0, 0]
        bank_ev = [Z, Z]
        guard2 = [Z, Z]
        for nt in range(NT):
            tln = ln_tiles(nt, 19456, PB[0], PB[1], pre_t=(guard2[0], guard2[1]))
            guard2[0] = guard2[1] = ("v", tln)
            b.wait("vector", "t", t_stats[0])
            for cp in range(0, C8, 2):
                for c in (cp, cp + 1):
                    V(lambda e, c=c, nt=nt: e.tensor_tensor(
                        XHW(c), XB[:, T * c + TT * nt: T * c + TT * nt + TT], BINV, ALU.mult))
                for c in (cp, cp + 1):
                    V(lambda e, c=c: e.tensor_tensor(XHW(c), XHW(c), BMINV, ALU.subtract))
                for c in (cp, cp + 1):
                    V(lambda e, c=c: e.tensor_scalar(I16R(c), XHW(c), COL(1), None, ALU.mult))
                for c in (cp, cp + 1):
                    V(lambda e, c=c, nt=nt: e.tensor_scalar(
                        XB[:, T * c + TT * nt: T * c + TT * nt + TT], I16R(c), COL(0),
                        None, ALU.mult), "v")
            txq_nt = b.tick("v")
            if nt == 0:
                b.wait("tensor", "dS", twg[-1])
            for mi in range(MI):
                i = nt * MI + mi
                bank = PB[2 + mi % 2]
                b.waitp("tensor", bank_ev[mi % 2])
                b.wait("tensor", "v", txq_nt)
                for K in range(C8):
                    MM(bank[:, :], wT(K)[:, 128 * mi:128 * (mi + 1)],
                       XB[:, T * K + TT * nt: T * K + TT * nt + TT],
                       K == 0, K == C8 - 1, "t" if K == C8 - 1 else None)
                tmm = b.tick("t")
                b.wait("vector", "t", tmm)
                b.wait("vector", "dS", spill_hold[i % 4])
                V(lambda e, bank=bank: e.reduce_max(
                    MX(3), bank[:, :], AX.X, apply_absolute_value=True))
                V(lambda e, i=i, bank=bank: e.tensor_copy(GB(i), bank[:, :]), "v")
                tve = V(lambda e: e.tensor_max(MX(2), MX(2), MX(3)), "v")
                bank_ev[mi % 2] = ("v", tve)
                b.wait("sync", "v", tve)
                spill_hold[i % 4] = b.dma("sync", lambda e, nt=nt, mi=mi, i=i:
                    e.dma_start(out=sp_gate[128 * mi:128 * (mi + 1), TT * nt: TT * nt + TT],
                                in_=GB(i)), "dS")
        if L1CUT == 'B':
            return
        allreduce([MX(2)], [4])
        mkscale(4, 5, 6)
        bcast_col(5, 2, PB[4])
        bcast_col(6, 3, PB[4])

        twu = load_wt(wuq, C8, 3072, b.tick("t"))
        b.wait("tensor", "dS", twu[-1])
        V(lambda e: e.memset(MX(4), 0.0))
        V(lambda e: e.memset(MX(6), 0.0), "v")
        p2_hold = [0, 0, 0, 0]
        t4_hold = [0, 0, 0, 0]
        p2_prev = {"sg": 0, "su": 0, "i16": 0}
        for i, nt, mi in mm_spill_phase(sp_up, 4, 5):
            b.wait("sync", "dS", p2_hold[i % 4])
            tld = b.dma("sync", lambda e, nt=nt, mi=mi, i=i: e.dma_start(
                out=LB(i), in_=sp_gate[128 * mi:128 * (mi + 1), TT * nt: TT * nt + TT]), "dS")
            b.wait("vector", "dS", tld)
            if p2_prev["sg"]:
                b.wait("vector", "s", p2_prev["sg"])
            tvi = V(lambda e, i=i: e.tensor_scalar(I16A(i), LB(i), COL(3), None, ALU.mult), "v")
            b.wait("scalar", "v", tvi)
            SC(lambda e, i=i: e.activation(SG_(i), I16A(i), AF.Sigmoid, scale=COL(2)), "s")
            b.drain("scalar")
            if p2_prev["su"]:
                b.wait("scalar", "v", p2_prev["su"])
            tsg = SC(lambda e, i=i: e.activation(SUr(i), SG_(i), AF.Copy, scale=65536.0), "s")
            p2_prev["sg"] = tsg
            b.wait("vector", "s", tsg)
            V(lambda e, i=i: e.tensor_tensor(T4U(i), I16A(i), SUr(i), ALU.mult))
            b.drain("vector")
            V(lambda e, i=i: e.reduce_max(MX(7), T4U(i), AX.X, apply_absolute_value=True))
            b.drain("vector")
            V(lambda e: e.tensor_max(MX(6), MX(6), MX(7)))
            b.wait("vector", "dS", t4_hold[i % 4])
            tvt = V(lambda e, i=i: e.tensor_copy(T4B(i), T4U(i)), "v")
            p2_prev["su"] = tvt
            p2_hold[i % 4] = tld
            b.wait("sync", "v", tvt)
            t4_hold[i % 4] = b.dma("sync", lambda e, nt=nt, mi=mi, i=i: e.dma_start(
                out=sp_t4[128 * mi:128 * (mi + 1), TT * nt: TT * nt + TT],
                in_=T4B(i)), "dS")
        if L1CUT == 'p2':
            return
        allreduce([MX(4), MX(6)], [8, 10])
        mkscale(8, 9, 12)
        V(lambda e: e.tensor_scalar(ROWS[0:1, 13:14], ROWS[0:1, 5:6], 1.0 / 65536.0,
                                    None, ALU.mult))
        b.drain("vector")
        V(lambda e: e.tensor_mul(ROWS[0:1, 14:15], ROWS[0:1, 10:11], ROWS[0:1, 13:14]))
        b.drain("vector")
        V(lambda e: e.tensor_scalar(ROWS[0:1, 15:16], ROWS[0:1, 14:15], 1.0 / 32767.0,
                                    1e-12, ALU.mult, ALU.max))
        b.drain("vector")
        V(lambda e: e.reciprocal(ROWS[0:1, 16:17], ROWS[0:1, 15:16]))
        b.drain("vector")
        V(lambda e: e.tensor_mul(ROWS[0:1, 17:18], ROWS[0:1, 13:14], ROWS[0:1, 16:17]), "v")
        b.drain("vector")
        bcast_col(12, 4, PB[4])
        bcast_col(17, 5, PB[4])
        V(lambda e: e.memset(MX(0), 0.0), "v")
        ld_hold = [0, 0, 0, 0]
        op_hold = [0, 0, 0, 0]
        p3_prev = {"a": 0, "b": 0}
        for i in range(MI * NT):
            mi, nt = i % MI, i // MI
            b.wait("sync", "dS", ld_hold[i % 4])
            tl1 = b.dma("sync", lambda e, nt=nt, mi=mi, i=i: e.dma_start(
                out=T4B(i), in_=sp_t4[128 * mi:128 * (mi + 1), TT * nt: TT * nt + TT]), "dS")
            tl2 = b.dma("sync", lambda e, nt=nt, mi=mi, i=i: e.dma_start(
                out=UB(i), in_=sp_up[128 * mi:128 * (mi + 1), TT * nt: TT * nt + TT]), "dS")
            b.wait("scalar", "dS", tl2)
            if p3_prev["a"]:
                b.wait("scalar", "v", p3_prev["a"])
            SC(lambda e, i=i: e.activation(I16A(i), T4B(i), AF.Copy, scale=COL(5)), "s")
            b.drain("scalar")
            tsc = SC(lambda e, i=i: e.activation(I16B(i), UB(i), AF.Copy, scale=COL(4)), "s")
            b.wait("vector", "s", tsc)
            V(lambda e, i=i: e.tensor_tensor(T4U(i), I16A(i), I16B(i), ALU.mult))
            b.drain("vector")
            V(lambda e, i=i: e.reduce_max(MX(1), T4U(i), AX.X, apply_absolute_value=True))
            b.drain("vector")
            V(lambda e: e.tensor_max(MX(0), MX(0), MX(1)))
            b.wait("vector", "dS", op_hold[i % 4])
            tvo = V(lambda e, i=i: e.tensor_copy(OPB(i), T4U(i)), "v")
            p3_prev["a"] = tvo
            ld_hold[i % 4] = tl2
            b.wait("sync", "v", tvo)
            op_hold[i % 4] = b.dma("sync", lambda e, nt=nt, mi=mi, i=i: e.dma_start(
                out=sp_op[128 * mi:128 * (mi + 1), TT * nt: TT * nt + TT],
                in_=OPB(i)), "dS")
        if L1CUT == 'p3':
            return
        allreduce([MX(0)], [0])
        V(lambda e: e.tensor_mul(ROWS[0:1, 18:19], ROWS[0:1, 15:16], ROWS[0:1, 9:10]))
        b.drain("vector")
        V(lambda e: e.tensor_mul(ROWS[0:1, 19:20], ROWS[0:1, 0:1], ROWS[0:1, 18:19]))
        b.drain("vector")
        V(lambda e: e.tensor_scalar(ROWS[0:1, 20:21], ROWS[0:1, 19:20], 1.0 / 32767.0,
                                    1e-12, ALU.mult, ALU.max))
        b.drain("vector")
        V(lambda e: e.reciprocal(ROWS[0:1, 21:22], ROWS[0:1, 20:21]))
        b.drain("vector")
        V(lambda e: e.tensor_mul(ROWS[0:1, 22:23], ROWS[0:1, 18:19], ROWS[0:1, 21:22]), "v")
        b.drain("vector")
        bcast_col(22, 6, PB[4])
        bcast_col(20, 7, PB[4])
        twd = load_wt(wdq, MI, 1024, b.tick("t"))
        b.wait("tensor", "dS", twd[-1])
        b.wait("sync", "dX", t_pre)
        accum_hold = [0, 0]
        bank_ev2 = [Z, Z]
        p4_prev = 0
        tdm_last = 0
        tall = 0
        for nt in range(NT):
            if tdm_last:
                b.wait("vector", "t", tdm_last)
            for k in range(MI):
                b.wait("sync", "s", p4_prev)
                tl = b.dma("sync", lambda e, k=k, nt=nt: e.dma_start(
                    out=GB4(k), in_=sp_op[128 * k:128 * (k + 1), TT * nt: TT * nt + TT]), "dS")
                b.wait("scalar", "dS", tl)
                if tall:
                    b.wait("scalar", "v", tall)
                tsc = SC(lambda e, k=k: e.activation(I16R(k), GB4(k), AF.Copy,
                                                     scale=COL(6)), "s")
                p4_prev = tsc
                b.wait("vector", "s", tsc)
                tall = V(lambda e, k=k: e.tensor_copy(OOP(k), I16R(k)), "v")
            accum, _rb = mk_residual(nt, master)
            for m in range(C8):
                bank = PB[2 + m % 2]
                b.wait("tensor", "v", tall)
                b.waitp("tensor", bank_ev2[m % 2])
                for K in range(MI):
                    MM(bank[:, :], wTd(K)[:, 128 * m:128 * (m + 1)], OOP(K),
                       K == 0, K == MI - 1, "t" if K == MI - 1 else None)
                tdm = b.tick("t")
                tdm_last = tdm
                b.wait("vector", "t", tdm)
                b.wait("vector", "dG", accum_hold[m % 2])
                tvx = V(lambda e, m=m, bank=bank: e.tensor_scalar(
                    XNEW(m), bank[:, :], COL(7), None, ALU.mult), "v")
                bank_ev2[m % 2] = ("v", tvx)
                accum_hold[m % 2] = accum(m, tvx)

    seq = [lambda: l0_attn(xmA), lambda: l0_ffn(xmA),
           lambda: l1_attn(xmA), lambda: l1_ffn(outT)]
    for i in range(N_BLOCKS):
        seq[i]()
        barrier()
    if N_BLOCKS < 4:
        b.wait("sync", "dG", b.tick("dG"))
        b.dma("sync", lambda e: e.dma_start(out=outT, in_=xmA), "dX")
    b.wait("sync", "dS", b.tick("dS"))
    b.wait("sync", "dG", b.tick("dG"))
    b.wait("sync", "dX", b.tick("dX"))
    b.wait("sync", "dC", b.tick("dC"))

    block = es.enter_context(nc.Block())

    def mk(eng_name):
        def run(eng):
            for kind, a1, a2, a3 in b.q[eng_name]:
                if kind == "op":
                    inst = a1(eng)
                    if a2 is not None:
                        inst.then_inc(sems[a2], a3)
                elif kind == "wait":
                    eng.wait_ge(sems[a1], a2)
                else:
                    eng.drain()
        return run

    block.tensor(mk("tensor"))
    block.vector(mk("vector"))
    block.scalar(mk("scalar"))
    block.gpsimd(mk("gpsimd"))
    block.sync(mk("sync"))

    es.close()
    return nc


def kernel(**inputs):
    xs = np.ascontiguousarray(np.asarray(inputs["L0_attn_hidden_states"], np.float32))
    tr = lambda a: np.ascontiguousarray(np.asarray(a, dtype=np.float32).T)

    cst_f = np.zeros((128, 1024), np.float32)
    cst_f[:, 0:128] = np.eye(128, dtype=np.float32)
    cst_f[:, 128:256] = np.tril(np.ones((128, 128), np.float32)).T
    cst_f[:, 256] = 1.0
    for h in range(8):
        cst_f[:, 320 + 8 * h + h] = 1.0

    cstf_f = np.zeros((128, 384), np.float32)
    cstf_f[:, 128:256] = np.eye(128, dtype=np.float32)
    cstf_f[0, 256:384] = 1.0
    cstf_f[:, 40] = 1e-5

    sel8 = np.zeros((8, 1024), np.float32)
    for h in range(8):
        sel8[h, 128 * h:128 * (h + 1)] = 1.0

    shared = {
        "WqT": tr(inputs["L0_attn_Wq"]), "WkT": tr(inputs["L0_attn_Wk"]),
        "WvT": tr(inputs["L0_attn_Wv"]), "WoT": tr(inputs["L0_attn_Wo"]),
        "fmq1T": tr(inputs["L0_attn_fmq_w1"]), "fmq2T": tr(inputs["L0_attn_fmq_w2"]),
        "fmk1T": tr(inputs["L0_attn_fmk_w1"]), "fmk2T": tr(inputs["L0_attn_fmk_w2"]),
        "fmb": np.ascontiguousarray(np.stack([
            np.asarray(inputs["L0_attn_fmq_b1"], np.float32),
            np.asarray(inputs["L0_attn_fmq_b2"], np.float32),
            np.asarray(inputs["L0_attn_fmk_b1"], np.float32),
            np.asarray(inputs["L0_attn_fmk_b2"], np.float32)], axis=1)),
        "rmsw": np.ascontiguousarray(np.asarray(inputs["L0_attn_rms_w"], np.float32)[:, None]),
        "WrT": tr(inputs["L0_ffn_W_r"]), "WkfT": tr(inputs["L0_ffn_W_k"]),
        "WvfT": tr(inputs["L0_ffn_W_v"]),
        "mus": np.ascontiguousarray(np.concatenate([
            np.asarray(inputs["L0_ffn_mu_r"], np.float32).reshape(8, 128).T,
            np.asarray(inputs["L0_ffn_mu_k"], np.float32).reshape(8, 128).T], axis=1)),
        "qpT": np.ascontiguousarray(np.asarray(inputs["L1_attn_q_proj"], np.float32)),
        "kpT": np.ascontiguousarray(np.asarray(inputs["L1_attn_k_proj"], np.float32)),
        "wg_nat": np.ascontiguousarray(np.asarray(inputs["L1_ffn_w_gate"], np.float32)),
        "wu_nat": np.ascontiguousarray(np.asarray(inputs["L1_ffn_w_up"], np.float32)),
        "wd_nat": np.ascontiguousarray(np.asarray(inputs["L1_ffn_w_down"], np.float32)),
        "cst_f": cst_f, "cstf_f": cstf_f, "sel8T_f": sel8,
    }
    in_maps = []
    for core in range(8):
        m = dict(shared)
        m["xT"] = np.ascontiguousarray(xs[core].T)
        in_maps.append(m)
    nc = build_nc()
    res = run_bass_kernel_spmd(nc, in_maps, core_ids=list(range(8)))
    out = np.empty((8, T, D), np.float32)
    for core in range(8):
        out[core] = np.asarray(res.results[core]["outT"]).T
    return out


# version 05
# speedup vs baseline: 1.0396x; 1.0249x over previous
import numpy as np
import ml_dtypes
import concourse.bass as bass
import concourse.mybir as mybir
from concourse.bass_utils import run_bass_kernel_spmd
from contextlib import ExitStack

F32 = mybir.dt.float32
BF16 = mybir.dt.bfloat16
I16 = mybir.dt.int16
U16 = mybir.dt.uint16
AF = mybir.ActivationFunctionType
ALU = mybir.AluOpType
AX = mybir.AxisListType

T = 4096
D = 1024
C8 = 8
TT = 512
NT = T // TT
DI = 3072
MI = DI // 128
N_BLOCKS = 4
DEBUG = False
L1CUT = 'full'

ENGS = ("tensor", "vector", "scalar", "gpsimd", "sync")
Z = ("v", 0)


class Bld:
    def __init__(self):
        self.q = {e: [] for e in ENGS}
        self.cnt = {}

    def op(self, eng, fn, inc=None, amt=1):
        self.q[eng].append(("op", fn, inc, amt))
        if inc is not None:
            self.cnt[inc] = self.cnt.get(inc, 0) + amt
            return self.cnt[inc]
        return None

    def dma(self, eng, fn, sem):
        return self.op(eng, fn, sem, 16)

    def wait(self, eng, sem, val):
        assert val <= self.cnt.get(sem, 0), f"wait {sem}>={val} emitted={self.cnt.get(sem, 0)}"
        if val > 0:
            self.q[eng].append(("wait", sem, val, None))

    def waitp(self, eng, pair):
        if pair and pair[1] > 0:
            self.wait(eng, pair[0], pair[1])

    def drain(self, eng):
        self.q[eng].append(("drain", None, None, None))

    def tick(self, sem):
        return self.cnt.get(sem, 0)


def build_nc():
    nc = bass.Bass("TRN2", target_bir_lowering=False)

    xT = nc.dram_tensor("xT", [D, T], F32, kind="ExternalInput").ap()
    outT = nc.dram_tensor("outT", [D, T], F32, kind="ExternalOutput").ap()
    xmA = nc.dram_tensor("xmA", [D, T], F32).ap()
    dbg = nc.dram_tensor("dbg", [128, 4096], F32, kind="ExternalOutput").ap() if DEBUG else None

    wz = {}
    def win(name, shape):
        wz[name] = nc.dram_tensor(name, shape, F32, kind="ExternalInput").ap()
    for nm in ("WqT", "WkT", "WvT", "WoT", "WrT", "WkfT", "WvfT"):
        win(nm, [D, D])
    for nm in ("fmq1T", "fmq2T", "fmk1T", "fmk2T"):
        win(nm, [128, 128])
    win("fmb", [128, 4]); win("rmsw", [128, 1]); win("mus", [128, 16])
    win("qpT", [D, 256]); win("kpT", [D, 256])
    win("wg_nat", [DI, D]); win("wu_nat", [DI, D]); win("wd_nat", [D, DI])
    win("cst_f", [128, 1024]); win("cstf_f", [128, 384]); win("sel8T_f", [8, 1024])

    wgq = nc.dram_tensor("wgq", [DI, D], BF16).ap()
    wuq = nc.dram_tensor("wuq", [DI, D], BF16).ap()
    wdq = nc.dram_tensor("wdq", [D, DI], BF16).ap()
    sp_gate = nc.dram_tensor("sp_gate", [DI, T], BF16).ap()
    sp_up = nc.dram_tensor("sp_up", [DI, T], BF16).ap()
    sp_t4 = nc.dram_tensor("sp_t4", [DI, T], BF16).ap()
    sp_op = nc.dram_tensor("sp_op", [DI, T], BF16).ap()
    cc_in = nc.dram_tensor("cc_in", [1, 8], F32).ap()
    cc_out = nc.dram_tensor("cc_out", [1, 8], F32).ap()
    wobf = nc.dram_tensor("wobf", [D, D], BF16).ap()

    es = ExitStack()
    S = lambda name, shape, dt: es.enter_context(nc.sbuf_tensor(name, shape, dt))

    XB = S("XB", [128, C8 * T], BF16)
    WB = S("WB", [128, 26624], BF16)
    SB = S("SB", [128, 21504], BF16)
    SF = S("SF", [128, 7680], F32)
    ROWS = S("ROWS", [1, 1536], F32)
    SI = S("SI", [128, 1024], I16)
    SU = S("SU", [128, 512], U16)
    CST = S("CST", [128, 1024], BF16)
    CSTF = S("CSTF", [128, 384], F32)
    SEL = S("SEL", [8, 1024], BF16)
    SELI = S("SELI", [8, 512], BF16)
    FMT = S("FMT", [128, 512], BF16)

    PB = [es.enter_context(nc.psum_tensor(f"PB{i}", [128, 512], F32)) for i in range(6)]
    PT = [es.enter_context(nc.psum_tensor(f"PT{i}", [128, 1024], BF16)) for i in range(2)]

    semlist = ["t", "v", "s", "g", "dS", "dG", "dC", "dX", "cc", "bar"]
    sems = {n: es.enter_context(nc.semaphore(n)) for n in semlist}
    b = Bld()

    IDT = CST[:, 0:128]
    ONESC = CST[:, 256:257]
    ONES8 = CST[:, 320:384]
    FM = {"q1": FMT[:, 0:128], "q2": FMT[:, 128:256],
          "k1": FMT[:, 256:384], "k2": FMT[:, 384:512]}
    RMSW = CSTF[:, 0:1]
    MUR = lambda c: CSTF[:, 16 + c:17 + c]
    MUK = lambda c: CSTF[:, 24 + c:25 + c]
    FMB = {"q1": CSTF[:, 32:33], "q2": CSTF[:, 33:34],
           "k1": CSTF[:, 34:35], "k2": CSTF[:, 35:36]}
    COL = lambda j: CSTF[:, 64 + j:65 + j]
    EPS5 = CSTF[:, 40:41]
    XLAST = CSTF[:, 96:104]
    IDTF = CSTF[:, 128:256]
    ROW1 = CSTF[0:1, 256:384]

    xb_ready = [0] * NT

    def MM(out_ap, lhsT, rhs, start, stop, inc=None):
        return b.op("tensor", lambda e, o=out_ap, l=lhsT, r=rhs, s1=start, s2=stop:
                    e.matmul(o, l, r, start=s1, stop=s2), inc)

    def TR(out_ap, in_ap, ident=None, inc=None):
        idt = IDT if ident is None else ident
        return b.op("tensor", lambda e, o=out_ap, i=in_ap, d=idt: e.transpose(o, i, d), inc)

    def V(fn, inc=None):
        return b.op("vector", fn, inc)

    def SC(fn, inc=None):
        return b.op("scalar", fn, inc)

    def barrier():
        b.wait("sync", "dS", b.tick("dS"))
        b.wait("sync", "dX", b.tick("dX"))
        b.wait("gpsimd", "dG", b.tick("dG"))
        b.wait("gpsimd", "dC", b.tick("dC"))
        b.wait("gpsimd", "cc", b.tick("cc"))
        for e in ENGS:
            b.op(e, lambda eng: eng.nop(), "bar")
        tgt = b.tick("bar")
        for e in ENGS:
            b.wait(e, "bar", tgt)

    b.dma("gpsimd", lambda e: e.dma_start(
        out=bass.AP(XB, 0, [[C8 * T, 128], [T, 8], [1, T]]),
        in_=bass.AP(xT.tensor, 0, [[T, 128], [T * 128, 8], [1, T]])), "dG")
    b.dma("sync", lambda e: e.dma_start(out=xmA, in_=xT), "dX")
    t_cstload = b.dma("gpsimd", lambda e: e.dma_start(out=CST[:, 0:1024], in_=wz["cst_f"]), "dG")
    b.dma("gpsimd", lambda e: e.dma_start(out=SEL[:, :], in_=wz["sel8T_f"]), "dC")
    tcf = b.dma("sync", lambda e: e.dma_start(out=CSTF[:, 0:384], in_=wz["cstf_f"]), "dS")

    def load_wb(names, width=1024):
        tks = []
        for j, nm in enumerate(names):
            for c in range(C8):
                o = j * C8 * width + c * width
                tk = b.dma("gpsimd", lambda e, nm=nm, c=c, o=o, w=width: e.dma_start(
                    out=WB[:, o:o + w], in_=wz[nm][128 * c:128 * (c + 1), 0:w]), "dG")
            tks.append(tk)
        return tks

    twq = load_wb(["WqT", "WkT", "WvT"])
    t_wobf = b.dma("gpsimd", lambda e: e.dma_start(out=wobf, in_=wz["WoT"]), "dC")
    WQ = lambda K, M: WB[:, 1024 * K + 128 * M: 1024 * K + 128 * (M + 1)]
    WKa = lambda K, M: WB[:, 8192 + 1024 * K + 128 * M: 8192 + 1024 * K + 128 * (M + 1)]
    WVrow = lambda K: WB[:, 16384 + 1024 * K: 16384 + 1024 * (K + 1)]
    WOW = lambda i, K: WB[:, 24576 + 1024 * (i % 2) + 128 * K: 24576 + 1024 * (i % 2) + 128 * (K + 1)]

    for i in range(NT):
        xb_ready[i] = b.tick("dG")
    b.wait("gpsimd", "dG", b.tick("dG"))
    b.wait("gpsimd", "dC", b.tick("dC"))
    t_init = b.op("gpsimd", lambda e: e.nop(), "g")
    b.wait("sync", "dS", tcf)
    b.wait("vector", "g", t_init)
    V(lambda e: e.memset(ROWS[0:1, 0:128], 0.0))
    b.drain("vector")
    for nm, key in [("fmq1T", "q1"), ("fmq2T", "q2"), ("fmk1T", "k1"), ("fmk2T", "k2")]:
        tk = b.dma("sync", lambda e, nm=nm: e.dma_start(out=SF[:, 0:128], in_=wz[nm]), "dS")
        b.wait("vector", "dS", tk)
        tv = V(lambda e, key=key: e.tensor_scalar(FM[key], SF[:, 0:128], 1.0, None, ALU.mult), "v")
        b.wait("sync", "v", tv)
    tk = b.dma("sync", lambda e: e.dma_start(out=SF[:, 0:4], in_=wz["fmb"]), "dS")
    b.wait("vector", "dS", tk)
    V(lambda e: e.tensor_scalar(CSTF[:, 32:33], SF[:, 0:1], float(128 ** -0.5), None, ALU.mult))
    tv = V(lambda e: e.tensor_copy(CSTF[:, 33:36], SF[:, 1:4]), "v")
    b.wait("sync", "v", tv)
    tk = b.dma("sync", lambda e: e.dma_start(out=SF[:, 0:1], in_=wz["rmsw"]), "dS")
    b.wait("vector", "dS", tk)
    tv = V(lambda e: e.tensor_copy(CSTF[:, 0:1], SF[:, 0:1]), "v")
    b.wait("sync", "v", tv)
    tk = b.dma("sync", lambda e: e.dma_start(out=SF[:, 0:16], in_=wz["mus"]), "dS")
    b.wait("vector", "dS", tk)
    tv = V(lambda e: e.tensor_copy(CSTF[:, 16:32], SF[:, 0:16]), "v")
    b.wait("sync", "v", tv)

    def dequant(src, dst, nrow_chunks, width):
        nb = 32
        nslab = width // 1024
        for r in range(nrow_chunks):
            for sl in range(nslab):
                tk = b.dma("sync", lambda e, r=r, sl=sl, src=src: e.dma_start(
                    out=SF[:, 0:1024], in_=src[128 * r:128 * (r + 1), 1024 * sl:1024 * (sl + 1)]), "dS")
                b.wait("scalar", "dS", tk)
                tks = SC(lambda e: e.activation(SF[:, 1024:2048], SF[:, 0:1024], AF.Abs), "s")
                b.wait("vector", "s", tks)
                hw = 16
                while hw >= 2:
                    V(lambda e, hw=hw: e.tensor_tensor(
                        bass.AP(SF, 1024, [[7680, 128], [32, nb], [1, hw]]),
                        bass.AP(SF, 1024, [[7680, 128], [32, nb], [1, hw]]),
                        bass.AP(SF, 1024 + hw, [[7680, 128], [32, nb], [1, hw]]), ALU.max))
                    b.drain("vector")
                    hw //= 2
                V(lambda e: e.tensor_tensor(
                    bass.AP(SF, 2048, [[7680, 128], [1, nb]]),
                    bass.AP(SF, 1024, [[7680, 128], [32, nb]]),
                    bass.AP(SF, 1025, [[7680, 128], [32, nb]]), ALU.max))
                b.drain("vector")
                V(lambda e: e.tensor_scalar(
                    SF[:, 2048:2048 + nb], SF[:, 2048:2048 + nb], 1.0 / 7.0, 1e-12,
                    ALU.mult, ALU.max))
                b.drain("vector")
                V(lambda e: e.reciprocal(SF[:, 2112:2112 + nb], SF[:, 2048:2048 + nb]))
                b.drain("vector")
                V(lambda e: e.tensor_tensor(
                    bass.AP(SI, 0, [[1024, 128], [32, nb], [1, 32]]),
                    bass.AP(SF, 0, [[7680, 128], [32, nb], [1, 32]]),
                    bass.AP(SF, 2112, [[7680, 128], [1, nb], [0, 32]]), ALU.mult), "v")
                b.drain("vector")
                tvw = V(lambda e: e.tensor_tensor(
                    bass.AP(SB, 0, [[21504, 128], [32, nb], [1, 32]]),
                    bass.AP(SI, 0, [[1024, 128], [32, nb], [1, 32]]),
                    bass.AP(SF, 2048, [[7680, 128], [1, nb], [0, 32]]), ALU.mult), "v")
                b.wait("sync", "v", tvw)
                tst = b.dma("sync", lambda e, r=r, sl=sl, dst=dst: e.dma_start(
                    out=dst[128 * r:128 * (r + 1), 1024 * sl:1024 * (sl + 1)], in_=SB[:, 0:1024]), "dS")
                b.wait("sync", "s", tks)
                b.wait("sync", "dS", tst)

    b.wait("sync", "v", b.tick("v"))
    dequant(wz["wg_nat"], wgq, MI, D)
    dequant(wz["wu_nat"], wuq, MI, D)
    dequant(wz["wd_nat"], wdq, C8, DI)
    barrier()

    BINV = SF[:, 0:512]
    BMINV = SF[:, 512:1024]
    t_stats = [0]

    def ln_tiles(nt, sq_cols, ba, bb, pre_t=(), pre_s=()):
        xw = xb_ready[nt]
        b.wait("scalar", "dG", xw)
        b.wait("tensor", "dG", xw)
        for p in pre_s:
            b.waitp("scalar", p)
        for p in pre_t:
            b.waitp("tensor", p)
        for c in range(C8):
            MM(ba[0:1, :], ONESC, XB[:, T * c + TT * nt: T * c + TT * nt + TT],
               c == 0, c == C8 - 1, "t" if c == C8 - 1 else None)
        tmmq = [0] * C8
        for c in range(C8):
            if c >= 2:
                b.wait("scalar", "t", tmmq[c - 2])
            tsq = SC(lambda e, c=c, o=sq_cols, nt=nt: e.activation(
                SB[:, o + 512 * (c % 2): o + 512 * (c % 2) + 512],
                XB[:, T * c + TT * nt: T * c + TT * nt + TT], AF.Square), "s")
            b.wait("tensor", "s", tsq)
            tmmq[c] = MM(bb[0:1, :], ONESC,
                         SB[:, sq_cols + 512 * (c % 2): sq_cols + 512 * (c % 2) + 512],
                         c == 0, c == C8 - 1, "t")
        t_stats[0] = tmmq[-1]
        tk = b.tick("t")
        b.wait("vector", "t", tk)
        V(lambda e, ba=ba: e.tensor_scalar(ROWS[0:1, 0:512], ba[0:1, :], 1.0 / D, None, ALU.mult))
        V(lambda e, bb=bb: e.tensor_scalar(ROWS[0:1, 512:1024], bb[0:1, :], 1.0 / D, None, ALU.mult), "v")
        V(lambda e: e.tensor_mul(ROWS[0:1, 1024:1536], ROWS[0:1, 0:512], ROWS[0:1, 0:512]))
        b.drain("vector")
        tv = V(lambda e: e.tensor_sub(ROWS[0:1, 512:1024], ROWS[0:1, 512:1024], ROWS[0:1, 1024:1536]), "v")
        b.wait("scalar", "v", tv)
        ts2 = SC(lambda e: e.activation(ROWS[0:1, 1024:1536], ROWS[0:1, 512:1024],
                                        AF.Sqrt, bias=EPS5[0:1, :], scale=1.0), "s")
        b.wait("vector", "s", ts2)
        V(lambda e: e.reciprocal(ROWS[0:1, 512:1024], ROWS[0:1, 1024:1536]))
        b.drain("vector")
        tv = V(lambda e: e.tensor_mul(ROWS[0:1, 1024:1536], ROWS[0:1, 0:512], ROWS[0:1, 512:1024]), "v")
        b.wait("tensor", "v", tv)
        MM(ba[:, :], ROW1, ROWS[0:1, 512:1024], True, True)
        tk = MM(bb[:, :], ROW1, ROWS[0:1, 1024:1536], True, True, "t")
        b.wait("vector", "t", tk)
        V(lambda e, ba=ba: e.tensor_copy(BINV, ba[:, :]))
        tv = V(lambda e, bb=bb: e.tensor_copy(BMINV, bb[:, :]), "v")
        b.drain("vector")
        return tv

    def bcast_col(rc, j, bank):
        V(lambda e, rc=rc: e.tensor_tensor(
            ROWS[0:1, 1280:1408], ROW1,
            bass.AP(ROWS, rc, [[1536, 1], [0, 128]]), ALU.mult), "v")
        b.drain("vector")
        b.wait("tensor", "v", b.tick("v"))
        tk = MM(bank[:, 0:128], ROW1, ROWS[0:1, 1280:1408], True, True, "t")
        b.wait("vector", "t", tk)
        tv = V(lambda e, j=j, bank=bank: e.tensor_copy(COL(j), bank[:, 0:1]), "v")
        b.drain("vector")
        return tv

    XNEW = lambda m: SF[:, 5632 + 512 * (m % 2): 5632 + 512 * (m % 2) + 512]

    def mk_residual(nt, master):
        tks = []
        def accum(m, tv):
            b.wait("gpsimd", "v", tv)
            tk = b.dma("gpsimd", lambda e, m=m, nt=nt, ms=master: e.dma_start(
                out=ms[128 * m:128 * (m + 1), TT * nt: TT * nt + TT],
                in_=XNEW(m), accum_op=ALU.add), "dG")
            tks.append(tk)
            return tk
        def readback():
            b.wait("gpsimd", "dG", tks[-1])
            tk = b.dma("gpsimd", lambda e, nt=nt, ms=master: e.dma_start(
                out=bass.AP(XB, TT * nt, [[C8 * T, 128], [T, 8], [1, TT]]),
                in_=bass.AP(ms.tensor, TT * nt, [[T, 128], [T * 128, 8], [1, TT]])), "dG")
            xb_ready[nt] = tk
            return tk
        return accum, readback

    def dump(dst_cols, src_ap):
        b.wait("gpsimd", "v", b.tick("v"))
        b.wait("gpsimd", "s", b.tick("s"))
        tk = b.dma("gpsimd", lambda e, d=dst_cols, s_=src_ap: e.dma_start(
            out=dbg[:, d:d + 512], in_=s_), "dG")
        for eng in ("vector", "scalar", "tensor"):
            b.wait(eng, "dG", tk)

    def l0_attn(master):
        XH = lambda c: SB[:, 512 * c: 512 * (c + 1)]
        QF = lambda h: SB[:, 4096 + 512 * h: 4096 + 512 * (h + 1)]
        KF = lambda h: SB[:, 8192 + 512 * h: 8192 + 512 * (h + 1)]
        OBF = lambda h: SB[:, 12288 + 512 * h: 12288 + 512 * (h + 1)]
        VT = lambda s: SB[:, 16384 + 1024 * (s % 2): 16384 + 1024 * (s % 2) + 1024]
        KFT = lambda g: SB[:, 18432 + 512 * g: 18432 + 512 * (g + 1)]
        ATM = KFT
        QRAW = lambda i: SB[:, 19456 + 512 * (i % 2): 19456 + 512 * (i % 2) + 512]
        SBH = lambda h: SB[:, 20480 + 128 * h: 20480 + 128 * (h + 1)]
        SST = SF[:, 2048:3072]
        A2s = lambda i: SF[:, 3072:3584] if i % 2 == 0 else SF[:, 7168:7680]
        A1s = lambda i: SF[:, 3584:4096] if i % 2 == 0 else SF[:, 6656:7168]
        MSK4 = lambda g: bass.AP(CST, 128, [[1024, 128], [0, 4], [1, 128]])
        ATM3 = lambda g: bass.AP(SB, 18432 + 512 * g, [[21504, 128], [128, 4], [1, 128]])
        AT3 = lambda bank: bass.AP(bank, 0, [[512, 128], [128, 4], [1, 128]])

        b.wait("tensor", "dG", twq[-1])
        V(lambda e: e.memset(SST, 0.0))
        tv0 = V(lambda e: e.memset(SB[:, 20480:21504], 0.0), "v")
        b.wait("tensor", "v", tv0)

        if DEBUG:
            tv_ = V(lambda e: e.tensor_copy(SF[:, 3712:3840], FM["q1"]), "v")
            b.wait("gpsimd", "v", tv_)
            tkw = b.dma("gpsimd", lambda e: e.dma_start(out=dbg[:, 3200:3328], in_=SF[:, 3712:3840]), "dG")
            for eng_ in ("vector", "scalar", "tensor"):
                b.wait(eng_, "dG", tkw)
        bankf = {i: Z for i in range(6)}
        tptf = [Z, Z]
        kft_guard = [0, 0]
        a2_guard = [0, 0]
        wo_prev = [0]
        accum_hold = [0, 0]
        tms_prev = 0
        wo_hold = [0, 0]

        for nt in range(NT):
            tln = ln_tiles(nt, 19456, PB[0], PB[1],
                           pre_t=(bankf[0], bankf[1]),
                           pre_s=((("t", tms_prev)) ,) if tms_prev else ())
            bankf[0] = bankf[1] = ("v", tln)
            b.wait("vector", "t", b.tick("t"))
            for cp in range(0, C8, 2):
                for c in (cp, cp + 1):
                    V(lambda e, c=c, nt=nt: e.tensor_tensor(
                        SF[:, 1024 + 512 * (c % 2): 1024 + 512 * (c % 2) + 512],
                        XB[:, T * c + TT * nt: T * c + TT * nt + TT], BINV, ALU.mult))
                for c in (cp, cp + 1):
                    V(lambda e, c=c: e.tensor_tensor(
                        XH(c), SF[:, 1024 + 512 * (c % 2): 1024 + 512 * (c % 2) + 512],
                        BMINV, ALU.subtract), "v")
            txh = b.tick("v")

            for Wf, fm1, fm2, fb1, fb2, DST in (
                    (WQ, FM["q1"], FM["q2"], FMB["q1"], FMB["q2"], QF),
                    (WKa, FM["k1"], FM["k2"], FMB["k1"], FMB["k2"], KF)):
                for h in range(C8):
                    bank = PB[0 + h % 2]
                    b.wait("tensor", "v", txh)
                    b.waitp("tensor", bankf[0 + h % 2])
                    for K in range(C8):
                        MM(bank[:, :], Wf(K, h), XH(K), K == 0, K == C8 - 1,
                           "t" if K == C8 - 1 else None)
                    tmm = b.tick("t")
                    b.wait("vector", "t", tmm)
                    tqr = V(lambda e, h=h, bank=bank: e.tensor_copy(QRAW(h), bank[:, :]), "v")
                    bankf[0 + h % 2] = ("v", tqr)
                    fa, fb_ = 2 + 2 * (h % 2), 3 + 2 * (h % 2)
                    b.wait("tensor", "v", tqr)
                    b.waitp("tensor", bankf[fa])
                    b.waitp("tensor", bankf[fb_])
                    MM(PB[fa][:, :], fm1, QRAW(h), True, True, "t")
                    t45 = MM(PB[fb_][:, :], fm2, QRAW(h), True, True, "t")
                    b.wait("scalar", "t", t45)
                    b.wait("scalar", "v", a2_guard[h % 2])
                    sc_ = float(128 ** -0.5) if DST is QF else 1.0
                    SC(lambda e, fb1=fb1, sc_=sc_, fa=fa, h=h: e.activation(
                        A1s(h), PB[fa][:, :], AF.Identity, bias=fb1, scale=sc_), "s")
                    ts1 = SC(lambda e, fb2=fb2, fb_=fb_, h=h: e.activation(
                        A2s(h), PB[fb_][:, :], AF.Identity, bias=fb2, scale=1.0), "s")
                    b.wait("vector", "s", ts1)
                    tvq = V(lambda e, h=h, DST=DST: e.tensor_tensor(
                        DST(h), A1s(h), A2s(h), ALU.mult), "v")
                    a2_guard[h % 2] = tvq
                    bankf[fa] = bankf[fb_] = ("v", tvq)
            tqk = b.tick("v")
            if DEBUG and nt == 0:
                dump(2048, QF(0))
            for s_ in range(4):
                b.wait("tensor", "v", tqk)
                b.waitp("tensor", bankf[2])
                b.waitp("tensor", bankf[3])
                for K in range(C8):
                    MM(PB[2][:, :], XH(K)[:, 128 * s_:128 * (s_ + 1)], WVrow(K)[:, 0:512],
                       K == 0, K == C8 - 1, "t" if K == C8 - 1 else None)
                for K in range(C8):
                    MM(PB[3][:, :], XH(K)[:, 128 * s_:128 * (s_ + 1)], WVrow(K)[:, 512:1024],
                       K == 0, K == C8 - 1, "t" if K == C8 - 1 else None)
                tvm = b.tick("t")
                b.wait("vector", "t", tvm)
                V(lambda e, s_=s_: e.tensor_copy(VT(s_)[:, 0:512], PB[2][:, :]))
                tvt = V(lambda e, s_=s_: e.tensor_copy(VT(s_)[:, 512:1024], PB[3][:, :]), "v")
                bankf[2] = bankf[3] = ("v", tvt)
                for g in range(2):
                    b.waitp("tensor", tptf[g])
                    for j in range(4):
                        h = 4 * g + j
                        TR(PT[g][:, 128 * j:128 * (j + 1)], KF(h)[:, 128 * s_:128 * (s_ + 1)],
                           inc="t" if j == 3 else None)
                    ttr = b.tick("t")
                    b.wait("vector", "t", ttr)
                    if kft_guard[g]:
                        b.wait("vector", "t", kft_guard[g])
                    tke = V(lambda e, g=g: e.tensor_copy(KFT(g), PT[g][:, 0:512]), "v")
                    tptf[g] = ("v", tke)
                    atb, kvb_, ob = PB[4 + g], PB[0 + g], PB[2 + g]
                    b.waitp("tensor", bankf[4 + g])
                    for j in range(4):
                        h = 4 * g + j
                        MM(atb[:, 128 * j:128 * (j + 1)], KF(h)[:, 128 * s_:128 * (s_ + 1)],
                           QF(h)[:, 128 * s_:128 * (s_ + 1)], True, True,
                           "t" if j == 3 else None)
                    tat = b.tick("t")
                    b.wait("tensor", "v", max(tke, tvt))
                    b.waitp("tensor", bankf[0 + g])
                    for j in range(4):
                        h = 4 * g + j
                        MM(kvb_[:, 128 * j:128 * (j + 1)], KFT(g)[:, 128 * j:128 * (j + 1)],
                           VT(s_)[:, 128 * h:128 * (h + 1)], True, True,
                           "t" if j == 3 else None)
                    tkv = b.tick("t")
                    b.wait("vector", "t", max(tat, tkv))
                    tam = V(lambda e, g=g, atb=atb: e.tensor_tensor(
                        ATM3(g), AT3(atb), MSK4(g), ALU.mult), "v")
                    bankf[4 + g] = ("v", tam)
                    b.wait("tensor", "v", tam)
                    b.waitp("tensor", bankf[2 + g])
                    for j in range(4):
                        h = 4 * g + j
                        MM(ob[:, 128 * j:128 * (j + 1)], SBH(h),
                           QF(h)[:, 128 * s_:128 * (s_ + 1)], True, False)
                        MM(ob[:, 128 * j:128 * (j + 1)], VT(s_)[:, 128 * h:128 * (h + 1)],
                           ATM(g)[:, 128 * j:128 * (j + 1)], False, True,
                           "t" if j == 3 else None)
                    tob = b.tick("t")
                    kft_guard[g] = tob
                    b.wait("vector", "t", tkv)
                    V(lambda e, g=g, kvb_=kvb_: e.tensor_tensor(
                        SST[:, 512 * g:512 * (g + 1)], SST[:, 512 * g:512 * (g + 1)],
                        kvb_[:, :], ALU.add), "v")
                    bankf[0 + g] = ("v", b.tick("v"))
                    b.drain("vector")
                    b.wait("vector", "t", tob)
                    V(lambda e, g=g: e.tensor_copy(
                        SB[:, 20480 + 512 * g: 20480 + 512 * (g + 1)],
                        SST[:, 512 * g:512 * (g + 1)]), "v")
                    b.wait("scalar", "t", tob)
                    tse = SC(lambda e, s_=s_, g=g, ob=ob: e.activation(
                        bass.AP(SB, 12288 + 512 * (4 * g) + 128 * s_,
                                [[21504, 128], [512, 4], [1, 128]]),
                        ob[:, :], AF.Copy), "s")
                    bankf[2 + g] = ("s", tse)
            tchunk_s = b.tick("s")

            b.drain("scalar")
            tsql = [0] * C8
            for h in range(C8):
                if h >= 2:
                    b.wait("scalar", "t", tsql[h - 2])
                tss = SC(lambda e, h=h: e.activation(QRAW(h), OBF(h), AF.Square), "s")
                b.drain("scalar")
                b.wait("tensor", "s", tss)
                if h == 0:
                    b.waitp("tensor", bankf[0])
                tsql[h] = MM(PB[0][0:8, :], ONES8[:, 8 * h:8 * h + 8], QRAW(h),
                             h == 0, h == C8 - 1, "t")
            tms = tsql[-1]
            tms_prev = tms
            b.wait("scalar", "t", tms)
            ts3 = SC(lambda e: e.activation(SF[0:8, 4608:5120], PB[0][0:8, :],
                                            AF.Sqrt, bias=EPS5[0:8, :], scale=1.0 / 128.0), "s")
            bankf[0] = ("s", ts3)
            b.wait("vector", "s", ts3)
            V(lambda e: e.reciprocal(SF[0:8, 5120:5632], SF[0:8, 4608:5120]))
            b.drain("vector")
            tv4 = V(lambda e: e.tensor_copy(SELI[:, :], SF[0:8, 5120:5632]), "v")
            b.wait("tensor", "v", tv4)
            b.wait("tensor", "s", tchunk_s)
            ton = 0
            for h in range(C8):
                bank = PB[1 + h % 2]
                b.waitp("tensor", bankf[1 + h % 2])
                tbi = MM(bank[:, :], SEL[:, 128 * h:128 * (h + 1)], SELI[:, :],
                         True, True, "t")
                b.wait("vector", "t", tbi)
                ton = V(lambda e, h=h, bank=bank: e.scalar_tensor_tensor(
                    QF(h), OBF(h), RMSW, bank[:, :], ALU.mult, ALU.mult), "v")
                bankf[1 + h % 2] = ("v", ton)
            accum, readback = mk_residual(nt, master)

            b.wait("tensor", "v", ton)
            for m in range(C8):
                if wo_hold[m % 2]:
                    b.wait("sync", "t", wo_hold[m % 2])
                else:
                    b.wait("sync", "dS", b.tick("dS"))
                wo_tk = b.dma("sync", lambda e, m=m: e.dma_start(
                    out=bass.AP(WB, 24576 + 1024 * (m % 2), [[26624, 128], [128, 8], [1, 128]]),
                    in_=bass.AP(wobf.tensor, 128 * m, [[D, 128], [D * 128, 8], [1, 128]])), "dS")
                wo_prev[0] = wo_tk
                bank = PB[3 + m % 2]
                b.waitp("tensor", bankf[3 + m % 2])
                b.wait("tensor", "dS", wo_tk)
                for K in range(C8):
                    MM(bank[:, :], WOW(m, K), QF(K), K == 0, K == C8 - 1,
                       "t" if K == C8 - 1 else None)
                two = b.tick("t")
                wo_hold[m % 2] = two
                b.wait("vector", "t", two)
                b.wait("vector", "dG", accum_hold[m % 2])
                tvx = V(lambda e, m=m, bank=bank: e.tensor_copy(XNEW(m), bank[:, :]), "v")
                bankf[3 + m % 2] = ("v", tvx)
                accum_hold[m % 2] = accum(m, tvx)
            readback()

    def l0_ffn(master):
        XR = lambda c: SB[:, 512 * c: 512 * (c + 1)]
        XK = lambda c: SB[:, 4096 + 512 * c: 4096 + 512 * (c + 1)]
        KACT = lambda c: SB[:, 8192 + 512 * c: 8192 + 512 * (c + 1)]
        RSIG = lambda m: SF[:, 1024 + 512 * m: 1024 + 512 * (m + 1)]
        XHT = lambda i: SF[:, 6656 + 512 * (i % 2): 6656 + 512 * (i % 2) + 512]
        DT_ = SF[:, 5120:5632]

        tw = load_wb(["WrT", "WkfT", "WvfT"])
        WR = lambda K, M: WB[:, 1024 * K + 128 * M: 1024 * K + 128 * (M + 1)]
        WKF = lambda K, M: WB[:, 8192 + 1024 * K + 128 * M: 8192 + 1024 * K + 128 * (M + 1)]
        WVF = lambda K, M: WB[:, 16384 + 1024 * K + 128 * M: 16384 + 1024 * K + 128 * (M + 1)]
        b.wait("tensor", "dG", tw[-1])
        V(lambda e: e.memset(CSTF[:, 96:104], 0.0), "v")

        bankf = {i: Z for i in range(6)}
        accum_hold = [0, 0]
        tka_prev = 0
        for nt in range(NT):
            tln = ln_tiles(nt, 19456, PB[0], PB[1], pre_t=(bankf[0], bankf[1]))
            bankf[0] = bankf[1] = ("v", tln)
            if tka_prev:
                b.wait("vector", "s", tka_prev)
            b.wait("vector", "t", b.tick("t"))
            for c in range(C8):
                V(lambda e, c=c, nt=nt: e.tensor_tensor(
                    XHT(c), XB[:, T * c + TT * nt: T * c + TT * nt + TT], BINV, ALU.mult))
                b.drain("vector")
                V(lambda e, c=c: e.tensor_tensor(XHT(c), XHT(c), BMINV, ALU.subtract))
                b.drain("vector")
                V(lambda e, c=c: e.tensor_tensor(
                    DT_[:, 1:512], XHT(c)[:, 1:512], XHT(c)[:, 0:511], ALU.subtract))
                V(lambda e, c=c: e.tensor_tensor(
                    DT_[:, 0:1], XHT(c)[:, 0:1], XLAST[:, c:c + 1], ALU.subtract))
                b.drain("vector")
                V(lambda e, c=c: e.scalar_tensor_tensor(
                    XR(c)[:, 1:512], DT_[:, 1:512], MUR(c), XHT(c)[:, 0:511],
                    ALU.mult, ALU.add))
                V(lambda e, c=c: e.scalar_tensor_tensor(
                    XR(c)[:, 0:1], DT_[:, 0:1], MUR(c), XLAST[:, c:c + 1],
                    ALU.mult, ALU.add))
                V(lambda e, c=c: e.scalar_tensor_tensor(
                    XK(c)[:, 1:512], DT_[:, 1:512], MUK(c), XHT(c)[:, 0:511],
                    ALU.mult, ALU.add))
                V(lambda e, c=c: e.scalar_tensor_tensor(
                    XK(c)[:, 0:1], DT_[:, 0:1], MUK(c), XLAST[:, c:c + 1],
                    ALU.mult, ALU.add), "v")
                b.drain("vector")
                V(lambda e, c=c: e.tensor_copy(XLAST[:, c:c + 1], XHT(c)[:, 511:512]), "v")
                b.drain("vector")
            txr = b.tick("v")
            for m in range(C8):
                bank = PB[m % 3]
                b.wait("tensor", "v", txr)
                b.waitp("tensor", bankf[m % 3])
                for K in range(C8):
                    MM(bank[:, :], WR(K, m), XR(K), K == 0, K == C8 - 1,
                       "t" if K == C8 - 1 else None)
                tr_ = b.tick("t")
                b.wait("scalar", "t", tr_)
                trs = SC(lambda e, m=m, bank=bank: e.activation(
                    RSIG(m), bank[:, :], AF.Sigmoid), "s")
                bankf[m % 3] = ("s", trs)
            for m in range(C8):
                bank = PB[3 + m % 3]
                b.wait("tensor", "v", txr)
                b.waitp("tensor", bankf[3 + m % 3])
                for K in range(C8):
                    MM(bank[:, :], WKF(K, m), XK(K), K == 0, K == C8 - 1,
                       "t" if K == C8 - 1 else None)
                tk_ = b.tick("t")
                b.wait("scalar", "t", tk_)
                SC(lambda e, m=m, bank=bank: e.activation(XHT(m), bank[:, :], AF.Relu), "s")
                tka = SC(lambda e, m=m: e.activation(KACT(m), XHT(m), AF.Square), "s")
                bankf[3 + m % 3] = ("s", tka)
            tka_prev = b.tick("s")
            accum, readback = mk_residual(nt, master)
            for m in range(C8):
                bank = PB[m % 3]
                b.wait("tensor", "s", tka_prev)
                b.waitp("tensor", bankf[m % 3])
                for K in range(C8):
                    MM(bank[:, :], WVF(K, m), KACT(K), K == 0, K == C8 - 1,
                       "t" if K == C8 - 1 else None)
                tvm = b.tick("t")
                b.wait("vector", "t", tvm)
                b.wait("vector", "s", tka_prev)
                b.wait("vector", "dG", accum_hold[m % 2])
                tvx = V(lambda e, m=m, bank=bank: e.tensor_tensor(
                    XNEW(m), bank[:, :], RSIG(m), ALU.mult), "v")
                bankf[m % 3] = ("v", tvx)
                accum_hold[m % 2] = accum(m, tvx)
            readback()

    def l1_attn(master):
        XH = lambda c: SB[:, 512 * c: 512 * (c + 1)]
        QP = lambda e_, nt: SB[:, 4096 + 4096 * e_ + 512 * nt: 4096 + 4096 * e_ + 512 * nt + 512]
        KP = lambda e_: SB[:, 12288 + 512 * e_: 12288 + 512 * (e_ + 1)]
        XHT_ = SB[:, 13312:14336]
        KPT = SB[:, 14336:14592]
        KVB = lambda e_: SB[:, 14592 + 1024 * e_: 14592 + 1024 * (e_ + 1)]
        BZ = SF[:, 1024:1536]

        tw = load_wb(["qpT", "kpT"], width=256)
        QPJ = lambda K, E: WB[:, 256 * K + 128 * E: 256 * K + 128 * (E + 1)]
        KPJ = lambda K, E: WB[:, 2048 + 256 * K + 128 * E: 2048 + 256 * K + 128 * (E + 1)]
        b.wait("tensor", "dG", tw[-1])

        bank45 = [Z, Z]
        tptf = [Z, Z]
        for nt in range(NT):
            tln = ln_tiles(nt, 19456, PB[4], PB[5], pre_t=(bank45[0], bank45[1]))
            bank45[0] = bank45[1] = ("v", tln)
            b.wait("vector", "t", b.tick("t"))
            for cp in range(0, C8, 2):
                for c in (cp, cp + 1):
                    V(lambda e, c=c, nt=nt: e.tensor_tensor(
                        SF[:, 1536 + 512 * (c % 2): 1536 + 512 * (c % 2) + 512],
                        XB[:, T * c + TT * nt: T * c + TT * nt + TT], BINV, ALU.mult))
                for c in (cp, cp + 1):
                    V(lambda e, c=c: e.tensor_tensor(
                        XH(c), SF[:, 1536 + 512 * (c % 2): 1536 + 512 * (c % 2) + 512],
                        BMINV, ALU.subtract), "v")
            txh = b.tick("v")
            for e_ in range(2):
                b.wait("tensor", "v", txh)
                b.waitp("tensor", bank45[0])
                for K in range(C8):
                    MM(PB[4][:, :], QPJ(K, e_), XH(K), K == 0, K == C8 - 1,
                       "t" if K == C8 - 1 else None)
                tmm = b.tick("t")
                b.wait("scalar", "t", tmm)
                te = SC(lambda e, e_=e_, nt=nt: e.activation(
                    QP(e_, nt), PB[4][:, :], AF.Exp), "s")
                bank45[0] = ("s", te)
                b.waitp("tensor", bank45[1])
                for K in range(C8):
                    MM(PB[5][:, :], KPJ(K, e_), XH(K), K == 0, K == C8 - 1,
                       "t" if K == C8 - 1 else None)
                tmm = b.tick("t")
                b.wait("scalar", "t", tmm)
                te = SC(lambda e, e_=e_: e.activation(KP(e_), PB[5][:, :], AF.Exp), "s")
                bank45[1] = ("s", te)
            tqk = b.tick("s")
            for s_ in range(4):
                b.wait("tensor", "s", tqk)
                for half in range(2):
                    b.waitp("tensor", tptf[half])
                    for j in range(4):
                        c = 4 * half + j
                        TR(PT[half][:, 128 * j:128 * (j + 1)],
                           XH(c)[:, 128 * s_:128 * (s_ + 1)],
                           inc="t" if j == 3 else None)
                    ttr = b.tick("t")
                    b.wait("vector", "t", ttr)
                    tev = V(lambda e, half=half: e.tensor_copy(
                        XHT_[:, 512 * half:512 * (half + 1)], PT[half][:, 0:512]), "v")
                    tptf[half] = ("v", tev)
                b.waitp("tensor", tptf[0])
                for e_ in range(2):
                    TR(PT[0][:, 512 + 128 * e_: 512 + 128 * (e_ + 1)],
                       KP(e_)[:, 128 * s_:128 * (s_ + 1)],
                       inc="t" if e_ == 1 else None)
                ttr = b.tick("t")
                b.wait("vector", "t", ttr)
                tev = V(lambda e: e.tensor_copy(KPT[:, 0:256], PT[0][:, 512:768]), "v")
                tptf[0] = ("v", tev)
                b.wait("tensor", "v", tev)
                first = nt == 0 and s_ == 0
                lastk = nt == NT - 1 and s_ == 3
                for e_ in range(2):
                    for half in range(2):
                        MM(PB[2 * e_ + half][:, :], KPT[:, 128 * e_:128 * (e_ + 1)],
                           XHT_[:, 512 * half:512 * (half + 1)], first, lastk,
                           "t" if (lastk and e_ == 1 and half == 1) else None)
        tkv = b.tick("t")
        b.wait("vector", "t", tkv)
        for e_ in range(2):
            for half in range(2):
                V(lambda e, e_=e_, half=half: e.tensor_copy(
                    KVB(e_)[:, 512 * half:512 * (half + 1)],
                    PB[2 * e_ + half][:, :]), "v")
        tkvb = b.tick("v")
        bankf = {i: ("v", tkvb) for i in range(6)}
        accum_hold = [0, 0]
        for nt in range(NT):
            b.waitp("tensor", bankf[4])
            for e_ in range(2):
                MM(PB[4][0:1, :], ONESC, QP(e_, nt), e_ == 0, e_ == 1,
                   "t" if e_ == 1 else None)
            tz = b.tick("t")
            b.wait("vector", "t", tz)
            V(lambda e: e.tensor_scalar(ROWS[0:1, 0:512], PB[4][0:1, :], 1e-8, None, ALU.add), "v")
            bankf[4] = ("v", b.tick("v"))
            b.drain("vector")
            tvz = V(lambda e: e.reciprocal(ROWS[0:1, 512:1024], ROWS[0:1, 0:512]), "v")
            b.wait("tensor", "v", tvz)
            b.waitp("tensor", bankf[5])
            tbz = MM(PB[5][:, :], ROW1, ROWS[0:1, 512:1024], True, True, "t")
            b.wait("vector", "t", tbz)
            tvb = V(lambda e: e.tensor_copy(BZ, PB[5][:, :]), "v")
            bankf[5] = ("v", tvb)
            accum, readback = mk_residual(nt, master)
            for m in range(C8):
                bank = PB[m % 2]
                b.wait("tensor", "v", tvb)
                b.waitp("tensor", bankf[m % 2])
                for e_ in range(2):
                    MM(bank[:, :], KVB(e_)[:, 128 * m:128 * (m + 1)], QP(e_, nt),
                       e_ == 0, e_ == 1, "t" if e_ == 1 else None)
                tnm = b.tick("t")
                b.wait("vector", "t", tnm)
                b.wait("vector", "dG", accum_hold[m % 2])
                tvx = V(lambda e, m=m, bank=bank: e.tensor_tensor(
                    XNEW(m), bank[:, :], BZ, ALU.mult), "v")
                bankf[m % 2] = ("v", tvx)
                accum_hold[m % 2] = accum(m, tvx)
            readback()

    def l1_ffn(master):
        XHW = lambda i: SF[:, 1024 + 512 * (i % 2): 1024 + 512 * (i % 2) + 512]
        MX = lambda j: SF[:, 2048 + j:2049 + j]
        SG_ = lambda i: SF[:, 2560 + 512 * (i % 2): 2560 + 512 * (i % 2) + 512]
        T4U = lambda i: SF[:, 3584 + 512 * (i % 2): 3584 + 512 * (i % 2) + 512]
        GB = lambda i: SB[:, 512 * (i % 4): 512 * (i % 4) + 512]
        LB = lambda i: SB[:, 2048 + 512 * (i % 4): 2048 + 512 * (i % 4) + 512]
        T4B = lambda i: SB[:, 4096 + 512 * (i % 4): 4096 + 512 * (i % 4) + 512]
        UB = lambda i: SB[:, 6144 + 512 * (i % 4): 6144 + 512 * (i % 4) + 512]
        OPB = lambda i: SB[:, 8192 + 512 * (i % 4): 8192 + 512 * (i % 4) + 512]
        OOP = lambda k: SB[:, 512 * k: 512 * (k + 1)]
        GB4 = lambda i: SB[:, 19456 + 512 * (i % 2): 19456 + 512 * (i % 2) + 512]
        I16A = lambda i: SI[:, 0:512]
        I16B = lambda i: SI[:, 512:1024]
        I16R = lambda i: SI[:, 512 * (i % 2): 512 * (i % 2) + 512]
        SUr = lambda i: SU[:, 0:512]

        wT = lambda K: WB[:, 3072 * K: 3072 * (K + 1)]
        wTd = lambda K: WB[:, 1024 * K: 1024 * (K + 1)]

        def allreduce(srcs, dsts):
            n = len(srcs)
            for j, src in enumerate(srcs):
                b.wait("tensor", "v", b.tick("v"))
                ttr = TR(PB[4][0:1, 0:128], src, ident=IDTF, inc="t")
                b.wait("vector", "t", ttr)
                V(lambda e, j=j: e.reduce_max(
                    ROWS[0:1, 1024 + j:1025 + j], PB[4][0:1, 0:128], AX.X), "v")
                b.drain("vector")
            tv = b.tick("v")
            b.wait("gpsimd", "v", tv)
            tg = b.dma("gpsimd", lambda e, n=n: e.dma_start(
                out=cc_in[0:1, 0:n], in_=ROWS[0:1, 1024:1024 + n]), "dC")
            b.wait("gpsimd", "dC", tg)
            tcc = b.op("gpsimd", lambda e: e.collective_compute(
                "AllReduce", ALU.max, replica_groups=[list(range(8))],
                ins=[cc_in], outs=[cc_out]), "cc")
            b.wait("gpsimd", "cc", tcc)
            tg = b.dma("gpsimd", lambda e, n=n: e.dma_start(
                out=ROWS[0:1, 1024:1024 + n], in_=cc_out[0:1, 0:n]), "dC")
            b.wait("vector", "dC", tg)
            for j, dc in enumerate(dsts):
                V(lambda e, j=j, dc=dc: e.tensor_copy(
                    ROWS[0:1, dc:dc + 1], ROWS[0:1, 1024 + j:1025 + j]), "v")
            b.drain("vector")

        def mkscale(m_rc, s_rc, rs_rc):
            V(lambda e, a=m_rc, o=s_rc: e.tensor_scalar(
                ROWS[0:1, o:o + 1], ROWS[0:1, a:a + 1], 1.0 / 32767.0, 1e-12,
                ALU.mult, ALU.max))
            b.drain("vector")
            V(lambda e, o=s_rc, r=rs_rc: e.reciprocal(
                ROWS[0:1, r:r + 1], ROWS[0:1, o:o + 1]), "v")
            b.drain("vector")

        b.dma("sync", lambda e: e.dma_start(out=master, in_=xmA), "dX")
        t_pre = b.tick("dX")

        V(lambda e: e.memset(MX(0), 0.0), "v")
        guard = [Z, Z]
        for nt in range(NT):
            tln = ln_tiles(nt, 19456, PB[0], PB[1], pre_t=(guard[0], guard[1]))
            guard[0] = guard[1] = ("v", tln)
            b.wait("vector", "t", b.tick("t"))
            for cp in range(0, C8, 2):
                for c in (cp, cp + 1):
                    V(lambda e, c=c, nt=nt: e.tensor_tensor(
                        XHW(c), XB[:, T * c + TT * nt: T * c + TT * nt + TT], BINV, ALU.mult))
                for c in (cp, cp + 1):
                    V(lambda e, c=c: e.tensor_tensor(XHW(c), XHW(c), BMINV, ALU.subtract))
                V(lambda e, c=cp: e.reduce_max(MX(1), XHW(c), AX.X,
                                               apply_absolute_value=True))
                V(lambda e, c=cp + 1: e.reduce_max(MX(8), XHW(c), AX.X,
                                                   apply_absolute_value=True))
                V(lambda e: e.tensor_max(MX(0), MX(0), MX(1)))
                b.drain("vector")
                V(lambda e: e.tensor_max(MX(0), MX(0), MX(8)), "v")
                b.drain("vector")
        if L1CUT == 'A0':
            return
        allreduce([MX(0)], [0])
        if L1CUT == 'A':
            return
        mkscale(0, 2, 3)
        if L1CUT == 'Am':
            return
        bcast_col(2, 0, PB[4])
        bcast_col(3, 1, PB[4])
        if L1CUT == 'Ab':
            return
        def load_wt(src, nchunk, wide, pre_t):
            b.wait("sync", "t", pre_t)
            tks = []
            for K in range(nchunk):
                tk = b.dma("sync", lambda e, K=K, src=src, w=wide: e.dma_start(
                    out=WB[:, w * K: w * (K + 1)], in_=src[:, 128 * K:128 * (K + 1)],
                    transpose=True), "dS")
                tks.append(tk)
            return tks

        def mm_spill_phase(dst_spill, mxacc, mxtmp):
            for nt in range(NT):
                for mi in range(MI):
                    i = nt * MI + mi
                    bank = PB[2 + mi % 2]
                    b.waitp("tensor", bank_ev[mi % 2])
                    for K in range(C8):
                        MM(bank[:, :], wT(K)[:, 128 * mi:128 * (mi + 1)],
                           XB[:, T * K + TT * nt: T * K + TT * nt + TT],
                           K == 0, K == C8 - 1, "t" if K == C8 - 1 else None)
                    tmm = b.tick("t")
                    b.wait("vector", "t", tmm)
                    b.wait("vector", "dS", spill_hold[i % 4])
                    V(lambda e, bank=bank, mxtmp=mxtmp: e.reduce_max(
                        MX(mxtmp), bank[:, :], AX.X, apply_absolute_value=True))
                    V(lambda e, i=i, bank=bank: e.tensor_copy(GB(i), bank[:, :]), "v")
                    tve = V(lambda e, mxacc=mxacc, mxtmp=mxtmp: e.tensor_max(
                        MX(mxacc), MX(mxacc), MX(mxtmp)), "v")
                    bank_ev[mi % 2] = ("v", tve)
                    b.wait("sync", "v", tve)
                    spill_hold[i % 4] = b.dma("sync", lambda e, nt=nt, mi=mi, i=i, d=dst_spill:
                        e.dma_start(out=d[128 * mi:128 * (mi + 1), TT * nt: TT * nt + TT],
                                    in_=GB(i)), "dS")
                    yield i, nt, mi

        twg = load_wt(wgq, C8, 3072, b.tick("t"))
        V(lambda e: e.memset(MX(2), 0.0), "v")
        spill_hold = [0, 0, 0, 0]
        bank_ev = [Z, Z]
        guard2 = [Z, Z]
        for nt in range(NT):
            tln = ln_tiles(nt, 19456, PB[0], PB[1], pre_t=(guard2[0], guard2[1]))
            guard2[0] = guard2[1] = ("v", tln)
            b.wait("vector", "t", t_stats[0])
            for cp in range(0, C8, 2):
                for c in (cp, cp + 1):
                    V(lambda e, c=c, nt=nt: e.tensor_tensor(
                        XHW(c), XB[:, T * c + TT * nt: T * c + TT * nt + TT], BINV, ALU.mult))
                for c in (cp, cp + 1):
                    V(lambda e, c=c: e.tensor_tensor(XHW(c), XHW(c), BMINV, ALU.subtract))
                for c in (cp, cp + 1):
                    V(lambda e, c=c: e.tensor_scalar(I16R(c), XHW(c), COL(1), None, ALU.mult))
                for c in (cp, cp + 1):
                    V(lambda e, c=c, nt=nt: e.tensor_scalar(
                        XB[:, T * c + TT * nt: T * c + TT * nt + TT], I16R(c), COL(0),
                        None, ALU.mult), "v")
            txq_nt = b.tick("v")
            if nt == 0:
                b.wait("tensor", "dS", twg[-1])
            for mi in range(MI):
                i = nt * MI + mi
                bank = PB[2 + mi % 2]
                b.waitp("tensor", bank_ev[mi % 2])
                b.wait("tensor", "v", txq_nt)
                for K in range(C8):
                    MM(bank[:, :], wT(K)[:, 128 * mi:128 * (mi + 1)],
                       XB[:, T * K + TT * nt: T * K + TT * nt + TT],
                       K == 0, K == C8 - 1, "t" if K == C8 - 1 else None)
                tmm = b.tick("t")
                b.wait("vector", "t", tmm)
                b.wait("vector", "dS", spill_hold[i % 4])
                V(lambda e, bank=bank: e.reduce_max(
                    MX(3), bank[:, :], AX.X, apply_absolute_value=True))
                V(lambda e, i=i, bank=bank: e.tensor_copy(GB(i), bank[:, :]), "v")
                tve = V(lambda e: e.tensor_max(MX(2), MX(2), MX(3)), "v")
                bank_ev[mi % 2] = ("v", tve)
                b.wait("sync", "v", tve)
                spill_hold[i % 4] = b.dma("sync", lambda e, nt=nt, mi=mi, i=i:
                    e.dma_start(out=sp_gate[128 * mi:128 * (mi + 1), TT * nt: TT * nt + TT],
                                in_=GB(i)), "dS")
        if L1CUT == 'B':
            return
        allreduce([MX(2)], [4])
        mkscale(4, 5, 6)
        bcast_col(5, 2, PB[4])
        bcast_col(6, 3, PB[4])

        twu = load_wt(wuq, C8, 3072, b.tick("t"))
        b.wait("tensor", "dS", twu[-1])
        V(lambda e: e.memset(MX(4), 0.0))
        V(lambda e: e.memset(MX(6), 0.0), "v")
        p2_hold = [0, 0, 0, 0]
        t4_hold = [0, 0, 0, 0]
        p2_prev = {"sg": 0, "su": 0, "i16": 0}
        for i, nt, mi in mm_spill_phase(sp_up, 4, 5):
            b.wait("sync", "dS", p2_hold[i % 4])
            tld = b.dma("sync", lambda e, nt=nt, mi=mi, i=i: e.dma_start(
                out=LB(i), in_=sp_gate[128 * mi:128 * (mi + 1), TT * nt: TT * nt + TT]), "dS")
            b.wait("vector", "dS", tld)
            if p2_prev["sg"]:
                b.wait("vector", "s", p2_prev["sg"])
            tvi = V(lambda e, i=i: e.tensor_scalar(I16A(i), LB(i), COL(3), None, ALU.mult), "v")
            b.wait("scalar", "v", tvi)
            SC(lambda e, i=i: e.activation(SG_(i), I16A(i), AF.Sigmoid, scale=COL(2)), "s")
            b.drain("scalar")
            if p2_prev["su"]:
                b.wait("scalar", "v", p2_prev["su"])
            tsg = SC(lambda e, i=i: e.activation(SUr(i), SG_(i), AF.Copy, scale=65536.0), "s")
            p2_prev["sg"] = tsg
            b.wait("vector", "s", tsg)
            V(lambda e, i=i: e.tensor_tensor(T4U(i), I16A(i), SUr(i), ALU.mult))
            b.drain("vector")
            V(lambda e, i=i: e.reduce_max(MX(7), T4U(i), AX.X, apply_absolute_value=True))
            b.drain("vector")
            V(lambda e: e.tensor_max(MX(6), MX(6), MX(7)))
            b.wait("vector", "dS", t4_hold[i % 4])
            tvt = V(lambda e, i=i: e.tensor_copy(T4B(i), T4U(i)), "v")
            p2_prev["su"] = tvt
            p2_hold[i % 4] = tld
            b.wait("sync", "v", tvt)
            t4_hold[i % 4] = b.dma("sync", lambda e, nt=nt, mi=mi, i=i: e.dma_start(
                out=sp_t4[128 * mi:128 * (mi + 1), TT * nt: TT * nt + TT],
                in_=T4B(i)), "dS")
        if L1CUT == 'p2':
            return
        allreduce([MX(4), MX(6)], [8, 10])
        mkscale(8, 9, 12)
        V(lambda e: e.tensor_scalar(ROWS[0:1, 13:14], ROWS[0:1, 5:6], 1.0 / 65536.0,
                                    None, ALU.mult))
        b.drain("vector")
        V(lambda e: e.tensor_mul(ROWS[0:1, 14:15], ROWS[0:1, 10:11], ROWS[0:1, 13:14]))
        b.drain("vector")
        V(lambda e: e.tensor_scalar(ROWS[0:1, 15:16], ROWS[0:1, 14:15], 1.0 / 32767.0,
                                    1e-12, ALU.mult, ALU.max))
        b.drain("vector")
        V(lambda e: e.reciprocal(ROWS[0:1, 16:17], ROWS[0:1, 15:16]))
        b.drain("vector")
        V(lambda e: e.tensor_mul(ROWS[0:1, 17:18], ROWS[0:1, 13:14], ROWS[0:1, 16:17]), "v")
        b.drain("vector")
        bcast_col(12, 4, PB[4])
        bcast_col(17, 5, PB[4])
        V(lambda e: e.memset(MX(0), 0.0), "v")
        ld_hold = [0, 0, 0, 0]
        op_hold = [0, 0, 0, 0]
        p3_prev = {"a": 0, "b": 0}
        for i in range(MI * NT):
            mi, nt = i % MI, i // MI
            b.wait("sync", "dS", ld_hold[i % 4])
            tl1 = b.dma("sync", lambda e, nt=nt, mi=mi, i=i: e.dma_start(
                out=T4B(i), in_=sp_t4[128 * mi:128 * (mi + 1), TT * nt: TT * nt + TT]), "dS")
            tl2 = b.dma("sync", lambda e, nt=nt, mi=mi, i=i: e.dma_start(
                out=UB(i), in_=sp_up[128 * mi:128 * (mi + 1), TT * nt: TT * nt + TT]), "dS")
            b.wait("scalar", "dS", tl2)
            if p3_prev["a"]:
                b.wait("scalar", "v", p3_prev["a"])
            SC(lambda e, i=i: e.activation(I16A(i), T4B(i), AF.Copy, scale=COL(5)), "s")
            b.drain("scalar")
            tsc = SC(lambda e, i=i: e.activation(I16B(i), UB(i), AF.Copy, scale=COL(4)), "s")
            b.wait("vector", "s", tsc)
            V(lambda e, i=i: e.tensor_tensor(T4U(i), I16A(i), I16B(i), ALU.mult))
            b.drain("vector")
            V(lambda e, i=i: e.reduce_max(MX(1), T4U(i), AX.X, apply_absolute_value=True))
            b.drain("vector")
            V(lambda e: e.tensor_max(MX(0), MX(0), MX(1)))
            b.wait("vector", "dS", op_hold[i % 4])
            tvo = V(lambda e, i=i: e.tensor_copy(OPB(i), T4U(i)), "v")
            p3_prev["a"] = tvo
            ld_hold[i % 4] = tl2
            b.wait("sync", "v", tvo)
            op_hold[i % 4] = b.dma("sync", lambda e, nt=nt, mi=mi, i=i: e.dma_start(
                out=sp_op[128 * mi:128 * (mi + 1), TT * nt: TT * nt + TT],
                in_=OPB(i)), "dS")
        if L1CUT == 'p3':
            return
        allreduce([MX(0)], [0])
        V(lambda e: e.tensor_mul(ROWS[0:1, 18:19], ROWS[0:1, 15:16], ROWS[0:1, 9:10]))
        b.drain("vector")
        V(lambda e: e.tensor_mul(ROWS[0:1, 19:20], ROWS[0:1, 0:1], ROWS[0:1, 18:19]))
        b.drain("vector")
        V(lambda e: e.tensor_scalar(ROWS[0:1, 20:21], ROWS[0:1, 19:20], 1.0 / 32767.0,
                                    1e-12, ALU.mult, ALU.max))
        b.drain("vector")
        V(lambda e: e.reciprocal(ROWS[0:1, 21:22], ROWS[0:1, 20:21]))
        b.drain("vector")
        V(lambda e: e.tensor_mul(ROWS[0:1, 22:23], ROWS[0:1, 18:19], ROWS[0:1, 21:22]), "v")
        b.drain("vector")
        bcast_col(22, 6, PB[4])
        bcast_col(20, 7, PB[4])
        twd = load_wt(wdq, MI, 1024, b.tick("t"))
        b.wait("tensor", "dS", twd[-1])
        b.wait("sync", "dX", t_pre)
        accum_hold = [0, 0]
        bank_ev2 = [Z, Z]
        p4_prev = 0
        tdm_last = 0
        tall = 0
        for nt in range(NT):
            if tdm_last:
                b.wait("vector", "t", tdm_last)
            for k in range(MI):
                b.wait("sync", "s", p4_prev)
                tl = b.dma("sync", lambda e, k=k, nt=nt: e.dma_start(
                    out=GB4(k), in_=sp_op[128 * k:128 * (k + 1), TT * nt: TT * nt + TT]), "dS")
                b.wait("scalar", "dS", tl)
                if tall:
                    b.wait("scalar", "v", tall)
                tsc = SC(lambda e, k=k: e.activation(I16R(k), GB4(k), AF.Copy,
                                                     scale=COL(6)), "s")
                p4_prev = tsc
                b.wait("vector", "s", tsc)
                tall = V(lambda e, k=k: e.tensor_copy(OOP(k), I16R(k)), "v")
            accum, _rb = mk_residual(nt, master)
            for m in range(C8):
                bank = PB[2 + m % 2]
                b.wait("tensor", "v", tall)
                b.waitp("tensor", bank_ev2[m % 2])
                for K in range(MI):
                    MM(bank[:, :], wTd(K)[:, 128 * m:128 * (m + 1)], OOP(K),
                       K == 0, K == MI - 1, "t" if K == MI - 1 else None)
                tdm = b.tick("t")
                tdm_last = tdm
                b.wait("vector", "t", tdm)
                b.wait("vector", "dG", accum_hold[m % 2])
                tvx = V(lambda e, m=m, bank=bank: e.tensor_scalar(
                    XNEW(m), bank[:, :], COL(7), None, ALU.mult), "v")
                bank_ev2[m % 2] = ("v", tvx)
                accum_hold[m % 2] = accum(m, tvx)

    seq = [lambda: l0_attn(xmA), lambda: l0_ffn(xmA),
           lambda: l1_attn(xmA), lambda: l1_ffn(outT)]
    for i in range(N_BLOCKS):
        seq[i]()
        barrier()
    if N_BLOCKS < 4:
        b.wait("sync", "dG", b.tick("dG"))
        b.dma("sync", lambda e: e.dma_start(out=outT, in_=xmA), "dX")
    b.wait("sync", "dS", b.tick("dS"))
    b.wait("sync", "dG", b.tick("dG"))
    b.wait("sync", "dX", b.tick("dX"))
    b.wait("sync", "dC", b.tick("dC"))

    block = es.enter_context(nc.Block())

    def mk(eng_name):
        def run(eng):
            for kind, a1, a2, a3 in b.q[eng_name]:
                if kind == "op":
                    inst = a1(eng)
                    if a2 is not None:
                        inst.then_inc(sems[a2], a3)
                elif kind == "wait":
                    eng.wait_ge(sems[a1], a2)
                else:
                    eng.drain()
        return run

    block.tensor(mk("tensor"))
    block.vector(mk("vector"))
    block.scalar(mk("scalar"))
    block.gpsimd(mk("gpsimd"))
    block.sync(mk("sync"))

    es.close()
    return nc


def kernel(**inputs):
    xs = np.ascontiguousarray(np.asarray(inputs["L0_attn_hidden_states"], np.float32))
    tr = lambda a: np.ascontiguousarray(np.asarray(a, dtype=np.float32).T)

    cst_f = np.zeros((128, 1024), np.float32)
    cst_f[:, 0:128] = np.eye(128, dtype=np.float32)
    cst_f[:, 128:256] = np.tril(np.ones((128, 128), np.float32)).T
    cst_f[:, 256] = 1.0
    for h in range(8):
        cst_f[:, 320 + 8 * h + h] = 1.0

    cstf_f = np.zeros((128, 384), np.float32)
    cstf_f[:, 128:256] = np.eye(128, dtype=np.float32)
    cstf_f[0, 256:384] = 1.0
    cstf_f[:, 40] = 1e-5

    sel8 = np.zeros((8, 1024), np.float32)
    for h in range(8):
        sel8[h, 128 * h:128 * (h + 1)] = 1.0

    shared = {
        "WqT": tr(inputs["L0_attn_Wq"]), "WkT": tr(inputs["L0_attn_Wk"]),
        "WvT": tr(inputs["L0_attn_Wv"]), "WoT": tr(inputs["L0_attn_Wo"]),
        "fmq1T": tr(inputs["L0_attn_fmq_w1"]), "fmq2T": tr(inputs["L0_attn_fmq_w2"]),
        "fmk1T": tr(inputs["L0_attn_fmk_w1"]), "fmk2T": tr(inputs["L0_attn_fmk_w2"]),
        "fmb": np.ascontiguousarray(np.stack([
            np.asarray(inputs["L0_attn_fmq_b1"], np.float32),
            np.asarray(inputs["L0_attn_fmq_b2"], np.float32),
            np.asarray(inputs["L0_attn_fmk_b1"], np.float32),
            np.asarray(inputs["L0_attn_fmk_b2"], np.float32)], axis=1)),
        "rmsw": np.ascontiguousarray(np.asarray(inputs["L0_attn_rms_w"], np.float32)[:, None]),
        "WrT": tr(inputs["L0_ffn_W_r"]), "WkfT": tr(inputs["L0_ffn_W_k"]),
        "WvfT": tr(inputs["L0_ffn_W_v"]),
        "mus": np.ascontiguousarray(np.concatenate([
            np.asarray(inputs["L0_ffn_mu_r"], np.float32).reshape(8, 128).T,
            np.asarray(inputs["L0_ffn_mu_k"], np.float32).reshape(8, 128).T], axis=1)),
        "qpT": np.ascontiguousarray(np.asarray(inputs["L1_attn_q_proj"], np.float32)),
        "kpT": np.ascontiguousarray(np.asarray(inputs["L1_attn_k_proj"], np.float32)),
        "wg_nat": np.ascontiguousarray(np.asarray(inputs["L1_ffn_w_gate"], np.float32)),
        "wu_nat": np.ascontiguousarray(np.asarray(inputs["L1_ffn_w_up"], np.float32)),
        "wd_nat": np.ascontiguousarray(np.asarray(inputs["L1_ffn_w_down"], np.float32)),
        "cst_f": cst_f, "cstf_f": cstf_f, "sel8T_f": sel8,
    }
    in_maps = []
    for core in range(8):
        m = dict(shared)
        m["xT"] = np.ascontiguousarray(xs[core].T)
        in_maps.append(m)
    nc = build_nc()
    res = run_bass_kernel_spmd(nc, in_maps, core_ids=list(range(8)))
    out = np.empty((8, T, D), np.float32)
    for core in range(8):
        out[core] = np.asarray(res.results[core]["outT"]).T
    return out


# version 06 (final)
# speedup vs baseline: 1.0419x; 1.0022x over previous
import numpy as np
import ml_dtypes
import concourse.bass as bass
import concourse.mybir as mybir
from concourse.bass_utils import run_bass_kernel_spmd
from contextlib import ExitStack

F32 = mybir.dt.float32
BF16 = mybir.dt.bfloat16
I16 = mybir.dt.int16
U16 = mybir.dt.uint16
AF = mybir.ActivationFunctionType
ALU = mybir.AluOpType
AX = mybir.AxisListType

T = 4096
D = 1024
C8 = 8
TT = 512
NT = T // TT
DI = 3072
MI = DI // 128
N_BLOCKS = 4
DEBUG = False
L1CUT = 'full'

ENGS = ("tensor", "vector", "scalar", "gpsimd", "sync")
Z = ("v", 0)


class Bld:
    def __init__(self):
        self.q = {e: [] for e in ENGS}
        self.cnt = {}

    def op(self, eng, fn, inc=None, amt=1):
        self.q[eng].append(("op", fn, inc, amt))
        if inc is not None:
            self.cnt[inc] = self.cnt.get(inc, 0) + amt
            return self.cnt[inc]
        return None

    def dma(self, eng, fn, sem):
        return self.op(eng, fn, sem, 16)

    def wait(self, eng, sem, val):
        assert val <= self.cnt.get(sem, 0), f"wait {sem}>={val} emitted={self.cnt.get(sem, 0)}"
        if val > 0:
            self.q[eng].append(("wait", sem, val, None))

    def waitp(self, eng, pair):
        if pair and pair[1] > 0:
            self.wait(eng, pair[0], pair[1])

    def drain(self, eng):
        self.q[eng].append(("drain", None, None, None))

    def tick(self, sem):
        return self.cnt.get(sem, 0)


def build_nc():
    nc = bass.Bass("TRN2", target_bir_lowering=False)

    xT = nc.dram_tensor("xT", [D, T], F32, kind="ExternalInput").ap()
    outT = nc.dram_tensor("outT", [D, T], F32, kind="ExternalOutput").ap()
    xmA = nc.dram_tensor("xmA", [D, T], F32).ap()
    dbg = nc.dram_tensor("dbg", [128, 4096], F32, kind="ExternalOutput").ap() if DEBUG else None

    wz = {}
    def win(name, shape):
        wz[name] = nc.dram_tensor(name, shape, F32, kind="ExternalInput").ap()
    for nm in ("WqT", "WkT", "WvT", "WoT", "WrT", "WkfT", "WvfT"):
        win(nm, [D, D])
    for nm in ("fmq1T", "fmq2T", "fmk1T", "fmk2T"):
        win(nm, [128, 128])
    win("fmb", [128, 4]); win("rmsw", [128, 1]); win("mus", [128, 16])
    win("qpT", [D, 256]); win("kpT", [D, 256])
    win("wg_nat", [DI, D]); win("wu_nat", [DI, D]); win("wd_nat", [D, DI])
    win("cst_f", [128, 1024]); win("cstf_f", [128, 384]); win("sel8T_f", [8, 1024])

    wgq = nc.dram_tensor("wgq", [DI, D], BF16).ap()
    wuq = nc.dram_tensor("wuq", [DI, D], BF16).ap()
    wdq = nc.dram_tensor("wdq", [D, DI], BF16).ap()
    sp_gate = nc.dram_tensor("sp_gate", [DI, T], BF16).ap()
    sp_up = nc.dram_tensor("sp_up", [DI, T], BF16).ap()
    sp_t4 = nc.dram_tensor("sp_t4", [DI, T], BF16).ap()
    sp_op = nc.dram_tensor("sp_op", [DI, T], BF16).ap()
    cc_in = nc.dram_tensor("cc_in", [1, 8], F32).ap()
    cc_out = nc.dram_tensor("cc_out", [1, 8], F32).ap()
    wobf = nc.dram_tensor("wobf", [D, D], BF16).ap()

    es = ExitStack()
    S = lambda name, shape, dt: es.enter_context(nc.sbuf_tensor(name, shape, dt))

    XB = S("XB", [128, C8 * T], BF16)
    WB = S("WB", [128, 26624], BF16)
    SB = S("SB", [128, 21504], BF16)
    SF = S("SF", [128, 7680], F32)
    ROWS = S("ROWS", [1, 1536], F32)
    SI = S("SI", [128, 1024], I16)
    SU = S("SU", [128, 512], U16)
    CST = S("CST", [128, 1024], BF16)
    CSTF = S("CSTF", [128, 384], F32)
    SEL = S("SEL", [8, 1024], BF16)
    SELI = S("SELI", [8, 512], BF16)
    FMT = S("FMT", [128, 512], BF16)

    PB = [es.enter_context(nc.psum_tensor(f"PB{i}", [128, 512], F32)) for i in range(6)]
    PT = [es.enter_context(nc.psum_tensor(f"PT{i}", [128, 1024], BF16)) for i in range(2)]

    semlist = ["t", "v", "s", "g", "dS", "dG", "dC", "dX", "cc", "bar"]
    sems = {n: es.enter_context(nc.semaphore(n)) for n in semlist}
    b = Bld()

    IDT = CST[:, 0:128]
    ONESC = CST[:, 256:257]
    ONES8 = CST[:, 320:384]
    FM = {"q1": FMT[:, 0:128], "q2": FMT[:, 128:256],
          "k1": FMT[:, 256:384], "k2": FMT[:, 384:512]}
    RMSW = CSTF[:, 0:1]
    MUR = lambda c: CSTF[:, 16 + c:17 + c]
    MUK = lambda c: CSTF[:, 24 + c:25 + c]
    FMB = {"q1": CSTF[:, 32:33], "q2": CSTF[:, 33:34],
           "k1": CSTF[:, 34:35], "k2": CSTF[:, 35:36]}
    COL = lambda j: CSTF[:, 64 + j:65 + j]
    EPS5 = CSTF[:, 40:41]
    XLAST = CSTF[:, 96:104]
    IDTF = CSTF[:, 128:256]
    ROW1 = CSTF[0:1, 256:384]

    xb_ready = [0] * NT

    def MM(out_ap, lhsT, rhs, start, stop, inc=None):
        return b.op("tensor", lambda e, o=out_ap, l=lhsT, r=rhs, s1=start, s2=stop:
                    e.matmul(o, l, r, start=s1, stop=s2), inc)

    def TR(out_ap, in_ap, ident=None, inc=None):
        idt = IDT if ident is None else ident
        return b.op("tensor", lambda e, o=out_ap, i=in_ap, d=idt: e.transpose(o, i, d), inc)

    def V(fn, inc=None):
        return b.op("vector", fn, inc)

    def SC(fn, inc=None):
        return b.op("scalar", fn, inc)

    def barrier():
        b.wait("sync", "dS", b.tick("dS"))
        b.wait("sync", "dX", b.tick("dX"))
        b.wait("gpsimd", "dG", b.tick("dG"))
        b.wait("gpsimd", "dC", b.tick("dC"))
        b.wait("gpsimd", "cc", b.tick("cc"))
        for e in ENGS:
            b.op(e, lambda eng: eng.nop(), "bar")
        tgt = b.tick("bar")
        for e in ENGS:
            b.wait(e, "bar", tgt)

    b.dma("gpsimd", lambda e: e.dma_start(
        out=bass.AP(XB, 0, [[C8 * T, 128], [T, 8], [1, T]]),
        in_=bass.AP(xT.tensor, 0, [[T, 128], [T * 128, 8], [1, T]])), "dG")
    b.dma("sync", lambda e: e.dma_start(out=xmA, in_=xT), "dX")
    t_cstload = b.dma("gpsimd", lambda e: e.dma_start(out=CST[:, 0:1024], in_=wz["cst_f"]), "dG")
    b.dma("gpsimd", lambda e: e.dma_start(out=SEL[:, :], in_=wz["sel8T_f"]), "dC")
    tcf = b.dma("sync", lambda e: e.dma_start(out=CSTF[:, 0:384], in_=wz["cstf_f"]), "dS")

    def load_wb(names, width=1024):
        tks = []
        for j, nm in enumerate(names):
            for c in range(C8):
                o = j * C8 * width + c * width
                tk = b.dma("gpsimd", lambda e, nm=nm, c=c, o=o, w=width: e.dma_start(
                    out=WB[:, o:o + w], in_=wz[nm][128 * c:128 * (c + 1), 0:w]), "dG")
            tks.append(tk)
        return tks

    twq = load_wb(["WqT", "WkT", "WvT"])
    t_wobf = b.dma("gpsimd", lambda e: e.dma_start(out=wobf, in_=wz["WoT"]), "dC")
    WQ = lambda K, M: WB[:, 1024 * K + 128 * M: 1024 * K + 128 * (M + 1)]
    WKa = lambda K, M: WB[:, 8192 + 1024 * K + 128 * M: 8192 + 1024 * K + 128 * (M + 1)]
    WVrow = lambda K: WB[:, 16384 + 1024 * K: 16384 + 1024 * (K + 1)]
    WOW = lambda i, K: WB[:, 24576 + 1024 * (i % 2) + 128 * K: 24576 + 1024 * (i % 2) + 128 * (K + 1)]

    for i in range(NT):
        xb_ready[i] = b.tick("dG")
    b.wait("gpsimd", "dG", b.tick("dG"))
    b.wait("gpsimd", "dC", b.tick("dC"))
    t_init = b.op("gpsimd", lambda e: e.nop(), "g")
    b.wait("sync", "dS", tcf)
    b.wait("vector", "g", t_init)
    V(lambda e: e.memset(ROWS[0:1, 0:128], 0.0))
    b.drain("vector")
    for nm, key in [("fmq1T", "q1"), ("fmq2T", "q2"), ("fmk1T", "k1"), ("fmk2T", "k2")]:
        tk = b.dma("sync", lambda e, nm=nm: e.dma_start(out=SF[:, 0:128], in_=wz[nm]), "dS")
        b.wait("vector", "dS", tk)
        tv = V(lambda e, key=key: e.tensor_scalar(FM[key], SF[:, 0:128], 1.0, None, ALU.mult), "v")
        b.wait("sync", "v", tv)
    tk = b.dma("sync", lambda e: e.dma_start(out=SF[:, 0:4], in_=wz["fmb"]), "dS")
    b.wait("vector", "dS", tk)
    V(lambda e: e.tensor_scalar(CSTF[:, 32:33], SF[:, 0:1], float(128 ** -0.5), None, ALU.mult))
    tv = V(lambda e: e.tensor_copy(CSTF[:, 33:36], SF[:, 1:4]), "v")
    b.wait("sync", "v", tv)
    tk = b.dma("sync", lambda e: e.dma_start(out=SF[:, 0:1], in_=wz["rmsw"]), "dS")
    b.wait("vector", "dS", tk)
    tv = V(lambda e: e.tensor_copy(CSTF[:, 0:1], SF[:, 0:1]), "v")
    b.wait("sync", "v", tv)
    tk = b.dma("sync", lambda e: e.dma_start(out=SF[:, 0:16], in_=wz["mus"]), "dS")
    b.wait("vector", "dS", tk)
    tv = V(lambda e: e.tensor_copy(CSTF[:, 16:32], SF[:, 0:16]), "v")
    b.wait("sync", "v", tv)

    def dequant(src, dst, nrow_chunks, width):
        nb = 32
        nslab = width // 1024
        for r in range(nrow_chunks):
            for sl in range(nslab):
                tk = b.dma("sync", lambda e, r=r, sl=sl, src=src: e.dma_start(
                    out=SF[:, 0:1024], in_=src[128 * r:128 * (r + 1), 1024 * sl:1024 * (sl + 1)]), "dS")
                b.wait("scalar", "dS", tk)
                tks = SC(lambda e: e.activation(SF[:, 1024:2048], SF[:, 0:1024], AF.Abs), "s")
                b.wait("vector", "s", tks)
                hw = 16
                while hw >= 2:
                    V(lambda e, hw=hw: e.tensor_tensor(
                        bass.AP(SF, 1024, [[7680, 128], [32, nb], [1, hw]]),
                        bass.AP(SF, 1024, [[7680, 128], [32, nb], [1, hw]]),
                        bass.AP(SF, 1024 + hw, [[7680, 128], [32, nb], [1, hw]]), ALU.max))
                    b.drain("vector")
                    hw //= 2
                V(lambda e: e.tensor_tensor(
                    bass.AP(SF, 2048, [[7680, 128], [1, nb]]),
                    bass.AP(SF, 1024, [[7680, 128], [32, nb]]),
                    bass.AP(SF, 1025, [[7680, 128], [32, nb]]), ALU.max))
                b.drain("vector")
                V(lambda e: e.tensor_scalar(
                    SF[:, 2048:2048 + nb], SF[:, 2048:2048 + nb], 1.0 / 7.0, 1e-12,
                    ALU.mult, ALU.max))
                b.drain("vector")
                V(lambda e: e.reciprocal(SF[:, 2112:2112 + nb], SF[:, 2048:2048 + nb]))
                b.drain("vector")
                V(lambda e: e.tensor_tensor(
                    bass.AP(SI, 0, [[1024, 128], [32, nb], [1, 32]]),
                    bass.AP(SF, 0, [[7680, 128], [32, nb], [1, 32]]),
                    bass.AP(SF, 2112, [[7680, 128], [1, nb], [0, 32]]), ALU.mult), "v")
                b.drain("vector")
                tvw = V(lambda e: e.tensor_tensor(
                    bass.AP(SB, 0, [[21504, 128], [32, nb], [1, 32]]),
                    bass.AP(SI, 0, [[1024, 128], [32, nb], [1, 32]]),
                    bass.AP(SF, 2048, [[7680, 128], [1, nb], [0, 32]]), ALU.mult), "v")
                b.wait("sync", "v", tvw)
                tst = b.dma("sync", lambda e, r=r, sl=sl, dst=dst: e.dma_start(
                    out=dst[128 * r:128 * (r + 1), 1024 * sl:1024 * (sl + 1)], in_=SB[:, 0:1024]), "dS")
                b.wait("sync", "s", tks)
                b.wait("sync", "dS", tst)

    b.wait("sync", "v", b.tick("v"))
    dequant(wz["wg_nat"], wgq, MI, D)
    dequant(wz["wu_nat"], wuq, MI, D)
    dequant(wz["wd_nat"], wdq, C8, DI)
    barrier()

    BINV = SF[:, 0:512]
    BMINV = SF[:, 512:1024]
    t_stats = [0]

    def ln_tiles(nt, sq_cols, ba, bb, pre_t=(), pre_s=()):
        xw = xb_ready[nt]
        b.wait("scalar", "dG", xw)
        b.wait("tensor", "dG", xw)
        for p in pre_s:
            b.waitp("scalar", p)
        for p in pre_t:
            b.waitp("tensor", p)
        for c in range(C8):
            MM(ba[0:1, :], ONESC, XB[:, T * c + TT * nt: T * c + TT * nt + TT],
               c == 0, c == C8 - 1, "t" if c == C8 - 1 else None)
        tmmq = [0] * C8
        for c in range(C8):
            if c >= 2:
                b.wait("scalar", "t", tmmq[c - 2])
            tsq = SC(lambda e, c=c, o=sq_cols, nt=nt: e.activation(
                SB[:, o + 512 * (c % 2): o + 512 * (c % 2) + 512],
                XB[:, T * c + TT * nt: T * c + TT * nt + TT], AF.Square), "s")
            b.wait("tensor", "s", tsq)
            tmmq[c] = MM(bb[0:1, :], ONESC,
                         SB[:, sq_cols + 512 * (c % 2): sq_cols + 512 * (c % 2) + 512],
                         c == 0, c == C8 - 1, "t")
        t_stats[0] = tmmq[-1]
        tk = b.tick("t")
        b.wait("vector", "t", tk)
        V(lambda e, ba=ba: e.tensor_scalar(ROWS[0:1, 0:512], ba[0:1, :], 1.0 / D, None, ALU.mult))
        V(lambda e, bb=bb: e.tensor_scalar(ROWS[0:1, 512:1024], bb[0:1, :], 1.0 / D, None, ALU.mult), "v")
        V(lambda e: e.tensor_mul(ROWS[0:1, 1024:1536], ROWS[0:1, 0:512], ROWS[0:1, 0:512]))
        b.drain("vector")
        tv = V(lambda e: e.tensor_sub(ROWS[0:1, 512:1024], ROWS[0:1, 512:1024], ROWS[0:1, 1024:1536]), "v")
        b.wait("scalar", "v", tv)
        ts2 = SC(lambda e: e.activation(ROWS[0:1, 1024:1536], ROWS[0:1, 512:1024],
                                        AF.Sqrt, bias=EPS5[0:1, :], scale=1.0), "s")
        b.wait("vector", "s", ts2)
        V(lambda e: e.reciprocal(ROWS[0:1, 512:1024], ROWS[0:1, 1024:1536]))
        b.drain("vector")
        tv = V(lambda e: e.tensor_mul(ROWS[0:1, 1024:1536], ROWS[0:1, 0:512], ROWS[0:1, 512:1024]), "v")
        b.wait("tensor", "v", tv)
        MM(ba[:, :], ROW1, ROWS[0:1, 512:1024], True, True)
        tk = MM(bb[:, :], ROW1, ROWS[0:1, 1024:1536], True, True, "t")
        b.wait("vector", "t", tk)
        V(lambda e, ba=ba: e.tensor_copy(BINV, ba[:, :]))
        tv = V(lambda e, bb=bb: e.tensor_copy(BMINV, bb[:, :]), "v")
        b.drain("vector")
        return tv

    def bcast_col(rc, j, bank):
        V(lambda e, rc=rc: e.tensor_tensor(
            ROWS[0:1, 1280:1408], ROW1,
            bass.AP(ROWS, rc, [[1536, 1], [0, 128]]), ALU.mult), "v")
        b.drain("vector")
        b.wait("tensor", "v", b.tick("v"))
        tk = MM(bank[:, 0:128], ROW1, ROWS[0:1, 1280:1408], True, True, "t")
        b.wait("vector", "t", tk)
        tv = V(lambda e, j=j, bank=bank: e.tensor_copy(COL(j), bank[:, 0:1]), "v")
        b.drain("vector")
        return tv

    XNEW = lambda m: SF[:, 5632 + 512 * (m % 2): 5632 + 512 * (m % 2) + 512]

    def mk_residual(nt, master):
        tks = []
        def accum(m, tv, sem="v"):
            b.wait("gpsimd", sem, tv)
            tk = b.dma("gpsimd", lambda e, m=m, nt=nt, ms=master: e.dma_start(
                out=ms[128 * m:128 * (m + 1), TT * nt: TT * nt + TT],
                in_=XNEW(m), accum_op=ALU.add), "dG")
            tks.append(tk)
            return tk
        def readback():
            b.wait("gpsimd", "dG", tks[-1])
            tk = b.dma("gpsimd", lambda e, nt=nt, ms=master: e.dma_start(
                out=bass.AP(XB, TT * nt, [[C8 * T, 128], [T, 8], [1, TT]]),
                in_=bass.AP(ms.tensor, TT * nt, [[T, 128], [T * 128, 8], [1, TT]])), "dG")
            xb_ready[nt] = tk
            return tk
        return accum, readback

    def dump(dst_cols, src_ap):
        b.wait("gpsimd", "v", b.tick("v"))
        b.wait("gpsimd", "s", b.tick("s"))
        tk = b.dma("gpsimd", lambda e, d=dst_cols, s_=src_ap: e.dma_start(
            out=dbg[:, d:d + 512], in_=s_), "dG")
        for eng in ("vector", "scalar", "tensor"):
            b.wait(eng, "dG", tk)

    def l0_attn(master):
        XH = lambda c: SB[:, 512 * c: 512 * (c + 1)]
        QF = lambda h: SB[:, 4096 + 512 * h: 4096 + 512 * (h + 1)]
        KF = lambda h: SB[:, 8192 + 512 * h: 8192 + 512 * (h + 1)]
        OBF = lambda h: SB[:, 12288 + 512 * h: 12288 + 512 * (h + 1)]
        VT = lambda s: SB[:, 16384 + 1024 * (s % 2): 16384 + 1024 * (s % 2) + 1024]
        KFT = lambda g: SB[:, 18432 + 512 * g: 18432 + 512 * (g + 1)]
        ATM = KFT
        QRAW = lambda i: SB[:, 19456 + 512 * (i % 2): 19456 + 512 * (i % 2) + 512]
        SBH = lambda h: SB[:, 20480 + 128 * h: 20480 + 128 * (h + 1)]
        SST = SF[:, 2048:3072]
        A2s = lambda i: SF[:, 3072:3584] if i % 2 == 0 else SF[:, 7168:7680]
        A1s = lambda i: SF[:, 3584:4096] if i % 2 == 0 else SF[:, 6656:7168]
        MSK4 = lambda g: bass.AP(CST, 128, [[1024, 128], [0, 4], [1, 128]])
        ATM3 = lambda g: bass.AP(SB, 18432 + 512 * g, [[21504, 128], [128, 4], [1, 128]])
        AT3 = lambda bank: bass.AP(bank, 0, [[512, 128], [128, 4], [1, 128]])

        b.wait("tensor", "dG", twq[-1])
        V(lambda e: e.memset(SST, 0.0))
        tv0 = V(lambda e: e.memset(SB[:, 20480:21504], 0.0), "v")
        b.wait("tensor", "v", tv0)

        if DEBUG:
            tv_ = V(lambda e: e.tensor_copy(SF[:, 3712:3840], FM["q1"]), "v")
            b.wait("gpsimd", "v", tv_)
            tkw = b.dma("gpsimd", lambda e: e.dma_start(out=dbg[:, 3200:3328], in_=SF[:, 3712:3840]), "dG")
            for eng_ in ("vector", "scalar", "tensor"):
                b.wait(eng_, "dG", tkw)
        bankf = {i: Z for i in range(6)}
        tptf = [Z, Z]
        kft_guard = [0, 0]
        a2_guard = [0, 0]
        wo_prev = [0]
        accum_hold = [0, 0]
        tms_prev = 0
        wo_hold = [0, 0]

        for nt in range(NT):
            tln = ln_tiles(nt, 19456, PB[0], PB[1],
                           pre_t=(bankf[0], bankf[1]),
                           pre_s=((("t", tms_prev)) ,) if tms_prev else ())
            bankf[0] = bankf[1] = ("v", tln)
            b.wait("vector", "t", b.tick("t"))
            for cp in range(0, C8, 2):
                for c in (cp, cp + 1):
                    V(lambda e, c=c, nt=nt: e.tensor_tensor(
                        SF[:, 1024 + 512 * (c % 2): 1024 + 512 * (c % 2) + 512],
                        XB[:, T * c + TT * nt: T * c + TT * nt + TT], BINV, ALU.mult))
                for c in (cp, cp + 1):
                    V(lambda e, c=c: e.tensor_tensor(
                        XH(c), SF[:, 1024 + 512 * (c % 2): 1024 + 512 * (c % 2) + 512],
                        BMINV, ALU.subtract), "v")
            txh = b.tick("v")

            for Wf, fm1, fm2, fb1, fb2, DST in (
                    (WQ, FM["q1"], FM["q2"], FMB["q1"], FMB["q2"], QF),
                    (WKa, FM["k1"], FM["k2"], FMB["k1"], FMB["k2"], KF)):
                for h in range(C8):
                    bank = PB[0 + h % 2]
                    b.wait("tensor", "v", txh)
                    b.waitp("tensor", bankf[0 + h % 2])
                    for K in range(C8):
                        MM(bank[:, :], Wf(K, h), XH(K), K == 0, K == C8 - 1,
                           "t" if K == C8 - 1 else None)
                    tmm = b.tick("t")
                    b.wait("scalar", "t", tmm)
                    tqr = SC(lambda e, h=h, bank=bank: e.activation(QRAW(h), bank[:, :], AF.Copy), "s")
                    bankf[0 + h % 2] = ("s", tqr)
                    fa, fb_ = 2 + 2 * (h % 2), 3 + 2 * (h % 2)
                    b.wait("tensor", "s", tqr)
                    b.waitp("tensor", bankf[fa])
                    b.waitp("tensor", bankf[fb_])
                    MM(PB[fa][:, :], fm1, QRAW(h), True, True, "t")
                    t45 = MM(PB[fb_][:, :], fm2, QRAW(h), True, True, "t")
                    b.wait("scalar", "t", t45)
                    b.wait("scalar", "v", a2_guard[h % 2])
                    sc_ = float(128 ** -0.5) if DST is QF else 1.0
                    SC(lambda e, fb1=fb1, sc_=sc_, fa=fa, h=h: e.activation(
                        A1s(h), PB[fa][:, :], AF.Identity, bias=fb1, scale=sc_), "s")
                    ts1 = SC(lambda e, fb2=fb2, fb_=fb_, h=h: e.activation(
                        A2s(h), PB[fb_][:, :], AF.Identity, bias=fb2, scale=1.0), "s")
                    b.wait("vector", "s", ts1)
                    tvq = V(lambda e, h=h, DST=DST: e.tensor_tensor(
                        DST(h), A1s(h), A2s(h), ALU.mult), "v")
                    a2_guard[h % 2] = tvq
                    bankf[fa] = bankf[fb_] = ("v", tvq)
            tqk = b.tick("v")
            if DEBUG and nt == 0:
                dump(2048, QF(0))
            for s_ in range(4):
                b.wait("tensor", "v", tqk)
                b.waitp("tensor", bankf[2])
                b.waitp("tensor", bankf[3])
                for K in range(C8):
                    MM(PB[2][:, :], XH(K)[:, 128 * s_:128 * (s_ + 1)], WVrow(K)[:, 0:512],
                       K == 0, K == C8 - 1, "t" if K == C8 - 1 else None)
                for K in range(C8):
                    MM(PB[3][:, :], XH(K)[:, 128 * s_:128 * (s_ + 1)], WVrow(K)[:, 512:1024],
                       K == 0, K == C8 - 1, "t" if K == C8 - 1 else None)
                tvm = b.tick("t")
                b.wait("scalar", "t", tvm)
                SC(lambda e, s_=s_: e.activation(VT(s_)[:, 0:512], PB[2][:, :], AF.Copy), "s")
                tvt = SC(lambda e, s_=s_: e.activation(VT(s_)[:, 512:1024], PB[3][:, :], AF.Copy), "s")
                bankf[2] = bankf[3] = ("s", tvt)
                for g in range(2):
                    b.waitp("tensor", tptf[g])
                    for j in range(4):
                        h = 4 * g + j
                        TR(PT[g][:, 128 * j:128 * (j + 1)], KF(h)[:, 128 * s_:128 * (s_ + 1)],
                           inc="t" if j == 3 else None)
                    ttr = b.tick("t")
                    b.wait("vector", "t", ttr)
                    if kft_guard[g]:
                        b.wait("vector", "t", kft_guard[g])
                    tke = V(lambda e, g=g: e.tensor_copy(KFT(g), PT[g][:, 0:512]), "v")
                    tptf[g] = ("v", tke)
                    atb, kvb_, ob = PB[4 + g], PB[0 + g], PB[2 + g]
                    b.waitp("tensor", bankf[4 + g])
                    for j in range(4):
                        h = 4 * g + j
                        MM(atb[:, 128 * j:128 * (j + 1)], KF(h)[:, 128 * s_:128 * (s_ + 1)],
                           QF(h)[:, 128 * s_:128 * (s_ + 1)], True, True,
                           "t" if j == 3 else None)
                    tat = b.tick("t")
                    b.wait("tensor", "v", tke)
                    b.waitp("tensor", ("s", tvt))
                    b.waitp("tensor", bankf[0 + g])
                    for j in range(4):
                        h = 4 * g + j
                        MM(kvb_[:, 128 * j:128 * (j + 1)], KFT(g)[:, 128 * j:128 * (j + 1)],
                           VT(s_)[:, 128 * h:128 * (h + 1)], True, True,
                           "t" if j == 3 else None)
                    tkv = b.tick("t")
                    b.wait("vector", "t", max(tat, tkv))
                    tam = V(lambda e, g=g, atb=atb: e.tensor_tensor(
                        ATM3(g), AT3(atb), MSK4(g), ALU.mult), "v")
                    bankf[4 + g] = ("v", tam)
                    b.wait("tensor", "v", tam)
                    b.waitp("tensor", bankf[2 + g])
                    for j in range(4):
                        h = 4 * g + j
                        MM(ob[:, 128 * j:128 * (j + 1)], SBH(h),
                           QF(h)[:, 128 * s_:128 * (s_ + 1)], True, False)
                        MM(ob[:, 128 * j:128 * (j + 1)], VT(s_)[:, 128 * h:128 * (h + 1)],
                           ATM(g)[:, 128 * j:128 * (j + 1)], False, True,
                           "t" if j == 3 else None)
                    tob = b.tick("t")
                    kft_guard[g] = tob
                    b.wait("vector", "t", tkv)
                    V(lambda e, g=g, kvb_=kvb_: e.tensor_tensor(
                        SST[:, 512 * g:512 * (g + 1)], SST[:, 512 * g:512 * (g + 1)],
                        kvb_[:, :], ALU.add), "v")
                    bankf[0 + g] = ("v", b.tick("v"))
                    b.drain("vector")
                    b.wait("vector", "t", tob)
                    V(lambda e, g=g: e.tensor_copy(
                        SB[:, 20480 + 512 * g: 20480 + 512 * (g + 1)],
                        SST[:, 512 * g:512 * (g + 1)]), "v")
                    b.wait("scalar", "t", tob)
                    tse = SC(lambda e, s_=s_, g=g, ob=ob: e.activation(
                        bass.AP(SB, 12288 + 512 * (4 * g) + 128 * s_,
                                [[21504, 128], [512, 4], [1, 128]]),
                        ob[:, :], AF.Copy), "s")
                    bankf[2 + g] = ("s", tse)
            tchunk_s = b.tick("s")

            b.drain("scalar")
            tsql = [0] * C8
            for h in range(C8):
                if h >= 2:
                    b.wait("scalar", "t", tsql[h - 2])
                tss = SC(lambda e, h=h: e.activation(QRAW(h), OBF(h), AF.Square), "s")
                b.drain("scalar")
                b.wait("tensor", "s", tss)
                if h == 0:
                    b.waitp("tensor", bankf[0])
                tsql[h] = MM(PB[0][0:8, :], ONES8[:, 8 * h:8 * h + 8], QRAW(h),
                             h == 0, h == C8 - 1, "t")
            tms = tsql[-1]
            tms_prev = tms
            b.wait("scalar", "t", tms)
            ts3 = SC(lambda e: e.activation(SF[0:8, 4608:5120], PB[0][0:8, :],
                                            AF.Sqrt, bias=EPS5[0:8, :], scale=1.0 / 128.0), "s")
            bankf[0] = ("s", ts3)
            b.wait("vector", "s", ts3)
            V(lambda e: e.reciprocal(SF[0:8, 5120:5632], SF[0:8, 4608:5120]))
            b.drain("vector")
            tv4 = V(lambda e: e.tensor_copy(SELI[:, :], SF[0:8, 5120:5632]), "v")
            b.wait("tensor", "v", tv4)
            b.wait("tensor", "s", tchunk_s)
            ton = 0
            for h in range(C8):
                bank = PB[1 + h % 2]
                b.waitp("tensor", bankf[1 + h % 2])
                tbi = MM(bank[:, :], SEL[:, 128 * h:128 * (h + 1)], SELI[:, :],
                         True, True, "t")
                b.wait("vector", "t", tbi)
                ton = V(lambda e, h=h, bank=bank: e.scalar_tensor_tensor(
                    QF(h), OBF(h), RMSW, bank[:, :], ALU.mult, ALU.mult), "v")
                bankf[1 + h % 2] = ("v", ton)
            accum, readback = mk_residual(nt, master)

            b.wait("tensor", "v", ton)
            for m in range(C8):
                if wo_hold[m % 2]:
                    b.wait("sync", "t", wo_hold[m % 2])
                else:
                    b.wait("sync", "dS", b.tick("dS"))
                wo_tk = b.dma("sync", lambda e, m=m: e.dma_start(
                    out=bass.AP(WB, 24576 + 1024 * (m % 2), [[26624, 128], [128, 8], [1, 128]]),
                    in_=bass.AP(wobf.tensor, 128 * m, [[D, 128], [D * 128, 8], [1, 128]])), "dS")
                wo_prev[0] = wo_tk
                bank = PB[3 + m % 2]
                b.waitp("tensor", bankf[3 + m % 2])
                b.wait("tensor", "dS", wo_tk)
                for K in range(C8):
                    MM(bank[:, :], WOW(m, K), QF(K), K == 0, K == C8 - 1,
                       "t" if K == C8 - 1 else None)
                two = b.tick("t")
                wo_hold[m % 2] = two
                b.wait("scalar", "t", two)
                b.wait("scalar", "dG", accum_hold[m % 2])
                tvx = SC(lambda e, m=m, bank=bank: e.activation(XNEW(m), bank[:, :], AF.Copy), "s")
                bankf[3 + m % 2] = ("s", tvx)
                accum_hold[m % 2] = accum(m, tvx, "s")
            readback()

    def l0_ffn(master):
        XR = lambda c: SB[:, 512 * c: 512 * (c + 1)]
        XK = lambda c: SB[:, 4096 + 512 * c: 4096 + 512 * (c + 1)]
        KACT = lambda c: SB[:, 8192 + 512 * c: 8192 + 512 * (c + 1)]
        RSIG = lambda m: SF[:, 1024 + 512 * m: 1024 + 512 * (m + 1)]
        XHT = lambda i: SF[:, 6656 + 512 * (i % 2): 6656 + 512 * (i % 2) + 512]
        DT_ = SF[:, 5120:5632]

        tw = load_wb(["WrT", "WkfT", "WvfT"])
        WR = lambda K, M: WB[:, 1024 * K + 128 * M: 1024 * K + 128 * (M + 1)]
        WKF = lambda K, M: WB[:, 8192 + 1024 * K + 128 * M: 8192 + 1024 * K + 128 * (M + 1)]
        WVF = lambda K, M: WB[:, 16384 + 1024 * K + 128 * M: 16384 + 1024 * K + 128 * (M + 1)]
        b.wait("tensor", "dG", tw[-1])
        V(lambda e: e.memset(CSTF[:, 96:104], 0.0), "v")

        bankf = {i: Z for i in range(6)}
        accum_hold = [0, 0]
        tka_prev = 0
        for nt in range(NT):
            tln = ln_tiles(nt, 19456, PB[0], PB[1], pre_t=(bankf[0], bankf[1]))
            bankf[0] = bankf[1] = ("v", tln)
            if tka_prev:
                b.wait("vector", "s", tka_prev)
            b.wait("vector", "t", b.tick("t"))
            for c in range(C8):
                V(lambda e, c=c, nt=nt: e.tensor_tensor(
                    XHT(c), XB[:, T * c + TT * nt: T * c + TT * nt + TT], BINV, ALU.mult))
                b.drain("vector")
                V(lambda e, c=c: e.tensor_tensor(XHT(c), XHT(c), BMINV, ALU.subtract))
                b.drain("vector")
                V(lambda e, c=c: e.tensor_tensor(
                    DT_[:, 1:512], XHT(c)[:, 1:512], XHT(c)[:, 0:511], ALU.subtract))
                V(lambda e, c=c: e.tensor_tensor(
                    DT_[:, 0:1], XHT(c)[:, 0:1], XLAST[:, c:c + 1], ALU.subtract))
                b.drain("vector")
                V(lambda e, c=c: e.scalar_tensor_tensor(
                    XR(c)[:, 1:512], DT_[:, 1:512], MUR(c), XHT(c)[:, 0:511],
                    ALU.mult, ALU.add))
                V(lambda e, c=c: e.scalar_tensor_tensor(
                    XR(c)[:, 0:1], DT_[:, 0:1], MUR(c), XLAST[:, c:c + 1],
                    ALU.mult, ALU.add))
                V(lambda e, c=c: e.scalar_tensor_tensor(
                    XK(c)[:, 1:512], DT_[:, 1:512], MUK(c), XHT(c)[:, 0:511],
                    ALU.mult, ALU.add))
                V(lambda e, c=c: e.scalar_tensor_tensor(
                    XK(c)[:, 0:1], DT_[:, 0:1], MUK(c), XLAST[:, c:c + 1],
                    ALU.mult, ALU.add), "v")
                b.drain("vector")
                V(lambda e, c=c: e.tensor_copy(XLAST[:, c:c + 1], XHT(c)[:, 511:512]), "v")
                b.drain("vector")
            txr = b.tick("v")
            for m in range(C8):
                bank = PB[m % 3]
                b.wait("tensor", "v", txr)
                b.waitp("tensor", bankf[m % 3])
                for K in range(C8):
                    MM(bank[:, :], WR(K, m), XR(K), K == 0, K == C8 - 1,
                       "t" if K == C8 - 1 else None)
                tr_ = b.tick("t")
                b.wait("scalar", "t", tr_)
                trs = SC(lambda e, m=m, bank=bank: e.activation(
                    RSIG(m), bank[:, :], AF.Sigmoid), "s")
                bankf[m % 3] = ("s", trs)
            for m in range(C8):
                bank = PB[3 + m % 3]
                b.wait("tensor", "v", txr)
                b.waitp("tensor", bankf[3 + m % 3])
                for K in range(C8):
                    MM(bank[:, :], WKF(K, m), XK(K), K == 0, K == C8 - 1,
                       "t" if K == C8 - 1 else None)
                tk_ = b.tick("t")
                b.wait("scalar", "t", tk_)
                SC(lambda e, m=m, bank=bank: e.activation(XHT(m), bank[:, :], AF.Relu), "s")
                tka = SC(lambda e, m=m: e.activation(KACT(m), XHT(m), AF.Square), "s")
                bankf[3 + m % 3] = ("s", tka)
            tka_prev = b.tick("s")
            accum, readback = mk_residual(nt, master)
            for m in range(C8):
                bank = PB[m % 3]
                b.wait("tensor", "s", tka_prev)
                b.waitp("tensor", bankf[m % 3])
                for K in range(C8):
                    MM(bank[:, :], WVF(K, m), KACT(K), K == 0, K == C8 - 1,
                       "t" if K == C8 - 1 else None)
                tvm = b.tick("t")
                b.wait("vector", "t", tvm)
                b.wait("vector", "s", tka_prev)
                b.wait("vector", "dG", accum_hold[m % 2])
                tvx = V(lambda e, m=m, bank=bank: e.tensor_tensor(
                    XNEW(m), bank[:, :], RSIG(m), ALU.mult), "v")
                bankf[m % 3] = ("v", tvx)
                accum_hold[m % 2] = accum(m, tvx)
            readback()

    def l1_attn(master):
        XH = lambda c: SB[:, 512 * c: 512 * (c + 1)]
        QP = lambda e_, nt: SB[:, 4096 + 4096 * e_ + 512 * nt: 4096 + 4096 * e_ + 512 * nt + 512]
        KP = lambda e_: SB[:, 12288 + 512 * e_: 12288 + 512 * (e_ + 1)]
        XHT_ = SB[:, 13312:14336]
        KPT = SB[:, 14336:14592]
        KVB = lambda e_: SB[:, 14592 + 1024 * e_: 14592 + 1024 * (e_ + 1)]
        BZ = SF[:, 1024:1536]

        tw = load_wb(["qpT", "kpT"], width=256)
        QPJ = lambda K, E: WB[:, 256 * K + 128 * E: 256 * K + 128 * (E + 1)]
        KPJ = lambda K, E: WB[:, 2048 + 256 * K + 128 * E: 2048 + 256 * K + 128 * (E + 1)]
        b.wait("tensor", "dG", tw[-1])

        bank45 = [Z, Z]
        tptf = [Z, Z]
        for nt in range(NT):
            tln = ln_tiles(nt, 19456, PB[4], PB[5], pre_t=(bank45[0], bank45[1]))
            bank45[0] = bank45[1] = ("v", tln)
            b.wait("vector", "t", b.tick("t"))
            for cp in range(0, C8, 2):
                for c in (cp, cp + 1):
                    V(lambda e, c=c, nt=nt: e.tensor_tensor(
                        SF[:, 1536 + 512 * (c % 2): 1536 + 512 * (c % 2) + 512],
                        XB[:, T * c + TT * nt: T * c + TT * nt + TT], BINV, ALU.mult))
                for c in (cp, cp + 1):
                    V(lambda e, c=c: e.tensor_tensor(
                        XH(c), SF[:, 1536 + 512 * (c % 2): 1536 + 512 * (c % 2) + 512],
                        BMINV, ALU.subtract), "v")
            txh = b.tick("v")
            for e_ in range(2):
                b.wait("tensor", "v", txh)
                b.waitp("tensor", bank45[0])
                for K in range(C8):
                    MM(PB[4][:, :], QPJ(K, e_), XH(K), K == 0, K == C8 - 1,
                       "t" if K == C8 - 1 else None)
                tmm = b.tick("t")
                b.wait("scalar", "t", tmm)
                te = SC(lambda e, e_=e_, nt=nt: e.activation(
                    QP(e_, nt), PB[4][:, :], AF.Exp), "s")
                bank45[0] = ("s", te)
                b.waitp("tensor", bank45[1])
                for K in range(C8):
                    MM(PB[5][:, :], KPJ(K, e_), XH(K), K == 0, K == C8 - 1,
                       "t" if K == C8 - 1 else None)
                tmm = b.tick("t")
                b.wait("scalar", "t", tmm)
                te = SC(lambda e, e_=e_: e.activation(KP(e_), PB[5][:, :], AF.Exp), "s")
                bank45[1] = ("s", te)
            tqk = b.tick("s")
            for s_ in range(4):
                b.wait("tensor", "s", tqk)
                for half in range(2):
                    b.waitp("tensor", tptf[half])
                    for j in range(4):
                        c = 4 * half + j
                        TR(PT[half][:, 128 * j:128 * (j + 1)],
                           XH(c)[:, 128 * s_:128 * (s_ + 1)],
                           inc="t" if j == 3 else None)
                    ttr = b.tick("t")
                    b.wait("vector", "t", ttr)
                    tev = V(lambda e, half=half: e.tensor_copy(
                        XHT_[:, 512 * half:512 * (half + 1)], PT[half][:, 0:512]), "v")
                    tptf[half] = ("v", tev)
                b.waitp("tensor", tptf[0])
                for e_ in range(2):
                    TR(PT[0][:, 512 + 128 * e_: 512 + 128 * (e_ + 1)],
                       KP(e_)[:, 128 * s_:128 * (s_ + 1)],
                       inc="t" if e_ == 1 else None)
                ttr = b.tick("t")
                b.wait("vector", "t", ttr)
                tev = V(lambda e: e.tensor_copy(KPT[:, 0:256], PT[0][:, 512:768]), "v")
                tptf[0] = ("v", tev)
                b.wait("tensor", "v", tev)
                first = nt == 0 and s_ == 0
                lastk = nt == NT - 1 and s_ == 3
                for e_ in range(2):
                    for half in range(2):
                        MM(PB[2 * e_ + half][:, :], KPT[:, 128 * e_:128 * (e_ + 1)],
                           XHT_[:, 512 * half:512 * (half + 1)], first, lastk,
                           "t" if (lastk and e_ == 1 and half == 1) else None)
        tkv = b.tick("t")
        b.wait("vector", "t", tkv)
        for e_ in range(2):
            for half in range(2):
                V(lambda e, e_=e_, half=half: e.tensor_copy(
                    KVB(e_)[:, 512 * half:512 * (half + 1)],
                    PB[2 * e_ + half][:, :]), "v")
        tkvb = b.tick("v")
        bankf = {i: ("v", tkvb) for i in range(6)}
        accum_hold = [0, 0]
        for nt in range(NT):
            b.waitp("tensor", bankf[4])
            for e_ in range(2):
                MM(PB[4][0:1, :], ONESC, QP(e_, nt), e_ == 0, e_ == 1,
                   "t" if e_ == 1 else None)
            tz = b.tick("t")
            b.wait("vector", "t", tz)
            V(lambda e: e.tensor_scalar(ROWS[0:1, 0:512], PB[4][0:1, :], 1e-8, None, ALU.add), "v")
            bankf[4] = ("v", b.tick("v"))
            b.drain("vector")
            tvz = V(lambda e: e.reciprocal(ROWS[0:1, 512:1024], ROWS[0:1, 0:512]), "v")
            b.wait("tensor", "v", tvz)
            b.waitp("tensor", bankf[5])
            tbz = MM(PB[5][:, :], ROW1, ROWS[0:1, 512:1024], True, True, "t")
            b.wait("vector", "t", tbz)
            tvb = V(lambda e: e.tensor_copy(BZ, PB[5][:, :]), "v")
            bankf[5] = ("v", tvb)
            accum, readback = mk_residual(nt, master)
            for m in range(C8):
                bank = PB[m % 2]
                b.wait("tensor", "v", tvb)
                b.waitp("tensor", bankf[m % 2])
                for e_ in range(2):
                    MM(bank[:, :], KVB(e_)[:, 128 * m:128 * (m + 1)], QP(e_, nt),
                       e_ == 0, e_ == 1, "t" if e_ == 1 else None)
                tnm = b.tick("t")
                b.wait("vector", "t", tnm)
                b.wait("vector", "dG", accum_hold[m % 2])
                tvx = V(lambda e, m=m, bank=bank: e.tensor_tensor(
                    XNEW(m), bank[:, :], BZ, ALU.mult), "v")
                bankf[m % 2] = ("v", tvx)
                accum_hold[m % 2] = accum(m, tvx)
            readback()

    def l1_ffn(master):
        XHW = lambda i: SF[:, 1024 + 512 * (i % 2): 1024 + 512 * (i % 2) + 512]
        MX = lambda j: SF[:, 2048 + j:2049 + j]
        SG_ = lambda i: SF[:, 2560 + 512 * (i % 2): 2560 + 512 * (i % 2) + 512]
        T4U = lambda i: SF[:, 3584 + 512 * (i % 2): 3584 + 512 * (i % 2) + 512]
        GB = lambda i: SB[:, 512 * (i % 4): 512 * (i % 4) + 512]
        LB = lambda i: SB[:, 2048 + 512 * (i % 4): 2048 + 512 * (i % 4) + 512]
        T4B = lambda i: SB[:, 4096 + 512 * (i % 4): 4096 + 512 * (i % 4) + 512]
        UB = lambda i: SB[:, 6144 + 512 * (i % 4): 6144 + 512 * (i % 4) + 512]
        OPB = lambda i: SB[:, 8192 + 512 * (i % 4): 8192 + 512 * (i % 4) + 512]
        OOP = lambda k: SB[:, 512 * k: 512 * (k + 1)]
        GB4 = lambda i: SB[:, 19456 + 512 * (i % 2): 19456 + 512 * (i % 2) + 512]
        I16A = lambda i: SI[:, 0:512]
        I16B = lambda i: SI[:, 512:1024]
        I16R = lambda i: SI[:, 512 * (i % 2): 512 * (i % 2) + 512]
        SUr = lambda i: SU[:, 0:512]

        wT = lambda K: WB[:, 3072 * K: 3072 * (K + 1)]
        wTd = lambda K: WB[:, 1024 * K: 1024 * (K + 1)]

        def allreduce(srcs, dsts):
            n = len(srcs)
            for j, src in enumerate(srcs):
                b.wait("tensor", "v", b.tick("v"))
                ttr = TR(PB[4][0:1, 0:128], src, ident=IDTF, inc="t")
                b.wait("vector", "t", ttr)
                V(lambda e, j=j: e.reduce_max(
                    ROWS[0:1, 1024 + j:1025 + j], PB[4][0:1, 0:128], AX.X), "v")
                b.drain("vector")
            tv = b.tick("v")
            b.wait("gpsimd", "v", tv)
            tg = b.dma("gpsimd", lambda e, n=n: e.dma_start(
                out=cc_in[0:1, 0:n], in_=ROWS[0:1, 1024:1024 + n]), "dC")
            b.wait("gpsimd", "dC", tg)
            tcc = b.op("gpsimd", lambda e: e.collective_compute(
                "AllReduce", ALU.max, replica_groups=[list(range(8))],
                ins=[cc_in], outs=[cc_out]), "cc")
            b.wait("gpsimd", "cc", tcc)
            tg = b.dma("gpsimd", lambda e, n=n: e.dma_start(
                out=ROWS[0:1, 1024:1024 + n], in_=cc_out[0:1, 0:n]), "dC")
            b.wait("vector", "dC", tg)
            for j, dc in enumerate(dsts):
                V(lambda e, j=j, dc=dc: e.tensor_copy(
                    ROWS[0:1, dc:dc + 1], ROWS[0:1, 1024 + j:1025 + j]), "v")
            b.drain("vector")

        def mkscale(m_rc, s_rc, rs_rc):
            V(lambda e, a=m_rc, o=s_rc: e.tensor_scalar(
                ROWS[0:1, o:o + 1], ROWS[0:1, a:a + 1], 1.0 / 32767.0, 1e-12,
                ALU.mult, ALU.max))
            b.drain("vector")
            V(lambda e, o=s_rc, r=rs_rc: e.reciprocal(
                ROWS[0:1, r:r + 1], ROWS[0:1, o:o + 1]), "v")
            b.drain("vector")

        b.dma("sync", lambda e: e.dma_start(out=master, in_=xmA), "dX")
        t_pre = b.tick("dX")

        V(lambda e: e.memset(MX(0), 0.0), "v")
        guard = [Z, Z]
        for nt in range(NT):
            tln = ln_tiles(nt, 19456, PB[0], PB[1], pre_t=(guard[0], guard[1]))
            guard[0] = guard[1] = ("v", tln)
            b.wait("vector", "t", b.tick("t"))
            for cp in range(0, C8, 2):
                for c in (cp, cp + 1):
                    V(lambda e, c=c, nt=nt: e.tensor_tensor(
                        XHW(c), XB[:, T * c + TT * nt: T * c + TT * nt + TT], BINV, ALU.mult))
                for c in (cp, cp + 1):
                    V(lambda e, c=c: e.tensor_tensor(XHW(c), XHW(c), BMINV, ALU.subtract))
                V(lambda e, c=cp: e.reduce_max(MX(1), XHW(c), AX.X,
                                               apply_absolute_value=True))
                V(lambda e, c=cp + 1: e.reduce_max(MX(8), XHW(c), AX.X,
                                                   apply_absolute_value=True))
                V(lambda e: e.tensor_max(MX(0), MX(0), MX(1)))
                b.drain("vector")
                V(lambda e: e.tensor_max(MX(0), MX(0), MX(8)), "v")
                b.drain("vector")
        if L1CUT == 'A0':
            return
        allreduce([MX(0)], [0])
        if L1CUT == 'A':
            return
        mkscale(0, 2, 3)
        if L1CUT == 'Am':
            return
        bcast_col(2, 0, PB[4])
        bcast_col(3, 1, PB[4])
        if L1CUT == 'Ab':
            return
        def load_wt(src, nchunk, wide, pre_t):
            b.wait("sync", "t", pre_t)
            tks = []
            for K in range(nchunk):
                tk = b.dma("sync", lambda e, K=K, src=src, w=wide: e.dma_start(
                    out=WB[:, w * K: w * (K + 1)], in_=src[:, 128 * K:128 * (K + 1)],
                    transpose=True), "dS")
                tks.append(tk)
            return tks

        def mm_spill_phase(dst_spill, mxacc, mxtmp):
            for nt in range(NT):
                for mi in range(MI):
                    i = nt * MI + mi
                    bank = PB[2 + mi % 2]
                    b.waitp("tensor", bank_ev[mi % 2])
                    for K in range(C8):
                        MM(bank[:, :], wT(K)[:, 128 * mi:128 * (mi + 1)],
                           XB[:, T * K + TT * nt: T * K + TT * nt + TT],
                           K == 0, K == C8 - 1, "t" if K == C8 - 1 else None)
                    tmm = b.tick("t")
                    b.wait("scalar", "t", tmm)
                    b.wait("scalar", "dS", spill_hold[i % 4])
                    tse = SC(lambda e, i=i, bank=bank: e.activation(GB(i), bank[:, :], AF.Copy), "s")
                    bank_ev[mi % 2] = ("s", tse)
                    b.wait("vector", "s", tse)
                    V(lambda e, i=i, mxtmp=mxtmp: e.reduce_max(
                        MX(mxtmp), GB(i), AX.X, apply_absolute_value=True))
                    tve = V(lambda e, mxacc=mxacc, mxtmp=mxtmp: e.tensor_max(
                        MX(mxacc), MX(mxacc), MX(mxtmp)), "v")
                    b.wait("sync", "s", tse)
                    spill_hold[i % 4] = b.dma("sync", lambda e, nt=nt, mi=mi, i=i, d=dst_spill:
                        e.dma_start(out=d[128 * mi:128 * (mi + 1), TT * nt: TT * nt + TT],
                                    in_=GB(i)), "dS")
                    yield i, nt, mi

        twg = load_wt(wgq, C8, 3072, b.tick("t"))
        V(lambda e: e.memset(MX(2), 0.0), "v")
        spill_hold = [0, 0, 0, 0]
        bank_ev = [Z, Z]
        guard2 = [Z, Z]
        for nt in range(NT):
            tln = ln_tiles(nt, 19456, PB[0], PB[1], pre_t=(guard2[0], guard2[1]))
            guard2[0] = guard2[1] = ("v", tln)
            b.wait("vector", "t", t_stats[0])
            for cp in range(0, C8, 2):
                for c in (cp, cp + 1):
                    V(lambda e, c=c, nt=nt: e.tensor_tensor(
                        XHW(c), XB[:, T * c + TT * nt: T * c + TT * nt + TT], BINV, ALU.mult))
                for c in (cp, cp + 1):
                    V(lambda e, c=c: e.tensor_tensor(XHW(c), XHW(c), BMINV, ALU.subtract))
                for c in (cp, cp + 1):
                    V(lambda e, c=c: e.tensor_scalar(I16R(c), XHW(c), COL(1), None, ALU.mult))
                for c in (cp, cp + 1):
                    V(lambda e, c=c, nt=nt: e.tensor_scalar(
                        XB[:, T * c + TT * nt: T * c + TT * nt + TT], I16R(c), COL(0),
                        None, ALU.mult), "v")
            txq_nt = b.tick("v")
            if nt == 0:
                b.wait("tensor", "dS", twg[-1])
            for mi in range(MI):
                i = nt * MI + mi
                bank = PB[2 + mi % 2]
                b.waitp("tensor", bank_ev[mi % 2])
                b.wait("tensor", "v", txq_nt)
                for K in range(C8):
                    MM(bank[:, :], wT(K)[:, 128 * mi:128 * (mi + 1)],
                       XB[:, T * K + TT * nt: T * K + TT * nt + TT],
                       K == 0, K == C8 - 1, "t" if K == C8 - 1 else None)
                tmm = b.tick("t")
                b.wait("scalar", "t", tmm)
                b.wait("scalar", "dS", spill_hold[i % 4])
                tse = SC(lambda e, i=i, bank=bank: e.activation(GB(i), bank[:, :], AF.Copy), "s")
                bank_ev[mi % 2] = ("s", tse)
                b.wait("vector", "s", tse)
                V(lambda e, i=i: e.reduce_max(
                    MX(3), GB(i), AX.X, apply_absolute_value=True))
                tve = V(lambda e: e.tensor_max(MX(2), MX(2), MX(3)), "v")
                b.wait("sync", "s", tse)
                spill_hold[i % 4] = b.dma("sync", lambda e, nt=nt, mi=mi, i=i:
                    e.dma_start(out=sp_gate[128 * mi:128 * (mi + 1), TT * nt: TT * nt + TT],
                                in_=GB(i)), "dS")
        if L1CUT == 'B':
            return
        allreduce([MX(2)], [4])
        mkscale(4, 5, 6)
        bcast_col(5, 2, PB[4])
        bcast_col(6, 3, PB[4])

        twu = load_wt(wuq, C8, 3072, b.tick("t"))
        b.wait("tensor", "dS", twu[-1])
        V(lambda e: e.memset(MX(4), 0.0))
        V(lambda e: e.memset(MX(6), 0.0), "v")
        p2_hold = [0, 0, 0, 0]
        t4_hold = [0, 0, 0, 0]
        p2_prev = {"sg": 0, "su": 0, "i16": 0}
        for i, nt, mi in mm_spill_phase(sp_up, 4, 5):
            b.wait("sync", "dS", p2_hold[i % 4])
            tld = b.dma("sync", lambda e, nt=nt, mi=mi, i=i: e.dma_start(
                out=LB(i), in_=sp_gate[128 * mi:128 * (mi + 1), TT * nt: TT * nt + TT]), "dS")
            b.wait("vector", "dS", tld)
            if p2_prev["sg"]:
                b.wait("vector", "s", p2_prev["sg"])
            tvi = V(lambda e, i=i: e.tensor_scalar(I16A(i), LB(i), COL(3), None, ALU.mult), "v")
            b.wait("scalar", "v", tvi)
            SC(lambda e, i=i: e.activation(SG_(i), I16A(i), AF.Sigmoid, scale=COL(2)), "s")
            b.drain("scalar")
            if p2_prev["su"]:
                b.wait("scalar", "v", p2_prev["su"])
            tsg = SC(lambda e, i=i: e.activation(SUr(i), SG_(i), AF.Copy, scale=65536.0), "s")
            p2_prev["sg"] = tsg
            b.wait("vector", "s", tsg)
            V(lambda e, i=i: e.tensor_tensor(T4U(i), I16A(i), SUr(i), ALU.mult))
            b.drain("vector")
            V(lambda e, i=i: e.reduce_max(MX(7), T4U(i), AX.X, apply_absolute_value=True))
            b.drain("vector")
            V(lambda e: e.tensor_max(MX(6), MX(6), MX(7)))
            b.wait("vector", "dS", t4_hold[i % 4])
            tvt = V(lambda e, i=i: e.tensor_copy(T4B(i), T4U(i)), "v")
            p2_prev["su"] = tvt
            p2_hold[i % 4] = tld
            b.wait("sync", "v", tvt)
            t4_hold[i % 4] = b.dma("sync", lambda e, nt=nt, mi=mi, i=i: e.dma_start(
                out=sp_t4[128 * mi:128 * (mi + 1), TT * nt: TT * nt + TT],
                in_=T4B(i)), "dS")
        if L1CUT == 'p2':
            return
        allreduce([MX(4), MX(6)], [8, 10])
        mkscale(8, 9, 12)
        V(lambda e: e.tensor_scalar(ROWS[0:1, 13:14], ROWS[0:1, 5:6], 1.0 / 65536.0,
                                    None, ALU.mult))
        b.drain("vector")
        V(lambda e: e.tensor_mul(ROWS[0:1, 14:15], ROWS[0:1, 10:11], ROWS[0:1, 13:14]))
        b.drain("vector")
        V(lambda e: e.tensor_scalar(ROWS[0:1, 15:16], ROWS[0:1, 14:15], 1.0 / 32767.0,
                                    1e-12, ALU.mult, ALU.max))
        b.drain("vector")
        V(lambda e: e.reciprocal(ROWS[0:1, 16:17], ROWS[0:1, 15:16]))
        b.drain("vector")
        V(lambda e: e.tensor_mul(ROWS[0:1, 17:18], ROWS[0:1, 13:14], ROWS[0:1, 16:17]), "v")
        b.drain("vector")
        bcast_col(12, 4, PB[4])
        bcast_col(17, 5, PB[4])
        V(lambda e: e.memset(MX(0), 0.0), "v")
        ld_hold = [0, 0, 0, 0]
        op_hold = [0, 0, 0, 0]
        p3_prev = {"a": 0, "b": 0}
        for i in range(MI * NT):
            mi, nt = i % MI, i // MI
            b.wait("sync", "dS", ld_hold[i % 4])
            tl1 = b.dma("sync", lambda e, nt=nt, mi=mi, i=i: e.dma_start(
                out=T4B(i), in_=sp_t4[128 * mi:128 * (mi + 1), TT * nt: TT * nt + TT]), "dS")
            tl2 = b.dma("sync", lambda e, nt=nt, mi=mi, i=i: e.dma_start(
                out=UB(i), in_=sp_up[128 * mi:128 * (mi + 1), TT * nt: TT * nt + TT]), "dS")
            b.wait("scalar", "dS", tl2)
            if p3_prev["a"]:
                b.wait("scalar", "v", p3_prev["a"])
            SC(lambda e, i=i: e.activation(I16A(i), T4B(i), AF.Copy, scale=COL(5)), "s")
            b.drain("scalar")
            tsc = SC(lambda e, i=i: e.activation(I16B(i), UB(i), AF.Copy, scale=COL(4)), "s")
            b.wait("vector", "s", tsc)
            V(lambda e, i=i: e.tensor_tensor(T4U(i), I16A(i), I16B(i), ALU.mult))
            b.drain("vector")
            V(lambda e, i=i: e.reduce_max(MX(1), T4U(i), AX.X, apply_absolute_value=True))
            b.drain("vector")
            V(lambda e: e.tensor_max(MX(0), MX(0), MX(1)))
            b.wait("vector", "dS", op_hold[i % 4])
            tvo = V(lambda e, i=i: e.tensor_copy(OPB(i), T4U(i)), "v")
            p3_prev["a"] = tvo
            ld_hold[i % 4] = tl2
            b.wait("sync", "v", tvo)
            op_hold[i % 4] = b.dma("sync", lambda e, nt=nt, mi=mi, i=i: e.dma_start(
                out=sp_op[128 * mi:128 * (mi + 1), TT * nt: TT * nt + TT],
                in_=OPB(i)), "dS")
        if L1CUT == 'p3':
            return
        allreduce([MX(0)], [0])
        V(lambda e: e.tensor_mul(ROWS[0:1, 18:19], ROWS[0:1, 15:16], ROWS[0:1, 9:10]))
        b.drain("vector")
        V(lambda e: e.tensor_mul(ROWS[0:1, 19:20], ROWS[0:1, 0:1], ROWS[0:1, 18:19]))
        b.drain("vector")
        V(lambda e: e.tensor_scalar(ROWS[0:1, 20:21], ROWS[0:1, 19:20], 1.0 / 32767.0,
                                    1e-12, ALU.mult, ALU.max))
        b.drain("vector")
        V(lambda e: e.reciprocal(ROWS[0:1, 21:22], ROWS[0:1, 20:21]))
        b.drain("vector")
        V(lambda e: e.tensor_mul(ROWS[0:1, 22:23], ROWS[0:1, 18:19], ROWS[0:1, 21:22]), "v")
        b.drain("vector")
        bcast_col(22, 6, PB[4])
        bcast_col(20, 7, PB[4])
        twd = load_wt(wdq, MI, 1024, b.tick("t"))
        b.wait("tensor", "dS", twd[-1])
        b.wait("sync", "dX", t_pre)
        accum_hold = [0, 0]
        bank_ev2 = [Z, Z]
        p4_prev = 0
        tdm_last = 0
        tall = 0
        for nt in range(NT):
            if tdm_last:
                b.wait("vector", "t", tdm_last)
            for k in range(MI):
                b.wait("sync", "s", p4_prev)
                tl = b.dma("sync", lambda e, k=k, nt=nt: e.dma_start(
                    out=GB4(k), in_=sp_op[128 * k:128 * (k + 1), TT * nt: TT * nt + TT]), "dS")
                b.wait("scalar", "dS", tl)
                if tall:
                    b.wait("scalar", "v", tall)
                tsc = SC(lambda e, k=k: e.activation(I16R(k), GB4(k), AF.Copy,
                                                     scale=COL(6)), "s")
                p4_prev = tsc
                b.wait("vector", "s", tsc)
                tall = V(lambda e, k=k: e.tensor_copy(OOP(k), I16R(k)), "v")
            accum, _rb = mk_residual(nt, master)
            for m in range(C8):
                bank = PB[2 + m % 2]
                b.wait("tensor", "v", tall)
                b.waitp("tensor", bank_ev2[m % 2])
                for K in range(MI):
                    MM(bank[:, :], wTd(K)[:, 128 * m:128 * (m + 1)], OOP(K),
                       K == 0, K == MI - 1, "t" if K == MI - 1 else None)
                tdm = b.tick("t")
                tdm_last = tdm
                b.wait("scalar", "t", tdm)
                b.wait("scalar", "dG", accum_hold[m % 2])
                tvx = SC(lambda e, m=m, bank=bank: e.activation(
                    XNEW(m), bank[:, :], AF.Copy, scale=COL(7)), "s")
                bank_ev2[m % 2] = ("s", tvx)
                accum_hold[m % 2] = accum(m, tvx, "s")

    seq = [lambda: l0_attn(xmA), lambda: l0_ffn(xmA),
           lambda: l1_attn(xmA), lambda: l1_ffn(outT)]
    for i in range(N_BLOCKS):
        seq[i]()
        barrier()
    if N_BLOCKS < 4:
        b.wait("sync", "dG", b.tick("dG"))
        b.dma("sync", lambda e: e.dma_start(out=outT, in_=xmA), "dX")
    b.wait("sync", "dS", b.tick("dS"))
    b.wait("sync", "dG", b.tick("dG"))
    b.wait("sync", "dX", b.tick("dX"))
    b.wait("sync", "dC", b.tick("dC"))

    block = es.enter_context(nc.Block())

    def mk(eng_name):
        def run(eng):
            for kind, a1, a2, a3 in b.q[eng_name]:
                if kind == "op":
                    inst = a1(eng)
                    if a2 is not None:
                        inst.then_inc(sems[a2], a3)
                elif kind == "wait":
                    eng.wait_ge(sems[a1], a2)
                else:
                    eng.drain()
        return run

    block.tensor(mk("tensor"))
    block.vector(mk("vector"))
    block.scalar(mk("scalar"))
    block.gpsimd(mk("gpsimd"))
    block.sync(mk("sync"))

    es.close()
    return nc


def kernel(**inputs):
    xs = np.ascontiguousarray(np.asarray(inputs["L0_attn_hidden_states"], np.float32))
    tr = lambda a: np.ascontiguousarray(np.asarray(a, dtype=np.float32).T)

    cst_f = np.zeros((128, 1024), np.float32)
    cst_f[:, 0:128] = np.eye(128, dtype=np.float32)
    cst_f[:, 128:256] = np.tril(np.ones((128, 128), np.float32)).T
    cst_f[:, 256] = 1.0
    for h in range(8):
        cst_f[:, 320 + 8 * h + h] = 1.0

    cstf_f = np.zeros((128, 384), np.float32)
    cstf_f[:, 128:256] = np.eye(128, dtype=np.float32)
    cstf_f[0, 256:384] = 1.0
    cstf_f[:, 40] = 1e-5

    sel8 = np.zeros((8, 1024), np.float32)
    for h in range(8):
        sel8[h, 128 * h:128 * (h + 1)] = 1.0

    shared = {
        "WqT": tr(inputs["L0_attn_Wq"]), "WkT": tr(inputs["L0_attn_Wk"]),
        "WvT": tr(inputs["L0_attn_Wv"]), "WoT": tr(inputs["L0_attn_Wo"]),
        "fmq1T": tr(inputs["L0_attn_fmq_w1"]), "fmq2T": tr(inputs["L0_attn_fmq_w2"]),
        "fmk1T": tr(inputs["L0_attn_fmk_w1"]), "fmk2T": tr(inputs["L0_attn_fmk_w2"]),
        "fmb": np.ascontiguousarray(np.stack([
            np.asarray(inputs["L0_attn_fmq_b1"], np.float32),
            np.asarray(inputs["L0_attn_fmq_b2"], np.float32),
            np.asarray(inputs["L0_attn_fmk_b1"], np.float32),
            np.asarray(inputs["L0_attn_fmk_b2"], np.float32)], axis=1)),
        "rmsw": np.ascontiguousarray(np.asarray(inputs["L0_attn_rms_w"], np.float32)[:, None]),
        "WrT": tr(inputs["L0_ffn_W_r"]), "WkfT": tr(inputs["L0_ffn_W_k"]),
        "WvfT": tr(inputs["L0_ffn_W_v"]),
        "mus": np.ascontiguousarray(np.concatenate([
            np.asarray(inputs["L0_ffn_mu_r"], np.float32).reshape(8, 128).T,
            np.asarray(inputs["L0_ffn_mu_k"], np.float32).reshape(8, 128).T], axis=1)),
        "qpT": np.ascontiguousarray(np.asarray(inputs["L1_attn_q_proj"], np.float32)),
        "kpT": np.ascontiguousarray(np.asarray(inputs["L1_attn_k_proj"], np.float32)),
        "wg_nat": np.ascontiguousarray(np.asarray(inputs["L1_ffn_w_gate"], np.float32)),
        "wu_nat": np.ascontiguousarray(np.asarray(inputs["L1_ffn_w_up"], np.float32)),
        "wd_nat": np.ascontiguousarray(np.asarray(inputs["L1_ffn_w_down"], np.float32)),
        "cst_f": cst_f, "cstf_f": cstf_f, "sel8T_f": sel8,
    }
    in_maps = []
    for core in range(8):
        m = dict(shared)
        m["xT"] = np.ascontiguousarray(xs[core].T)
        in_maps.append(m)
    nc = build_nc()
    res = run_bass_kernel_spmd(nc, in_maps, core_ids=list(range(8)))
    out = np.empty((8, T, D), np.float32)
    for core in range(8):
        out[core] = np.asarray(res.results[core]["outT"]).T
    return out
